# version 15
# baseline (speedup 1.0000x reference)
import numpy as np
import ml_dtypes
from contextlib import ExitStack

import concourse.bass as bass
import concourse.mybir as mybir
from concourse.bass_utils import run_bass_kernel_spmd

F32 = mybir.dt.float32
BF16 = mybir.dt.bfloat16
AF = mybir.ActivationFunctionType
ALU = mybir.AluOpType
AX = mybir.AxisListType

NEG = -1.0e38
EPS_LN = 1.0e-5
EPS_LA = 1.0e-12

N_CORES = 8


class Cfg:
    def __init__(self, small=False):
        if small:
            self.D, self.T, self.MACRO = 256, 256, 128
            self.L0H, self.L1H, self.L1KV = 4, 2, 1
            self.IM, self.ISH, self.E = 256, 512, 8
        else:
            self.D, self.T, self.MACRO = 2048, 2048, 512
            self.L0H, self.L1H, self.L1KV = 32, 16, 8
            self.IM, self.ISH, self.E = 1408, 5632, 8
        self.FD = 64
        self.HD0 = 64
        self.HD1 = 128
        self.NI = 4
        self.NL = 8
        self.PRE = self.T
        self.DB = self.D // 128
        self.NM = self.T // self.MACRO
        self.CPM = self.MACRO // 128
        self.NC = self.T // 128
        self.NPAIR = self.L0H // 2
        self.TT = self.T // 128
        self.NIB = self.IM // 128
        self.NSH = self.ISH // self.IM
        self.NPE = self.NSH + self.E
        self.VW = self.L0H * 65
        self.KV1 = self.L1KV * self.HD1
        self.GQ = self.L1H // self.L1KV
        self.TH = 2 if self.T >= 2048 else 1
        self.HTOK = self.T // self.TH


class Op:
    __slots__ = ("idx", "eng", "fn", "rd", "wr", "deps", "need_inc", "inc_val",
                 "is_dma", "chan", "chan_cnt", "tag")

    def __init__(self, idx, eng, fn, rd, wr, is_dma=False, tag=""):
        self.idx, self.eng, self.fn = idx, eng, fn
        self.rd, self.wr = rd, wr
        self.deps = set()
        self.need_inc = False
        self.inc_val = None
        self.is_dma = is_dma
        self.chan = None
        self.chan_cnt = None
        self.tag = tag


def _overlap(a, b):
    return (a[0] == b[0] and a[1] < b[2] and b[1] < a[2]
            and a[3] < b[4] and b[3] < a[4])


def _covers(a, b):
    return (a[0] == b[0] and a[1] <= b[1] and a[2] >= b[2]
            and a[3] <= b[3] and a[4] >= b[4])


class Bld:
    NCH = 24

    def __init__(self, nc, dbg=False):
        self.nc = nc
        self.dbg = dbg
        self.ops = []
        self.w_hist = {}
        self.r_hist = {}
        self.bars = []
        self.chan_rr = {"Q": 0, "G": 0}
        self.psum_names = set()

    def reg(self, t, p0, p1, f0, f1):
        name = t.name
        if name in self.psum_names:
            return (name, 0, 128, 0, 1 << 30)
        return (name, p0, p1, f0, f1)

    def _resolve(self, op):
        for r in op.rd:
            lst = self.w_hist.get(r[0])
            if lst:
                for (wreg, widx) in reversed(lst):
                    if _overlap(wreg, r):
                        op.deps.add(widx)
                        if _covers(wreg, r):
                            break
        for w in op.wr:
            lst = self.w_hist.get(w[0])
            if lst:
                for (wreg, widx) in reversed(lst):
                    if _overlap(wreg, w):
                        op.deps.add(widx)
                        if _covers(wreg, w):
                            break
            lst = self.r_hist.get(w[0])
            if lst:
                for (rreg, ridx) in lst:
                    if _overlap(rreg, w):
                        op.deps.add(ridx)
        for r in op.rd:
            self.r_hist.setdefault(r[0], []).append((r, op.idx))
        for w in op.wr:
            wl = self.w_hist.setdefault(w[0], [])
            wl[:] = [e for e in wl if not _covers(w, e[0])]
            wl.append((w, op.idx))
            rl = self.r_hist.get(w[0])
            if rl:
                rl[:] = [e for e in rl if not _covers(w, e[0])]

    def op(self, eng, fn, rd=(), wr=(), tag=""):
        o = Op(len(self.ops), eng, fn, list(rd), list(wr), tag=tag)
        self._resolve(o)
        self.ops.append(o)
        return o

    def dma(self, q, fn, rd=(), wr=(), tag=""):
        o = Op(len(self.ops), q, fn, list(rd), list(wr), is_dma=True, tag=tag)
        o.chan = self.chan_rr[q]
        self.chan_rr[q] = (self.chan_rr[q] + 1) % self.NCH
        self._resolve(o)
        self.ops.append(o)
        return o

    def bar(self, tag="bar"):
        o = Op(len(self.ops), "BAR", None, [], [], tag=tag)
        self.ops.append(o)
        self.bars.append(o.idx)
        self.w_hist.clear()
        self.r_hist.clear()
        return o

    def emit(self, ctx):
        nc = self.nc
        engines = {"T": nc.tensor, "V": nc.vector, "S": nc.scalar,
                   "G": nc.gpsimd, "Q": nc.sync}
        ops = self.ops
        for o in ops:
            if o.eng == "BAR":
                continue
            for d in o.deps:
                p = ops[d]
                if p.is_dma:
                    continue
                if p.eng == o.eng and p.eng == "T":
                    continue
                p.need_inc = True
        for bidx in self.bars:
            seen = set()
            i = bidx - 1
            while i >= 0 and len(seen) < 5:
                o = ops[i]
                if o.eng == "BAR":
                    break
                if not o.is_dma and o.eng not in seen:
                    o.need_inc = True
                    seen.add(o.eng)
                i -= 1

        esem = {k: ctx.enter_context(nc.semaphore(f"se_{k}")) for k in "TVSGQ"}
        csem = {(q, i): ctx.enter_context(nc.semaphore(f"sc_{q}{i}"))
                for q in "QG" for i in range(self.NCH)}
        einc = {k: 0 for k in "TVSGQ"}
        ccnt = {(q, i): 0 for q in "QG" for i in range(self.NCH)}
        water = {k: {} for k in "TVSGQ"}
        nwaits = 0
        for o in ops:
            if o.eng == "BAR":
                targets = [(("E", k), esem[k], einc[k])
                           for k in "TVSGQ" if einc[k] > 0]
                targets += [(("C", c), csem[c], ccnt[c])
                            for c in ccnt if ccnt[c] > 0]
                for k in "TVSGQ":
                    eng = engines[k]
                    for key, sem, val in targets:
                        if key == ("E", k):
                            continue
                        if water[k].get(key, 0) >= val:
                            continue
                        eng.wait_ge(sem, val)
                        water[k][key] = val
                        nwaits += 1
                continue
            eng = engines[o.eng]
            for d in sorted(o.deps):
                p = ops[d]
                if p.is_dma:
                    key = ("C", (p.eng, p.chan))
                    assert p.chan_cnt is not None, \
                        f"dma dep not yet emitted op{p.idx}"
                    val = ccnt[(p.eng, p.chan)]
                    sem = csem[(p.eng, p.chan)]
                else:
                    if p.eng == o.eng and p.eng == "T":
                        continue
                    val = p.inc_val
                    if val is None:
                        continue
                    key = ("E", p.eng)
                    sem = esem[p.eng]
                if water[o.eng].get(key, 0) >= val:
                    continue
                eng.wait_ge(sem, val)
                water[o.eng][key] = val
                nwaits += 1
            inst = o.fn(nc)
            if o.is_dma:
                c = (o.eng, o.chan)
                ccnt[c] += 16
                o.chan_cnt = ccnt[c]
                inst.then_inc(csem[c], 16)
            elif o.need_inc:
                einc[o.eng] += 1
                o.inc_val = einc[o.eng]
                inst.then_inc(esem[o.eng], 1)
        if self.dbg:
            print(f"[bld] ops={len(ops)} waits={nwaits} "
                  f"incs={sum(einc.values())}")


class MK:
    def __init__(self, cfg, stages, first, last, dbg=False):
        self.cfg = cfg
        self.stages = stages
        self.first = first
        self.last = last
        self.dbg = dbg
        self.nc = bass.Bass("TRN2", target_bir_lowering=False)
        self.b = Bld(self.nc, dbg=dbg)
        self.ctx = ExitStack()

    def sbuf(self, name, shape, dt, ctx=None):
        return (ctx or self.ctx).enter_context(
            self.nc.sbuf_tensor(name, shape, dt))

    def psumt(self, name, shape, dt):
        t = self.ctx.enter_context(self.nc.psum_tensor(name, shape, dt))
        self.b.psum_names.add(name)
        return t

    def R(self, t, p0, p1, f0, f1):
        return self.b.reg(t, p0, p1, f0, f1)

    def RF(self, t):
        return self.b.reg(t, 0, 128, 0, 1 << 30)

    def mm(self, out_t, out_ap, lhs_t, lhs_ap, rhs_t, rhs_ap, start, stop,
           lr=None, rr=None, tag="mm"):
        lreg = lr or self.RF(lhs_t)
        rreg = rr or self.RF(rhs_t)
        self.b.op("T", lambda nc, o=out_ap, l=lhs_ap, r=rhs_ap, s1=start,
                  s2=stop: nc.tensor.matmul(o, l, r, start=s1, stop=s2),
                  rd=[lreg, rreg], wr=[self.RF(out_t)], tag=tag)

    def trans(self, out_t, out_ap, in_t, in_ap, in_reg, tag="tr"):
        self.b.op("T", lambda nc, o=out_ap, i=in_ap:
                  nc.tensor.transpose(o, i, self.ident[:, :]),
                  rd=[in_reg, self.RF(self.ident)], wr=[self.RF(out_t)],
                  tag=tag)

    def copy(self, eng, dst_t, dst_ap, dst_reg, src_t, src_ap, src_reg,
             tag="cp"):
        if eng == "S":
            self.b.op("S", lambda nc, d=dst_ap, s=src_ap:
                      nc.scalar.copy(d, s), rd=[src_reg], wr=[dst_reg],
                      tag=tag)
        else:
            self.b.op("V", lambda nc, d=dst_ap, s=src_ap:
                      nc.vector.tensor_copy(d, s), rd=[src_reg], wr=[dst_reg],
                      tag=tag)

    def load_w(self, whbm_sw, j, width, dst, tag="w"):
        self.b.dma("G", lambda nc, s=whbm_sw, dd=dst, jj=j, w=width:
                   nc.gpsimd.dma_start(dd[:, 0:w], s[jj, :, :]),
                   wr=[self.R(dst, 0, 128, 0, width)], tag=tag)

    def ln_T(self, buf, stride, c0, W, tag="ln"):
        c = self.cfg
        b = self.b
        psA, psB = self.ps2, self.ps3
        for kb in range(c.DB):
            sqa = self.sq[:, 0:W]
            sqr = self.R(self.sq, 0, 128, 0, W)
            xa = buf[:, kb * stride + c0: kb * stride + c0 + W]
            xr = self.R(buf, 0, 128, kb * stride + c0, kb * stride + c0 + W)
            b.op("S", lambda nc, d=sqa, s=xa: nc.scalar.square(d, s),
                 rd=[xr], wr=[sqr], tag=tag + "sq")
            self.mm(psA, psA[0:1, 0:W], self.onesc, self.onesc[:, :],
                    buf, xa, kb == 0, kb == c.DB - 1, rr=xr, tag=tag + "s")
            self.mm(psB, psB[0:1, 0:W], self.onesc, self.onesc[:, :],
                    self.sq, sqa, kb == 0, kb == c.DB - 1, rr=sqr,
                    tag=tag + "q")
        R_ = self.rows
        inv = 1.0 / c.D
        b.op("V", lambda nc, w=W: nc.vector.tensor_scalar_mul(
            R_[0:1, 0:w], psA[0:1, 0:w], inv),
            rd=[self.RF(psA)], wr=[self.R(R_, 0, 1, 0, 512)], tag=tag + "m")
        b.op("V", lambda nc, w=W: nc.vector.tensor_scalar_mul(
            R_[0:1, 512:512 + w], psB[0:1, 0:w], inv),
            rd=[self.RF(psB)], wr=[self.R(R_, 0, 1, 512, 1024)],
            tag=tag + "m2")
        Rb = self.rowsb
        b.op("V", lambda nc, w=W: nc.vector.tensor_copy(
            Rb[0:1, 0:w], R_[0:1, 0:w]),
            rd=[self.R(R_, 0, 1, 0, 512)], wr=[self.R(Rb, 0, 1, 0, 512)],
            tag=tag + "cb")
        b.op("V", lambda nc, w=W: nc.vector.tensor_mul(
            R_[0:1, 0:w], R_[0:1, 0:w], R_[0:1, 0:w]),
            rd=[self.R(R_, 0, 1, 0, 512)], wr=[self.R(R_, 0, 1, 0, 512)],
            tag=tag + "mm")
        b.op("V", lambda nc, w=W: nc.vector.tensor_sub(
            R_[0:1, 512:512 + w], R_[0:1, 512:512 + w], R_[0:1, 0:w]),
            rd=[self.R(R_, 0, 1, 512, 1024), self.R(R_, 0, 1, 0, 512)],
            wr=[self.R(R_, 0, 1, 512, 1024)], tag=tag + "var")
        b.op("V", lambda nc, w=W: nc.vector.tensor_scalar_add(
            R_[0:1, 512:512 + w], R_[0:1, 512:512 + w], EPS_LN),
            rd=[self.R(R_, 0, 1, 512, 1024)], wr=[self.R(R_, 0, 1, 512, 1024)],
            tag=tag + "ve")
        b.op("S", lambda nc, w=W: nc.scalar.activation(
            R_[0:1, 0:w], R_[0:1, 512:512 + w], AF.Ln),
            rd=[self.R(R_, 0, 1, 512, 1024)], wr=[self.R(R_, 0, 1, 0, 512)],
            tag=tag + "ln")
        b.op("V", lambda nc, w=W: nc.vector.tensor_scalar_mul(
            R_[0:1, 0:w], R_[0:1, 0:w], -0.5),
            rd=[self.R(R_, 0, 1, 0, 512)], wr=[self.R(R_, 0, 1, 0, 512)],
            tag=tag + "vh")
        b.op("S", lambda nc, w=W: nc.scalar.activation(
            R_[0:1, 512:512 + w], R_[0:1, 0:w], AF.Exp),
            rd=[self.R(R_, 0, 1, 0, 512)], wr=[self.R(R_, 0, 1, 512, 1024)],
            tag=tag + "ex")
        b.op("V", lambda nc, w=W: nc.vector.tensor_copy(
            Rb[0:1, 512:512 + w], R_[0:1, 512:512 + w]),
            rd=[self.R(R_, 0, 1, 512, 1024)], wr=[self.R(Rb, 0, 1, 512, 1024)],
            tag=tag + "cb2")
        self.mm(psA, psA[0:128, 0:W], self.ones1, self.ones1[0:1, 0:128],
                Rb, Rb[0:1, 0:W], True, True,
                rr=self.R(Rb, 0, 1, 0, 512), tag=tag + "bc")
        self.mm(psB, psB[0:128, 0:W], self.ones1, self.ones1[0:1, 0:128],
                Rb, Rb[0:1, 512:512 + W], True, True,
                rr=self.R(Rb, 0, 1, 512, 1024), tag=tag + "bc2")
        self.copy("S", self.mb, self.mb[:, 0:W], self.R(self.mb, 0, 128, 0, W),
                  psA, psA[:, 0:W], self.RF(psA), tag=tag + "eb")
        self.copy("S", self.rb, self.rb[:, 0:W], self.R(self.rb, 0, 128, 0, W),
                  psB, psB[:, 0:W], self.RF(psB), tag=tag + "eb2")
        for kb in range(c.DB):
            xa = buf[:, kb * stride + c0: kb * stride + c0 + W]
            xr = self.R(buf, 0, 128, kb * stride + c0, kb * stride + c0 + W)
            b.op("V", lambda nc, x=xa, w=W: nc.vector.tensor_sub(
                x, x, self.mb[:, 0:w]),
                rd=[xr, self.R(self.mb, 0, 128, 0, W)], wr=[xr],
                tag=tag + "ap1")
            b.op("V", lambda nc, x=xa, w=W: nc.vector.tensor_mul(
                x, x, self.rb[:, 0:w]),
                rd=[xr, self.R(self.rb, 0, 128, 0, W)], wr=[xr],
                tag=tag + "ap2")

    def proj_T(self, whbm, wcols, jobs, tag="pj"):
        c = self.cfg
        nj = wcols // 128
        for j in range(nj + 1):
            if j < nj:
                self.load_w(whbm, j, c.DB * 128, self.wb[j % 2],
                            tag=tag + "w")
            if j == 0:
                continue
            jj = j - 1
            wbuf = self.wb[jj % 2]
            for job in jobs:
                src, sstride, sc0 = job["src"], job["sstride"], job["sc0"]
                ntok = job["ntok"]
                t0 = 0
                while t0 < ntok:
                    nt = min(512, ntok - t0)
                    ps = self.ps0 if (jj + t0 // 512) % 2 == 0 else self.ps1
                    for kb in range(c.DB):
                        self.mm(ps, ps[:, 0:nt],
                                wbuf, wbuf[:, kb * 128:(kb + 1) * 128],
                                src,
                                src[:, kb * sstride + sc0 + t0:
                                    kb * sstride + sc0 + t0 + nt],
                                kb == 0, kb == c.DB - 1,
                                lr=self.R(wbuf, 0, 128, kb * 128,
                                          (kb + 1) * 128),
                                rr=self.R(src, 0, 128, kb * sstride + sc0 + t0,
                                          kb * sstride + sc0 + t0 + nt),
                                tag=tag)
                    job["epi"](jj, t0, nt, ps)
                    t0 += nt

    def epi_copy(self, dst, dstride, dc0):
        def epi(j, t0, nt, ps):
            da = dst[:, j * dstride + dc0 + t0: j * dstride + dc0 + t0 + nt]
            dr = self.R(dst, 0, 128, j * dstride + dc0 + t0,
                        j * dstride + dc0 + t0 + nt)
            self.copy("S", dst, da, dr, ps, ps[:, 0:nt], self.RF(ps))
        return epi

    def epi_radd(self, dst, dstride, dc0):
        def epi(j, t0, nt, ps):
            da = dst[:, j * dstride + dc0 + t0: j * dstride + dc0 + t0 + nt]
            dr = self.R(dst, 0, 128, j * dstride + dc0 + t0,
                        j * dstride + dc0 + t0 + nt)
            self.b.op("V", lambda nc, d=da, p=ps, n=nt:
                      nc.vector.tensor_add(d, d, p[:, 0:n]),
                      rd=[dr, self.RF(ps)], wr=[dr], tag="radd")
        return epi

    def proj_N(self, whbm, wcols, src, sstride, sc0, ntok, epi, tag="pn"):
        c = self.cfg
        nj = wcols // 128
        ntt = ntok // 128
        for j in range(nj + 1):
            if j < nj:
                self.load_w(whbm, j, c.DB * 128, self.wb[j % 2],
                            tag=tag + "w")
            if j == 0:
                continue
            jj = j - 1
            wbuf = self.wb[jj % 2]
            for tt in range(ntt):
                ps = self.ps0 if (jj + tt) % 2 == 0 else self.ps1
                for kb in range(c.DB):
                    sa = src[:, kb * sstride + sc0 + tt * 128:
                             kb * sstride + sc0 + tt * 128 + 128]
                    self.mm(ps, ps[:, 0:128], src, sa,
                            wbuf, wbuf[:, kb * 128:(kb + 1) * 128],
                            kb == 0, kb == c.DB - 1,
                            lr=self.R(src, 0, 128, kb * sstride + sc0
                                      + tt * 128,
                                      kb * sstride + sc0 + tt * 128 + 128),
                            rr=self.R(wbuf, 0, 128, kb * 128, (kb + 1) * 128),
                            tag=tag)
                epi(jj, tt, ps)

    def build(self):
        c = self.cfg
        nc = self.nc
        b = self.b
        st = self.stages

        if self.first:
            self.i_xT = nc.dram_tensor("xT", [c.D, c.T], F32,
                                       kind="ExternalInput").ap()
        else:
            self.i_xst = nc.dram_tensor("xstate", [128, c.DB * c.T], BF16,
                                        kind="ExternalInput").ap()
            self.i_hst = nc.dram_tensor("halostate", [128, c.DB * 256], BF16,
                                        kind="ExternalInput").ap()
        if "l0a" in st:
            self.i_preT = nc.dram_tensor("xpreT", [c.D, c.PRE], F32,
                                         kind="ExternalInput").ap()
            self.i_wq0 = nc.dram_tensor("l0WqT", [c.D // 128, 128, c.DB * 128], F32,
                                        kind="ExternalInput").ap()
            self.i_wk0 = nc.dram_tensor("l0WkT", [c.D // 128, 128, c.DB * 128], F32,
                                        kind="ExternalInput").ap()
            self.i_wv0 = nc.dram_tensor("l0WvT", [c.D // 128, 128, c.DB * 128], F32,
                                        kind="ExternalInput").ap()
            self.i_wo0 = nc.dram_tensor("l0WoT", [c.D // 128, 128, c.DB * 128], F32,
                                        kind="ExternalInput").ap()
            self.i_wfq = nc.dram_tensor("l0Wfq", [128, c.NPAIR * 128], F32,
                                        kind="ExternalInput").ap()
            self.i_wfk = nc.dram_tensor("l0Wfk", [128, c.NPAIR * 128], F32,
                                        kind="ExternalInput").ap()
        if "l0f" in st:
            self.i_fw = nc.dram_tensor("l0fWT", [c.D // 128, 128, c.DB * 128], F32,
                                       kind="ExternalInput").ap()
            self.i_fs = nc.dram_tensor("l0fscale", [c.D], F32,
                                       kind="ExternalInput").ap()
            self.i_fb = nc.dram_tensor("l0fbias", [c.D], F32,
                                       kind="ExternalInput").ap()
        if "l1a" in st:
            self.i_wq1 = nc.dram_tensor("l1WqT", [c.D // 128, 128, c.DB * 128], F32,
                                        kind="ExternalInput").ap()
            self.i_wk1 = nc.dram_tensor("l1WkT", [c.KV1 // 128, 128, c.DB * 128], F32,
                                        kind="ExternalInput").ap()
            self.i_wv1 = nc.dram_tensor("l1WvT", [c.KV1 // 128, 128, c.DB * 128], F32,
                                        kind="ExternalInput").ap()
            self.i_wo1 = nc.dram_tensor("l1WoT", [c.D // 128, 128, c.DB * 128], F32,
                                        kind="ExternalInput").ap()
            self.i_mask = nc.dram_tensor("l1mask", [2, 128, 140], F32,
                                         kind="ExternalInput").ap()
        if "moe" in st:
            self.i_wr = nc.dram_tensor("rWT", [c.D, c.E], F32,
                                       kind="ExternalInput").ap()
            self.i_wsg = nc.dram_tensor("sgWT", [c.D, 1], F32,
                                        kind="ExternalInput").ap()
            self.i_wsgu = nc.dram_tensor("WsGU", [c.NSH * c.NIB, 128, c.DB * 256], F32,
                                         kind="ExternalInput").ap()
            self.i_wsdn = nc.dram_tensor("WsDN", [c.NSH * ((c.D + 511) // 512), 128, c.NIB * min(512, c.D)], F32,
                                         kind="ExternalInput").ap()
            self.i_wegu = nc.dram_tensor("WeGU", [c.E * c.NIB, 128, c.DB * 256], F32,
                                         kind="ExternalInput").ap()
            self.i_wedn = nc.dram_tensor("WeDN", [c.E * ((c.D + 511) // 512), 128, c.NIB * min(512, c.D)], F32,
                                         kind="ExternalInput").ap()
        self.i_ident = nc.dram_tensor("c_ident", [128, 128], F32,
                                      kind="ExternalInput").ap()
        self.i_tri = nc.dram_tensor("c_tri", [128, 128], F32,
                                    kind="ExternalInput").ap()
        self.i_ones = nc.dram_tensor("c_ones", [128, 1], F32,
                                     kind="ExternalInput").ap()
        self.i_flag = nc.dram_tensor("c_flag", [128, 1], F32,
                                     kind="ExternalInput").ap()
        if self.last:
            self.o_out = nc.dram_tensor("out", [c.T, c.D], F32,
                                        kind="ExternalOutput").ap()
        if not self.last:
            self.o_xst = nc.dram_tensor("xstate_o", [128, c.DB * c.T], BF16,
                                        kind="ExternalOutput").ap()
            self.o_hst = nc.dram_tensor("halostate_o", [128, c.DB * 256],
                                        BF16, kind="ExternalOutput").ap()

        self.xT = self.sbuf("xT_sb", [128, c.DB * c.T], BF16)
        self.wb = [self.sbuf(f"wb{i}", [128, c.DB * 128], BF16)
                   for i in range(2)]
        self.ident = self.sbuf("ident", [128, 128], BF16)
        self.ones1 = self.sbuf("ones1", [1, 128], BF16)
        self.onesc = self.sbuf("onesc", [128, 1], BF16)
        self.tri = self.sbuf("tri", [128, 128], BF16)
        self.flag = self.sbuf("flag", [128, 1], F32)

        self.ps0 = self.psumt("ps0", [128, 512], F32)
        self.ps1 = self.psumt("ps1", [128, 512], F32)
        self.ps2 = self.psumt("ps2", [128, 512], F32)
        self.ps3 = self.psumt("ps3", [128, 512], F32)
        self.ps4 = self.psumt("ps4", [128, 512], F32)
        self.ps5 = self.psumt("ps5", [128, 512], F32)
        self.pb0 = self.psumt("pb0", [128, 1024], BF16)
        self.pb1 = self.psumt("pb1", [128, 1024], BF16)

        for (dst, src) in ((self.ident, self.i_ident), (self.tri, self.i_tri),
                           (self.onesc, self.i_ones)):
            b.dma("G", lambda nc, d=dst, s=src: nc.gpsimd.dma_start(
                d[:, :], s[:, :]), wr=[self.RF(dst)], tag="c")
        b.dma("G", lambda nc: nc.gpsimd.dma_start(
            self.ones1[0:1, :], self.i_ones[:, :].rearrange("p c -> c p")),
            wr=[self.RF(self.ones1)], tag="c1")
        b.dma("Q", lambda nc: nc.sync.dma_start(
            self.flag[:, :], self.i_flag[:, :]), wr=[self.RF(self.flag)],
            tag="cf")

        if self.first:
            src = self.i_xT.rearrange("(bb p) t -> p bb t", p=128)
            dst = self.xT[:, :].rearrange("p (bb t) -> p bb t", t=c.T)
            b.dma("G", lambda nc, s=src, d=dst: nc.gpsimd.dma_start(d, s),
                  wr=[self.RF(self.xT)], tag="ldx")
        else:
            b.dma("Q", lambda nc: nc.sync.dma_start(
                self.xT[:, :], self.i_xst[:, :]), wr=[self.RF(self.xT)],
                tag="ldxs")
        b.bar("init")

        if "l0a" in st or "l0f" in st or "l1a" in st:
            g1 = ExitStack()
            self.halo = self.sbuf("haloX", [128, c.DB * 256], BF16, g1)
            if not self.first:
                b.dma("Q", lambda nc: nc.sync.dma_start(
                    self.halo[:, :], self.i_hst[:, :]),
                    wr=[self.RF(self.halo)], tag="ldhs")
            self.sq = self.sbuf("sq", [128, 512], BF16, g1)
            self.rows = self.sbuf("rows", [1, 1024], F32, g1)
            self.rowsb = self.sbuf("rowsb", [1, 1024], BF16, g1)
            self.mb = self.sbuf("mbc", [128, 512], BF16, g1)
            self.rb = self.sbuf("rbc", [128, 512], BF16, g1)
            self.qT = self.sbuf("qT", [128, c.DB * c.MACRO], BF16, g1)
            self.kT = self.sbuf("kT", [128, c.DB * c.MACRO], BF16, g1)
            self.vna = self.sbuf("vna", [128, c.CPM * c.VW], BF16, g1)
            self.yT = self.sbuf("yT", [128, c.DB * c.MACRO], BF16, g1)
            self.haloY = self.sbuf("haloY", [128, c.DB * 256], BF16, g1)
            self.wfq = self.sbuf("wfq", [128, c.NPAIR * 128], BF16, g1)
            self.wfk = self.sbuf("wfk", [128, c.NPAIR * 128], BF16, g1)
            self.fq2 = self.sbuf("fq2", [128, 512], BF16, g1)
            self.fk2 = self.sbuf("fk2", [128, 512], BF16, g1)
            self.fqT = self.sbuf("fqT", [128, 512], BF16, g1)
            self.fkT = self.sbuf("fkT", [128, 512], BF16, g1)
            self.ATm = self.sbuf("ATm", [128, 1024], BF16, g1)
            self.S32 = self.sbuf("S32", [128, c.NPAIR * 65], F32, g1)
            self.Sbf = self.sbuf("Sbf", [128, c.NPAIR * 65], BF16, g1)
            self.ysc = self.sbuf("ysc", [128, 1024], BF16, g1)
            self.dcol = self.sbuf("dcol", [128, 32], F32, g1)
            self.s2 = self.sbuf("s2", [128, 16], F32, g1)
            self.s2r = self.sbuf("s2r", [128, 16], F32, g1)
            self.mask_sb = self.sbuf("mask_sb", [128, 280], F32, g1)
            self.sc = self.sbuf("scsc", [128, 640], F32, g1)
            self.attn = self.sbuf("attnb", [128, 640], BF16, g1)
            self.T1sb = self.sbuf("T1sb", [128, 512], BF16, g1)
            self.T2sb = self.sbuf("T2sb", [8, 1024], BF16, g1)
            self.stK8 = self.sbuf("stK8", [128, c.L1KV * 8], BF16, g1)
            self.stK4 = self.sbuf("stK4", [128, c.L1KV * 4], BF16, g1)
            self.vst8 = self.sbuf("vst8", [8, c.KV1], BF16, g1)
            self.vst4 = self.sbuf("vst4", [4, c.KV1], BF16, g1)
            self.vroll = self.sbuf("vroll", [8, max(1, c.CPM - 1) * c.KV1], BF16, g1)
            self.khal8 = self.sbuf("khal8", [128, c.L1KV * 8], BF16, g1)
            self.khal4 = self.sbuf("khal4", [128, c.L1KV * 4], BF16, g1)
            self.vhal8 = self.sbuf("vhal8", [8, c.KV1], BF16, g1)
            self.vhal4 = self.sbuf("vhal4", [4, c.KV1], BF16, g1)
            self.mxr = self.sbuf("mxr", [128, 4], F32, g1)
            self.smr = self.sbuf("smr", [128, 4], F32, g1)
            self.rcr = self.sbuf("rcr", [128, 4], F32, g1)
            self.fsb = self.sbuf("fsb", [128, c.DB], F32, g1)
            self.fbb = self.sbuf("fbb", [128, c.DB], F32, g1)

            if "l0a" in st:
                self.stage_l0a()
                b.bar("l0a")
            if "l0f" in st:
                self.stage_l0f()
                b.bar("l0f")
            if "l1a" in st:
                self.stage_l1a()
                b.bar("l1a")
            if not self.last:
                b.dma("Q", lambda nc: nc.sync.dma_start(
                    self.o_xst[:, :], self.xT[:, :]), rd=[self.RF(self.xT)],
                    tag="stx")
                b.dma("Q", lambda nc: nc.sync.dma_start(
                    self.o_hst[:, :], self.halo[:, :]),
                    rd=[self.RF(self.halo)], tag="sth")
                b.bar("ststate")
            g1.close()

        if "moe" in st:
            g2 = ExitStack()
            self.acc = self.sbuf("acc", [128, c.TT * c.D], BF16, g2)
            self.hT = self.sbuf("hT", [128, c.NIB * c.HTOK], BF16, g2)
            self.wd = [self.sbuf(f"wd{i}", [128, c.NIB * 512], BF16,
                                 g2) for i in range(2)]
            self.o32 = self.sbuf("o32", [128, c.D // 2], F32, g2)
            self.comb = self.sbuf("comb", [128, c.TT * c.E], F32, g2)
            self.gatec = self.sbuf("gatec", [128, c.TT], F32, g2)
            self.prb = self.sbuf("prb", [128, c.E], F32, g2)
            self.m1 = self.sbuf("m1r", [128, 1], F32, g2)
            self.m2 = self.sbuf("m2r", [128, 1], F32, g2)
            self.eqm = self.sbuf("eqm", [128, c.E], F32, g2)
            self.wbm = [self.sbuf(f"wbm{i}", [128, c.DB * 256], BF16,
                                  g2) for i in range(2)]
            self.wrb = self.sbuf("wrb", [128, c.DB * c.E], BF16, g2)
            self.wsgb = self.sbuf("wsgb", [128, c.DB], BF16, g2)
            self.sg_scr = self.sbuf("sg_scr", [128, 1024], BF16, g2)
            self.lnr = self.sbuf("lnr", [128, 8], F32, g2)
            self.stage_moe()
            b.bar("moe")
            g2.close()

        b.bar("end")
        self.b.emit(self.ctx)
        return self


    def l0_wave(self, w, nw, qsrc, qstride, qc0, kc0, vtt, ydst, ystride,
                yc0, with_state, do_y, tag):
        c = self.cfg
        b = self.b
        prs = [4 * w + i for i in range(nw)]
        zkb = lambda i: self.ps4 if i % 2 == 0 else self.ps5
        zqb = lambda i: self.ps2 if i % 2 == 0 else self.ps3
        atb = lambda i: self.ps0 if i % 2 == 0 else self.ps1
        dsl = (w % 2) * 16

        for i, pr in enumerate(prs):
            ps = zkb(i)
            col = (i // 2) * 128
            self.mm(ps, ps[:, col:col + 128], self.kT,
                    self.kT[:, pr * c.MACRO + kc0:pr * c.MACRO + kc0 + 128],
                    self.wfk, self.wfk[:, pr * 128:(pr + 1) * 128],
                    True, True,
                    lr=self.R(self.kT, 0, 128, pr * c.MACRO + kc0,
                              pr * c.MACRO + kc0 + 128),
                    rr=self.R(self.wfk, 0, 128, pr * 128, (pr + 1) * 128),
                    tag=tag + "zk")
        for i, pr in enumerate(prs):
            sl = pr % 4
            ps = zkb(i)
            col = (i // 2) * 128
            da = self.fk2[:, sl * 128:sl * 128 + 128]
            dr = self.R(self.fk2, 0, 128, sl * 128, sl * 128 + 128)
            b.op("S", lambda nc, d=da, p=ps, o=col: nc.scalar.activation(
                d, p[:, o:o + 128], AF.Exp),
                rd=[self.RF(ps)], wr=[dr], tag=tag + "ek")
        for i, pr in enumerate(prs):
            sl = pr % 4
            da = self.fk2[:, sl * 128:sl * 128 + 128]
            dr = self.R(self.fk2, 0, 128, sl * 128, sl * 128 + 128)
            d3 = da.rearrange("p (h f) -> p h f", f=64)
            s2a = self.s2[:, sl * 4:sl * 4 + 2]
            s2reg = self.R(self.s2, 0, 128, sl * 4, sl * 4 + 2)
            s2ra = self.s2r[:, sl * 4:sl * 4 + 2]
            s2rreg = self.R(self.s2r, 0, 128, sl * 4, sl * 4 + 2)
            b.op("V", lambda nc, s=d3, o=s2a: nc.vector.reduce_sum(
                o, s, axis=AX.X), rd=[dr], wr=[s2reg], tag=tag + "ks")
            b.op("V", lambda nc, ii=s2a, o=s2ra: nc.vector.reciprocal(o, ii),
                 rd=[s2reg], wr=[s2rreg], tag=tag + "kr")
        for i, pr in enumerate(prs):
            sl = pr % 4
            dr = self.R(self.fk2, 0, 128, sl * 128, sl * 128 + 128)
            s2rreg = self.R(self.s2r, 0, 128, sl * 4, sl * 4 + 2)
            for h2 in range(2):
                b.op("V", lambda nc, s0=sl * 128 + h2 * 64,
                     sc2=self.s2r[:, (pr % 4) * 4 + h2:(pr % 4) * 4 + h2 + 1]:
                     nc.vector.tensor_scalar_mul(
                         self.fk2[:, s0:s0 + 64], self.fk2[:, s0:s0 + 64],
                         sc2),
                     rd=[dr, s2rreg], wr=[dr], tag=tag + "kn")
        if do_y:
            for i, pr in enumerate(prs):
                ps = zqb(i)
                col = (i // 2) * 128
                self.mm(ps, ps[:, col:col + 128], qsrc,
                        qsrc[:, pr * qstride + qc0:pr * qstride + qc0 + 128],
                        self.wfq, self.wfq[:, pr * 128:(pr + 1) * 128],
                        True, True,
                        lr=self.R(qsrc, 0, 128, pr * qstride + qc0,
                                  pr * qstride + qc0 + 128),
                        rr=self.R(self.wfq, 0, 128, pr * 128,
                                  (pr + 1) * 128), tag=tag + "zq")
            for i, pr in enumerate(prs):
                sl = pr % 4
                ps = zqb(i)
                col = (i // 2) * 128
                da = self.fq2[:, sl * 128:sl * 128 + 128]
                dr = self.R(self.fq2, 0, 128, sl * 128, sl * 128 + 128)
                b.op("S", lambda nc, d=da, p=ps, o=col: nc.scalar.activation(
                    d, p[:, o:o + 128], AF.Exp),
                    rd=[self.RF(ps)], wr=[dr], tag=tag + "eq")
            for i, pr in enumerate(prs):
                sl = pr % 4
                self.trans(self.pb0, self.pb0[:, i * 128:i * 128 + 128],
                           self.fq2, self.fq2[:, sl * 128:sl * 128 + 128],
                           self.R(self.fq2, 0, 128, sl * 128, sl * 128 + 128),
                           tag=tag + "tq")
                self.trans(self.pb1, self.pb1[:, i * 128:i * 128 + 128],
                           self.fk2, self.fk2[:, sl * 128:sl * 128 + 128],
                           self.R(self.fk2, 0, 128, sl * 128, sl * 128 + 128),
                           tag=tag + "tk")
            for i, pr in enumerate(prs):
                sl = pr % 4
                b.op("S", lambda nc, s0=sl * 128, o=i * 128:
                     nc.scalar.copy(self.fqT[:, s0:s0 + 128],
                                    self.pb0[:, o:o + 128]),
                     rd=[self.RF(self.pb0)],
                     wr=[self.R(self.fqT, 0, 128, sl * 128, sl * 128 + 128)],
                     tag=tag + "evq")
                b.op("S", lambda nc, s0=sl * 128, o=i * 128:
                     nc.scalar.copy(self.fkT[:, s0:s0 + 128],
                                    self.pb1[:, o:o + 128]),
                     rd=[self.RF(self.pb1)],
                     wr=[self.R(self.fkT, 0, 128, sl * 128, sl * 128 + 128)],
                     tag=tag + "evk")
            for i, pr in enumerate(prs):
                sl = pr % 4
                for h2 in range(2):
                    p0 = h2 * 64
                    ps = atb(i + h2)
                    col = (i // 2) * 256 + h2 * 128
                    self.mm(ps, ps[:, col:col + 128], self.fkT,
                            self.fkT[p0:p0 + 64, sl * 128:sl * 128 + 128],
                            self.fqT,
                            self.fqT[p0:p0 + 64, sl * 128:sl * 128 + 128],
                            True, True,
                            lr=self.R(self.fkT, p0, p0 + 64, sl * 128,
                                      sl * 128 + 128),
                            rr=self.R(self.fqT, p0, p0 + 64, sl * 128,
                                      sl * 128 + 128), tag=tag + "at")
            for i, pr in enumerate(prs):
                for h2 in range(2):
                    ps = atb(i + h2)
                    col = (i // 2) * 256 + h2 * 128
                    ac = (i * 2 + h2) * 128
                    atr = self.R(self.ATm, 0, 128, ac, ac + 128)
                    b.op("V", lambda nc, p=ps, o=col, a=ac:
                         nc.vector.tensor_mul(
                             self.ATm[:, a:a + 128], p[:, o:o + 128],
                             self.tri[:, :]),
                         rd=[self.RF(ps), self.RF(self.tri)], wr=[atr],
                         tag=tag + "msk")
            for i, pr in enumerate(prs):
                sl = pr % 4
                for h2 in range(2):
                    h = 2 * pr + h2
                    p0 = h2 * 64
                    ps = zqb(i)
                    col = (i // 2) * 256 + h2 * 128
                    ac = (i * 2 + h2) * 128
                    va = self.vna[:, vtt * c.VW + h * 65:
                                  vtt * c.VW + h * 65 + 65]
                    vr = self.R(self.vna, 0, 128, vtt * c.VW + h * 65,
                                vtt * c.VW + h * 65 + 65)
                    self.mm(ps, ps[:, col:col + 65], self.ATm,
                            self.ATm[:, ac:ac + 128], self.vna, va,
                            True, not with_state,
                            lr=self.R(self.ATm, 0, 128, ac, ac + 128),
                            rr=vr, tag=tag + "y1")
                    if with_state:
                        sa = self.Sbf[p0:p0 + 64, pr * 65:pr * 65 + 65]
                        sr = self.R(self.Sbf, p0, p0 + 64, pr * 65,
                                    pr * 65 + 65)
                        self.mm(ps, ps[:, col:col + 65], self.fqT,
                                self.fqT[p0:p0 + 64, sl * 128:sl * 128 + 128],
                                self.Sbf, sa,
                                False, True,
                                lr=self.R(self.fqT, p0, p0 + 64, sl * 128,
                                          sl * 128 + 128),
                                rr=sr, tag=tag + "y2")
            for bi, ps in ((0, self.ps2), (1, self.ps3)):
                idxs = [i for i in range(nw) if i % 2 == bi]
                if not idxs:
                    continue
                nt = 2 * len(idxs)
                base = (idxs[0] // 2) * 256 + 64
                pt = ps.tensor if hasattr(ps, "tensor") else ps
                dsrc = bass.AP(tensor=pt, offset=base,
                               ap=[[512, 128], [128, nt], [1, 1]])
                dca = self.dcol[:, dsl + bi * 8:dsl + bi * 8 + nt]
                dcr = self.R(self.dcol, 0, 128, dsl + bi * 8,
                             dsl + bi * 8 + nt)
                b.op("V", lambda nc, s=dsrc, d=dca: nc.vector.tensor_scalar_add(
                    d, s, EPS_LA), rd=[self.RF(ps)], wr=[dcr],
                    tag=tag + "dn")
                dca2 = self.dcol[:, dsl + bi * 8 + 4:dsl + bi * 8 + 4 + nt]
                dcr2 = self.R(self.dcol, 0, 128, dsl + bi * 8 + 4,
                              dsl + bi * 8 + 4 + nt)
                b.op("V", lambda nc, ii=dca, o=dca2: nc.vector.reciprocal(
                    o, ii), rd=[dcr], wr=[dcr2], tag=tag + "rc")
            ybase = (w % 2) * 512
            for i, pr in enumerate(prs):
                for h2 in range(2):
                    ps = zqb(i)
                    col = (i // 2) * 256 + h2 * 128
                    bi = i % 2
                    ri = dsl + bi * 8 + 4 + (i // 2) * 2 + h2
                    yo = ybase + (i * 2 + h2) * 64
                    b.op("V", lambda nc, p=ps, o=col, r=ri, d=yo:
                         nc.vector.tensor_scalar_mul(
                             self.ysc[:, d:d + 64], p[:, o:o + 64],
                             self.dcol[:, r:r + 1]),
                         rd=[self.RF(ps),
                             self.R(self.dcol, 0, 128, ri, ri + 1)],
                         wr=[self.R(self.ysc, 0, 128, yo, yo + 64)],
                         tag=tag + "ys")
            for i, pr in enumerate(prs):
                for h2 in range(2):
                    yo = ybase + (i * 2 + h2) * 64
                    pb = self.pb0 if h2 == 0 else self.pb1
                    self.trans(pb, pb[0:64, 512 + i * 128:512 + i * 128 + 128],
                               self.ysc, self.ysc[:, yo:yo + 64],
                               self.R(self.ysc, 0, 128, yo, yo + 64),
                               tag=tag + "ty")
            for h2 in range(2):
                pb = self.pb0 if h2 == 0 else self.pb1
                ssrc = pb[0:64, 512:512 + nw * 128].rearrange(
                    "p (j r) -> p j r", r=128)
                ddst = ydst[h2 * 64:h2 * 64 + 64, :].rearrange(
                    "p (j r) -> p j r", r=ystride)[
                    :, prs[0]:prs[0] + nw, yc0:yc0 + 128]
                b.op("S", lambda nc, s=ssrc, d=ddst: nc.scalar.copy(d, s),
                     rd=[self.RF(pb)],
                     wr=[self.R(ydst, h2 * 64, h2 * 64 + 64,
                                prs[0] * ystride + yc0,
                                (prs[-1] + 1) * ystride)],
                     tag=tag + "ye")
        for i, pr in enumerate(prs):
            sl = pr % 4
            ps = zkb(i)
            col = 256 + (i // 2) * 128
            for h2 in range(2):
                h = 2 * pr + h2
                fa = self.fk2[:, sl * 128 + h2 * 64:sl * 128 + h2 * 64 + 64]
                fr = self.R(self.fk2, 0, 128, sl * 128 + h2 * 64,
                            sl * 128 + h2 * 64 + 64)
                va = self.vna[:, vtt * c.VW + h * 65:vtt * c.VW + h * 65 + 65]
                vr = self.R(self.vna, 0, 128, vtt * c.VW + h * 65,
                            vtt * c.VW + h * 65 + 65)
                self.mm(ps, ps[h2 * 64:h2 * 64 + 64, col:col + 65],
                        self.fk2, fa, self.vna, va, True, True,
                        lr=fr, rr=vr, tag=tag + "ds")
        for bi, ps in ((0, self.ps4), (1, self.ps5)):
            idxs = [i for i in range(nw) if i % 2 == bi]
            if not idxs:
                continue
            pt = ps.tensor if hasattr(ps, "tensor") else ps
            ssrc = bass.AP(tensor=pt, offset=256 + (idxs[0] // 2) * 128,
                           ap=[[512, 128], [128, len(idxs)], [1, 65]])
            st0 = self.S32.tensor if hasattr(self.S32, "tensor") else self.S32
            sdst = bass.AP(tensor=st0, offset=(prs[0] + bi) * 65,
                           ap=[[self.S32.shape[1], 128], [130, len(idxs)],
                               [1, 65]])
            sreg = self.R(self.S32, 0, 128, prs[0] * 65,
                          (prs[-1] + 1) * 65)
            b.op("V", lambda nc, s=ssrc, d=sdst: nc.vector.tensor_tensor(
                d, d, s, op=ALU.add), rd=[sreg, self.RF(ps)], wr=[sreg],
                tag=tag + "sa")
        ca = self.Sbf[:, prs[0] * 65:(prs[-1] + 1) * 65]
        cr = self.R(self.Sbf, 0, 128, prs[0] * 65, (prs[-1] + 1) * 65)
        b.op("S", lambda nc, d=ca: nc.scalar.copy(
            d, self.S32[:, prs[0] * 65:(prs[-1] + 1) * 65]),
            rd=[self.R(self.S32, 0, 128, prs[0] * 65, (prs[-1] + 1) * 65)],
            wr=[cr], tag=tag + "sc")

    def l0_featmap(self, srcT, c0, pr, sl, is_q, tag):
        c = self.cfg
        b = self.b
        wf = self.wfq if is_q else self.wfk
        dst = self.fq2 if is_q else self.fk2
        ps = self.ps4 if pr % 2 == 0 else self.ps5
        self.mm(ps, ps[:, 0:128], srcT,
                srcT[:, pr * c.MACRO + c0:pr * c.MACRO + c0 + 128],
                wf, wf[:, pr * 128:(pr + 1) * 128], True, True,
                lr=self.R(srcT, 0, 128, 0, 1 << 30),
                rr=self.R(wf, 0, 128, pr * 128, (pr + 1) * 128), tag=tag + "z")
        da = dst[:, sl * 128:sl * 128 + 128]
        dr = self.R(dst, 0, 128, sl * 128, sl * 128 + 128)
        b.op("S", lambda nc, d=da, p=ps: nc.scalar.activation(d, p[:, 0:128],
                                                              AF.Exp),
             rd=[self.RF(ps)], wr=[dr], tag=tag + "e")
        if not is_q:
            d3 = da.rearrange("p (h f) -> p h f", f=64)
            s2a = self.s2[:, sl * 4:sl * 4 + 2]
            s2reg = self.R(self.s2, 0, 128, sl * 4, sl * 4 + 2)
            s2ra = self.s2r[:, sl * 4:sl * 4 + 2]
            s2rreg = self.R(self.s2r, 0, 128, sl * 4, sl * 4 + 2)
            b.op("V", lambda nc, s=d3, o=s2a: nc.vector.reduce_sum(
                o, s, axis=AX.X),
                rd=[dr], wr=[s2reg], tag=tag + "s")
            b.op("V", lambda nc, i=s2a, o=s2ra: nc.vector.reciprocal(o, i),
                rd=[s2reg], wr=[s2rreg], tag=tag + "r")
            for h2 in range(2):
                b.op("V", lambda nc, d=dst, s0=sl * 128 + h2 * 64,
                     sc2=self.s2r[:, sl * 4 + h2:sl * 4 + h2 + 1]:
                     nc.vector.tensor_scalar_mul(
                         d[:, s0:s0 + 64], d[:, s0:s0 + 64], sc2),
                     rd=[dr, s2rreg], wr=[dr], tag=tag + "n")

    def l0_attn_chunk(self, glob_zero, vtt, ytag, ydst, yc0, with_state,
                      pr, sl):
        c = self.cfg
        b = self.b
        self.trans(self.pb0, self.pb0[:, 0:128], self.fq2,
                   self.fq2[:, sl * 128:sl * 128 + 128],
                   self.R(self.fq2, 0, 128, sl * 128, sl * 128 + 128),
                   tag=ytag + "tq")
        self.copy("S", self.fqT, self.fqT[:, sl * 128:sl * 128 + 128],
                  self.R(self.fqT, 0, 128, sl * 128, sl * 128 + 128),
                  self.pb0, self.pb0[:, 0:128], self.RF(self.pb0),
                  tag=ytag + "eq")
        self.trans(self.pb1, self.pb1[:, 0:128], self.fk2,
                   self.fk2[:, sl * 128:sl * 128 + 128],
                   self.R(self.fk2, 0, 128, sl * 128, sl * 128 + 128),
                   tag=ytag + "tk")
        self.copy("S", self.fkT, self.fkT[:, sl * 128:sl * 128 + 128],
                  self.R(self.fkT, 0, 128, sl * 128, sl * 128 + 128),
                  self.pb1, self.pb1[:, 0:128], self.RF(self.pb1),
                  tag=ytag + "ek")
        atp = self.ps0 if pr % 2 == 0 else self.ps1
        yp = self.ps2 if pr % 2 == 0 else self.ps3
        bcp = self.ps5 if pr % 2 == 0 else self.ps4
        dn0 = sl * 256
        for h2 in range(2):
            h = 2 * pr + h2
            p0 = h2 * 64
            self.mm(atp, atp[0:128, h2 * 128:h2 * 128 + 128],
                    self.fkT, self.fkT[p0:p0 + 64, sl * 128:sl * 128 + 128],
                    self.fqT, self.fqT[p0:p0 + 64, sl * 128:sl * 128 + 128],
                    True, True,
                    lr=self.R(self.fkT, p0, p0 + 64, sl * 128, sl * 128 + 128),
                    rr=self.R(self.fqT, p0, p0 + 64, sl * 128, sl * 128 + 128),
                    tag=ytag + "at")
            atm = self.ATm[:, sl * 128:sl * 128 + 128]
            atr = self.R(self.ATm, 0, 128, sl * 128, sl * 128 + 128)
            b.op("V", lambda nc, d=atm, p=atp, o=h2 * 128:
                 nc.vector.tensor_mul(
                     d, p[0:128, o:o + 128], self.tri[:, :]),
                 rd=[self.RF(atp), self.RF(self.tri)], wr=[atr],
                 tag=ytag + "msk")
            va = self.vna[:, vtt * c.VW + h * 65: vtt * c.VW + h * 65 + 65]
            vr = self.R(self.vna, 0, 128, vtt * c.VW + h * 65,
                        vtt * c.VW + h * 65 + 65)
            self.mm(yp, yp[0:65, h2 * 128:h2 * 128 + 128], self.vna, va,
                    self.ATm, atm, True, not with_state, lr=vr, rr=atr,
                    tag=ytag + "y1")
            if with_state:
                sa = self.Sbf[p0:p0 + 64, pr * 65:pr * 65 + 65]
                sr = self.R(self.Sbf, p0, p0 + 64, pr * 65, pr * 65 + 65)
                self.mm(yp, yp[0:65, h2 * 128:h2 * 128 + 128], self.Sbf, sa,
                        self.fqT,
                        self.fqT[p0:p0 + 64, sl * 128:sl * 128 + 128],
                        False, True, lr=sr,
                        rr=self.R(self.fqT, p0, p0 + 64, sl * 128,
                                  sl * 128 + 128),
                        tag=ytag + "y2")
            b.op("V", lambda nc, p=yp, o=h2 * 128, d0=dn0:
                 nc.vector.tensor_scalar_add(
                     self.rows[0:1, d0:d0 + 128], p[64:65, o:o + 128],
                     EPS_LA),
                 rd=[self.RF(yp)], wr=[self.R(self.rows, 0, 1, dn0,
                                              dn0 + 128)],
                 tag=ytag + "dn")
            b.op("V", lambda nc, d0=dn0: nc.vector.reciprocal(
                self.rows[0:1, d0 + 128:d0 + 256], self.rows[0:1, d0:d0 + 128]),
                rd=[self.R(self.rows, 0, 1, dn0, dn0 + 128)],
                wr=[self.R(self.rows, 0, 1, dn0 + 128, dn0 + 256)],
                tag=ytag + "rc")
            b.op("V", lambda nc, d0=dn0, s0=sl * 128: nc.vector.tensor_copy(
                self.dnR[0:1, s0:s0 + 128],
                self.rows[0:1, d0 + 128:d0 + 256]),
                rd=[self.R(self.rows, 0, 1, dn0 + 128, dn0 + 256)],
                wr=[self.R(self.dnR, 0, 1, sl * 128, sl * 128 + 128)],
                tag=ytag + "rcc")
            self.mm(bcp, bcp[0:64, h2 * 128:h2 * 128 + 128], self.ones1,
                    self.ones1[0:1, 0:64], self.dnR,
                    self.dnR[0:1, sl * 128:sl * 128 + 128],
                    True, True,
                    rr=self.R(self.dnR, 0, 1, sl * 128, sl * 128 + 128),
                    tag=ytag + "bc")
            self.copy("S", self.rb64, self.rb64[0:64, sl * 128:sl * 128 + 128],
                      self.R(self.rb64, 0, 64, sl * 128, sl * 128 + 128),
                      bcp, bcp[0:64, h2 * 128:h2 * 128 + 128], self.RF(bcp),
                      tag=ytag + "eb")
            ya = ydst[p0:p0 + 64, pr * (256 if ydst is self.haloY else c.MACRO)
                      + yc0: pr * (256 if ydst is self.haloY else c.MACRO)
                      + yc0 + 128]
            yr = self.R(ydst, p0, p0 + 64,
                        pr * (256 if ydst is self.haloY else c.MACRO) + yc0,
                        pr * (256 if ydst is self.haloY else c.MACRO)
                        + yc0 + 128)
            b.op("V", lambda nc, y=ya, p=yp, o=h2 * 128, s0=sl * 128:
                 nc.vector.tensor_mul(
                     y, p[0:64, o:o + 128],
                     self.rb64[0:64, s0:s0 + 128]),
                 rd=[self.RF(yp), self.R(self.rb64, 0, 64, sl * 128,
                                         sl * 128 + 128)],
                 wr=[yr], tag=ytag + "yo")

    def l0_state_update(self, vtt, pr, sl, do_cast, tag):
        c = self.cfg
        b = self.b
        dsp = self.ps3 if pr % 2 == 0 else self.ps2
        o65 = 65 if pr % 2 == 0 else 0
        for h2 in range(2):
            h = 2 * pr + h2
            fa = self.fk2[:, sl * 128 + h2 * 64: sl * 128 + h2 * 64 + 64]
            fr = self.R(self.fk2, 0, 128, sl * 128 + h2 * 64,
                        sl * 128 + h2 * 64 + 64)
            va = self.vna[:, vtt * c.VW + h * 65: vtt * c.VW + h * 65 + 65]
            vr = self.R(self.vna, 0, 128, vtt * c.VW + h * 65,
                        vtt * c.VW + h * 65 + 65)
            self.mm(dsp, dsp[h2 * 64:h2 * 64 + 64, 256 + o65:256 + o65 + 65],
                    self.fk2, fa, self.vna, va, True, True, lr=fr, rr=vr,
                    tag=tag + "ds")
        sa = self.S32[:, pr * 65:pr * 65 + 65]
        sr = self.R(self.S32, 0, 128, pr * 65, pr * 65 + 65)
        b.op("V", lambda nc, s=sa, p=dsp, o=256 + o65: nc.vector.tensor_add(
            s, s, p[0:128, o:o + 65]),
            rd=[sr, self.RF(dsp)], wr=[sr], tag=tag + "sa")
        if do_cast:
            ba = self.Sbf[:, pr * 65:pr * 65 + 65]
            br = self.R(self.Sbf, 0, 128, pr * 65, pr * 65 + 65)
            self.copy("S", self.Sbf, ba, br, self.S32, sa, sr, tag=tag + "sc")

    def l0_load_macro_kv(self, srcT, m, wsrcq=None, tag="pkv"):
        c = self.cfg
        b = self.b
        self.proj_T(self.i_wk0, c.D,
                    [dict(src=srcT, sstride=c.MACRO, sc0=0, ntok=c.MACRO,
                          epi=self.epi_copy(self.kT, c.MACRO, 0))],
                    tag=tag + "k")

        def vepi(j, tt, ps):
            da = bass.AP(tensor=self.vna.tensor if hasattr(self.vna, "tensor")
                         else self.vna,
                         offset=tt * c.VW + (2 * j) * 65,
                         ap=[[self.vna.shape[1], 128], [65, 2], [1, 64]])
            pa = ps[:, 0:128].rearrange("p (g f) -> p g f", f=64)
            self.copy("S", self.vna, da,
                      self.R(self.vna, 0, 128, tt * c.VW + 2 * j * 65,
                             tt * c.VW + 2 * j * 65 + 130),
                      ps, pa, self.RF(ps), tag=tag + "ve")
        self.proj_N(self.i_wv0, c.D, srcT, c.MACRO, 0, c.MACRO, vepi,
                    tag=tag + "v")
        ones_ap = bass.AP(tensor=self.vna.tensor if hasattr(self.vna, "tensor")
                          else self.vna, offset=64,
                          ap=[[self.vna.shape[1], 128],
                              [c.VW, c.CPM], [65, c.L0H]])
        b.op("V", lambda nc, a=ones_ap: nc.vector.memset(a, 1.0),
             rd=[], wr=[self.RF(self.vna)], tag=tag + "o")

    def stage_l0a(self):
        c = self.cfg
        b = self.b
        for (dst, src) in ((self.wfq, self.i_wfq), (self.wfk, self.i_wfk)):
            b.dma("G", lambda nc, ss=src, dd=dst: nc.gpsimd.dma_start(
                dd[:, :], ss[:, :]), wr=[self.RF(dst)], tag="wf")
        b.op("V", lambda nc: nc.vector.memset(self.S32[:, :], 0.0),
             rd=[], wr=[self.RF(self.S32)], tag="s0")
        b.op("V", lambda nc: nc.vector.memset(self.Sbf[:, :], 0.0),
             rd=[], wr=[self.RF(self.Sbf)], tag="s0b")

        for pm in range(c.NM):
            src = self.i_preT[:, pm * c.MACRO:(pm + 1) * c.MACRO].rearrange(
                "(bb p) t -> p bb t", p=128)
            dst = self.qT[:, :].rearrange("p (bb t) -> p bb t", t=c.MACRO)
            b.dma("G", lambda nc, s=src, d=dst: nc.gpsimd.dma_start(d, s),
                  wr=[self.RF(self.qT)], tag="ldpre")
            self.l0_load_macro_kv(self.qT, pm, tag=f"p{pm}")
            halo_c = {(0, 0): 0, (c.NM - 1, c.CPM - 1): 1}
            nwv = (c.NPAIR + 3) // 4
            for cc in range(c.CPM):
                glob = pm * c.CPM + cc
                hc = halo_c.get((pm, cc))
                if hc is not None:
                    self.proj_T(self.i_wq0, c.D,
                                [dict(src=self.qT, sstride=c.MACRO,
                                      sc0=cc * 128, ntok=128,
                                      epi=self.epi_copy(self.yT, c.MACRO,
                                                        0))],
                                tag=f"hq{hc}")
                    for kb in range(c.DB):
                        sa = self.qT[:, kb * c.MACRO + cc * 128:
                                     kb * c.MACRO + cc * 128 + 128]
                        da = self.halo[:, kb * 256 + hc * 128:
                                       kb * 256 + hc * 128 + 128]
                        self.copy("V", self.halo, da,
                                  self.R(self.halo, 0, 128,
                                         kb * 256 + hc * 128,
                                         kb * 256 + hc * 128 + 128),
                                  self.qT, sa,
                                  self.R(self.qT, 0, 128,
                                         kb * c.MACRO + cc * 128,
                                         kb * c.MACRO + cc * 128 + 128),
                                  tag="hx")
                for w2 in range(nwv):
                    nw = min(4, c.NPAIR - 4 * w2)
                    if hc is not None:
                        self.l0_wave(w2, nw, self.yT, c.MACRO, 0, cc * 128,
                                     cc, self.haloY, 256, hc * 128,
                                     with_state=(glob != 0), do_y=True,
                                     tag=f"hw{glob}_{w2}")
                    else:
                        self.l0_wave(w2, nw, None, 0, 0, cc * 128, cc,
                                     None, 0, 0, with_state=False,
                                     do_y=False, tag=f"pw{glob}_{w2}")
        b.op("V", lambda nc: nc.vector.tensor_scalar_mul(
            self.S32[:, :], self.S32[:, :], self.flag[:, 0:1]),
            rd=[self.RF(self.S32), self.RF(self.flag)],
            wr=[self.RF(self.S32)], tag="fsc")
        for pr in range(c.NPAIR):
            sa = self.S32[:, pr * 65:pr * 65 + 65]
            sr = self.R(self.S32, 0, 128, pr * 65, pr * 65 + 65)
            ba = self.Sbf[:, pr * 65:pr * 65 + 65]
            br = self.R(self.Sbf, 0, 128, pr * 65, pr * 65 + 65)
            self.copy("S", self.Sbf, ba, br, self.S32, sa, sr, tag="fcc")

        for m in range(c.NM):
            self.proj_T(self.i_wq0, c.D,
                        [dict(src=self.xT, sstride=c.T, sc0=m * c.MACRO,
                              ntok=c.MACRO,
                              epi=self.epi_copy(self.qT, c.MACRO, 0))],
                        tag=f"oq{m}")
            self.proj_T(self.i_wk0, c.D,
                        [dict(src=self.xT, sstride=c.T, sc0=m * c.MACRO,
                              ntok=c.MACRO,
                              epi=self.epi_copy(self.kT, c.MACRO, 0))],
                        tag=f"ok{m}")

            def vepi(j, tt, ps):
                da = bass.AP(tensor=self.vna.tensor
                             if hasattr(self.vna, "tensor") else self.vna,
                             offset=tt * c.VW + (2 * j) * 65,
                             ap=[[self.vna.shape[1], 128], [65, 2], [1, 64]])
                pa = ps[:, 0:128].rearrange("p (g f) -> p g f", f=64)
                self.copy("S", self.vna, da,
                          self.R(self.vna, 0, 128, tt * c.VW + 2 * j * 65,
                                 tt * c.VW + 2 * j * 65 + 130),
                          ps, pa, self.RF(ps), tag=f"ov{m}")
            self.proj_N(self.i_wv0, c.D, self.xT, c.T, m * c.MACRO, c.MACRO,
                        vepi, tag=f"ovv{m}")
            ones_ap = bass.AP(tensor=self.vna.tensor
                              if hasattr(self.vna, "tensor") else self.vna,
                              offset=64,
                              ap=[[self.vna.shape[1], 128],
                                  [c.VW, c.CPM], [65, c.L0H]])
            b.op("V", lambda nc, a=ones_ap: nc.vector.memset(a, 1.0),
                 rd=[], wr=[self.RF(self.vna)], tag=f"oo{m}")
            nwv = (c.NPAIR + 3) // 4
            for cc in range(c.CPM):
                glob = m * c.CPM + cc
                for w2 in range(nwv):
                    nw = min(4, c.NPAIR - 4 * w2)
                    self.l0_wave(w2, nw, self.qT, c.MACRO, cc * 128,
                                 cc * 128, cc, self.yT, c.MACRO, cc * 128,
                                 with_state=True, do_y=True,
                                 tag=f"ow{glob}_{w2}")
            jobs = [dict(src=self.yT, sstride=c.MACRO, sc0=0, ntok=c.MACRO,
                         epi=self.epi_radd(self.xT, c.T, m * c.MACRO))]
            if m == 0:
                jobs.append(dict(src=self.haloY, sstride=256, sc0=0, ntok=256,
                                 epi=self.epi_radd(self.halo, 256, 0)))
            self.proj_T(self.i_wo0, c.D, jobs, tag=f"wo{m}")
            for w0 in range(0, c.MACRO, 512):
                self.ln_T(self.xT, c.T, m * c.MACRO + w0,
                          min(512, c.MACRO - w0), tag=f"l0n{m}_{w0}")
            if m == 0:
                self.ln_T(self.halo, 256, 0, 256, tag="l0nh")

    def stage_l0f(self):
        c = self.cfg
        b = self.b
        for (dst, src) in ((self.fsb, self.i_fs), (self.fbb, self.i_fb)):
            s = src.rearrange("(bb p) -> p bb", p=128)
            b.dma("Q", lambda nc, ss=s, dd=dst: nc.sync.dma_start(
                dd[:, :], ss, allow_slow_non_contiguous=True),
                wr=[self.RF(dst)], tag="fs")

        def mk_epi(dstbuf, dstride):
            def epi(j, t0, nt, ps):
                da = dstbuf[:, j * dstride + t0: j * dstride + t0 + nt]
                dr = self.R(dstbuf, 0, 128, j * dstride + t0,
                            j * dstride + t0 + nt)
                b.op("S", lambda nc, d=da, p=ps, n=nt, jj=j:
                     nc.scalar.activation(d, p[:, 0:n], AF.Identity,
                                          bias=self.fbb[:, jj:jj + 1],
                                          scale=self.fsb[:, jj:jj + 1]),
                     rd=[self.RF(ps), self.RF(self.fsb), self.RF(self.fbb)],
                     wr=[dr], tag="fep")
            return epi

        for m in range(c.NM):
            jobs = [dict(src=self.xT, sstride=c.T, sc0=m * c.MACRO,
                         ntok=c.MACRO, epi=mk_epi(self.yT, c.MACRO))]
            if m == 0:
                jobs.append(dict(src=self.halo, sstride=256, sc0=0, ntok=256,
                                 epi=mk_epi(self.haloY, 256)))
            self.proj_T(self.i_fw, c.D, jobs, tag=f"ff{m}")
            for kb in range(c.DB):
                xa = self.xT[:, kb * c.T + m * c.MACRO:
                             kb * c.T + (m + 1) * c.MACRO]
                xr = self.R(self.xT, 0, 128, kb * c.T + m * c.MACRO,
                            kb * c.T + (m + 1) * c.MACRO)
                ya = self.yT[:, kb * c.MACRO:(kb + 1) * c.MACRO]
                yr = self.R(self.yT, 0, 128, kb * c.MACRO,
                            (kb + 1) * c.MACRO)
                b.op("V", lambda nc, d=xa, f=ya: nc.vector.tensor_add(
                    d, d, f), rd=[xr, yr], wr=[xr], tag="fra")
                if m == 0:
                    ha = self.halo[:, kb * 256:(kb + 1) * 256]
                    hr = self.R(self.halo, 0, 128, kb * 256, (kb + 1) * 256)
                    hy = self.haloY[:, kb * 256:(kb + 1) * 256]
                    hyr = self.R(self.haloY, 0, 128, kb * 256,
                                 (kb + 1) * 256)
                    b.op("V", lambda nc, d=ha, f=hy: nc.vector.tensor_add(
                        d, d, f), rd=[hr, hyr], wr=[hr], tag="fhra")
            for w0 in range(0, c.MACRO, 512):
                self.ln_T(self.xT, c.T, m * c.MACRO + w0,
                          min(512, c.MACRO - w0), tag=f"fn{m}_{w0}")
            if m == 0:
                self.ln_T(self.halo, 256, 0, 256, tag="fnh")

    def stage_l1a(self):
        c = self.cfg
        b = self.b
        s = self.i_mask.rearrange("s p f -> p s f")
        d = self.mask_sb[:, :].rearrange("p (s f) -> p s f", f=140)
        b.dma("Q", lambda nc, ss=s, dd=d: nc.sync.dma_start(dd, ss),
              wr=[self.RF(self.mask_sb)], tag="mk")
        for t in (self.stK8, self.stK4, self.vst8, self.vst4, self.vroll,
                  self.khal8, self.khal4, self.vhal8, self.vhal4):
            b.op("V", lambda nc, tt=t: nc.vector.memset(tt[:, :], 0.0),
                 rd=[], wr=[self.RF(t)], tag="z")

        nkv = c.KV1 // 128
        for j in range(nkv):
            self.load_w(self.i_wk1, j, c.DB * 128, self.wb[j % 2],
                        tag="hwk")
            wbuf = self.wb[j % 2]
            for kb in range(c.DB):
                self.mm(self.ps0, self.ps0[:, 0:8],
                        wbuf, wbuf[:, kb * 128:(kb + 1) * 128],
                        self.halo, self.halo[:, kb * 256 + 248:kb * 256 + 256],
                        kb == 0, kb == c.DB - 1,
                        lr=self.R(wbuf, 0, 128, kb * 128, (kb + 1) * 128),
                        rr=self.R(self.halo, 0, 128, kb * 256 + 248,
                                  kb * 256 + 256), tag="hk8")
            for kb in range(c.DB):
                self.mm(self.ps0, self.ps0[:, 8:12],
                        wbuf, wbuf[:, kb * 128:(kb + 1) * 128],
                        self.halo, self.halo[:, kb * 256:kb * 256 + 4],
                        kb == 0, kb == c.DB - 1,
                        lr=self.R(wbuf, 0, 128, kb * 128, (kb + 1) * 128),
                        rr=self.R(self.halo, 0, 128, kb * 256, kb * 256 + 4),
                        tag="hk4")
            self.copy("S", self.khal8, self.khal8[:, j * 8:j * 8 + 8],
                      self.R(self.khal8, 0, 128, j * 8, j * 8 + 8),
                      self.ps0, self.ps0[:, 0:8], self.RF(self.ps0),
                      tag="hke")
            self.copy("S", self.khal4, self.khal4[:, j * 4:j * 4 + 4],
                      self.R(self.khal4, 0, 128, j * 4, j * 4 + 4),
                      self.ps0, self.ps0[:, 8:12], self.RF(self.ps0),
                      tag="hke4")
        for j in range(nkv):
            self.load_w(self.i_wv1, j, c.DB * 128, self.wb[j % 2],
                        tag="hwv")
            wbuf = self.wb[j % 2]
            for kb in range(c.DB):
                self.mm(self.ps1, self.ps1[0:8, 0:128],
                        self.halo, self.halo[:, kb * 256 + 248:kb * 256 + 256],
                        wbuf, wbuf[:, kb * 128:(kb + 1) * 128],
                        kb == 0, kb == c.DB - 1,
                        lr=self.R(self.halo, 0, 128, kb * 256 + 248,
                                  kb * 256 + 256),
                        rr=self.R(wbuf, 0, 128, kb * 128, (kb + 1) * 128),
                        tag="hv8")
            for kb in range(c.DB):
                self.mm(self.ps1, self.ps1[32:36, 0:128],
                        self.halo, self.halo[:, kb * 256:kb * 256 + 4],
                        wbuf, wbuf[:, kb * 128:(kb + 1) * 128],
                        kb == 0, kb == c.DB - 1,
                        lr=self.R(self.halo, 0, 128, kb * 256, kb * 256 + 4),
                        rr=self.R(wbuf, 0, 128, kb * 128, (kb + 1) * 128),
                        tag="hv4")
            self.copy("S", self.vhal8,
                      self.vhal8[0:8, j * 128:(j + 1) * 128],
                      self.R(self.vhal8, 0, 8, j * 128, (j + 1) * 128),
                      self.ps1, self.ps1[0:8, 0:128], self.RF(self.ps1),
                      tag="hve")
            self.copy("S", self.vhal4,
                      self.vhal4[0:4, j * 128:(j + 1) * 128],
                      self.R(self.vhal4, 0, 4, j * 128, (j + 1) * 128),
                      self.ps1, self.ps1[32:36, 0:128], self.RF(self.ps1),
                      tag="hve4")

        scale1 = 1.0 / np.sqrt(c.HD1)
        for m in range(c.NM):
            self.proj_T(self.i_wq1, c.D,
                        [dict(src=self.xT, sstride=c.T, sc0=m * c.MACRO,
                              ntok=c.MACRO,
                              epi=self.epi_copy(self.qT, c.MACRO, 0))],
                        tag=f"1q{m}")
            self.proj_T(self.i_wk1, c.KV1,
                        [dict(src=self.xT, sstride=c.T, sc0=m * c.MACRO,
                              ntok=c.MACRO,
                              epi=self.epi_copy(self.kT, c.MACRO, 0))],
                        tag=f"1k{m}")

            def vepi(j, tt, ps, m=m):
                da = self.vna[:, tt * c.KV1 + j * 128:
                              tt * c.KV1 + (j + 1) * 128]
                dr = self.R(self.vna, 0, 128, tt * c.KV1 + j * 128,
                            tt * c.KV1 + (j + 1) * 128)
                self.copy("S", self.vna, da, dr, ps, ps[:, 0:128],
                          self.RF(ps), tag=f"1v{m}")
            self.proj_N(self.i_wv1, c.KV1, self.xT, c.T, m * c.MACRO,
                        c.MACRO, vepi, tag=f"1vv{m}")
            nroll = max(1, c.CPM - 1)
            b.dma("G", lambda nc, nr=nroll: nc.gpsimd.dma_start(
                self.vroll[0:8, 0:nr * c.KV1],
                self.vna[120:128, 0:nr * c.KV1]),
                rd=[self.RF(self.vna)], wr=[self.RF(self.vroll)],
                tag=f"vr{m}")
            if m == 0:
                s0 = c.CPM * c.KV1
                scr = self.vna
                sr4 = self.R(scr, 0, 128, s0, s0 + c.L1KV * 4)
                vnr = self.R(self.vna, 0, 128, 0, c.CPM * c.KV1)
                b.op("V", lambda nc, s2=s0: nc.vector.tensor_sub(
                    self.vna[:, s2:s2 + c.L1KV * 4], self.khal4[:, :],
                    _k4own(self, c)),
                    rd=[self.RF(self.khal4),
                        self.R(self.kT, 0, 128, 0, c.L1KV * c.MACRO)],
                    wr=[sr4], tag="sel1")
                b.op("V", lambda nc, s2=s0: nc.vector.tensor_scalar_mul(
                    self.vna[:, s2:s2 + c.L1KV * 4],
                    self.vna[:, s2:s2 + c.L1KV * 4], self.flag[:, 0:1]),
                    rd=[sr4, self.RF(self.flag)], wr=[sr4], tag="sel2")
                b.op("V", lambda nc, s2=s0: nc.vector.tensor_add(
                    self.stK4[:, :], self.vna[:, s2:s2 + c.L1KV * 4],
                    _k4own(self, c)),
                    rd=[sr4, self.R(self.kT, 0, 128, 0, c.L1KV * c.MACRO)],
                    wr=[self.RF(self.stK4)], tag="sel3")
                v0 = s0 + c.L1KV * 4
                sv4 = self.R(scr, 0, 4, v0, v0 + c.KV1)
                b.op("V", lambda nc, s2=v0: nc.vector.tensor_sub(
                    self.vna[0:4, s2:s2 + c.KV1], self.vhal4[:, :],
                    self.vna[0:4, 0:c.KV1]),
                    rd=[self.RF(self.vhal4), vnr],
                    wr=[sv4], tag="sel4")
                b.op("V", lambda nc, s2=v0: nc.vector.tensor_scalar_mul(
                    self.vna[0:4, s2:s2 + c.KV1],
                    self.vna[0:4, s2:s2 + c.KV1], self.flag[0:4, 0:1]),
                    rd=[sv4, self.RF(self.flag)], wr=[sv4], tag="sel5")
                b.op("V", lambda nc, s2=v0: nc.vector.tensor_add(
                    self.vst4[:, :], self.vna[0:4, s2:s2 + c.KV1],
                    self.vna[0:4, 0:c.KV1]),
                    rd=[sv4, vnr],
                    wr=[self.RF(self.vst4)], tag="sel6")
                self.copy("V", self.stK8, self.stK8[:, :],
                          self.RF(self.stK8), self.khal8, self.khal8[:, :],
                          self.RF(self.khal8), tag="sel7")
                self.copy("V", self.vst8, self.vst8[:, :], self.RF(self.vst8),
                          self.vhal8, self.vhal8[:, :], self.RF(self.vhal8),
                          tag="sel8")
            for cc in range(c.CPM):
                glob = m * c.CPM + cc
                slot = 0 if glob == 0 else 1
                def sc_mm(H):
                    kv = H // c.GQ
                    ssl = H % 4
                    scp = self.ps4 if H % 2 == 0 else self.ps5
                    col0 = (ssl // 2) * 256
                    qa = self.qT[:, H * c.MACRO + cc * 128:
                                 H * c.MACRO + cc * 128 + 128]
                    qr = self.R(self.qT, 0, 128, H * c.MACRO + cc * 128,
                                H * c.MACRO + cc * 128 + 128)
                    self.mm(scp, scp[:, col0:col0 + 128], self.qT, qa,
                            self.kT,
                            self.kT[:, kv * c.MACRO + cc * 128:
                                    kv * c.MACRO + cc * 128 + 128],
                            True, True, lr=qr,
                            rr=self.R(self.kT, 0, 128,
                                      kv * c.MACRO + cc * 128,
                                      kv * c.MACRO + cc * 128 + 128),
                            tag=f"sc{glob}_{H}")
                    if cc == 0:
                        self.mm(scp, scp[:, col0 + 128:col0 + 136], self.qT,
                                qa, self.stK8,
                                self.stK8[:, kv * 8:(kv + 1) * 8],
                                True, True, lr=qr,
                                rr=self.R(self.stK8, 0, 128, kv * 8,
                                          (kv + 1) * 8),
                                tag=f"sp{glob}_{H}")
                    else:
                        self.mm(scp, scp[:, col0 + 128:col0 + 136], self.qT,
                                qa, self.kT,
                                self.kT[:, kv * c.MACRO + cc * 128 - 8:
                                        kv * c.MACRO + cc * 128],
                                True, True, lr=qr,
                                rr=self.R(self.kT, 0, 128,
                                          kv * c.MACRO + cc * 128 - 8,
                                          kv * c.MACRO + cc * 128),
                                tag=f"sp{glob}_{H}")
                    self.mm(scp, scp[:, col0 + 136:col0 + 140], self.qT, qa,
                            self.stK4, self.stK4[:, kv * 4:(kv + 1) * 4],
                            True, True, lr=qr,
                            rr=self.R(self.stK4, 0, 128, kv * 4,
                                      (kv + 1) * 4), tag=f"s4{glob}_{H}")

                def sc_soft(H):
                    ssl = H % 4
                    scp = self.ps4 if H % 2 == 0 else self.ps5
                    col0 = (ssl // 2) * 256
                    sca = self.sc[:, ssl * 160:ssl * 160 + 140]
                    scr = self.R(self.sc, 0, 128, ssl * 160, ssl * 160 + 140)
                    b.op("V", lambda nc, d=sca, sl2=slot, p=scp, o=col0:
                         nc.vector.scalar_tensor_tensor(
                             d, p[:, o:o + 140], scale1,
                             self.mask_sb[:, sl2 * 140:sl2 * 140 + 140],
                             ALU.mult, ALU.add),
                         rd=[self.RF(scp), self.RF(self.mask_sb)],
                         wr=[scr], tag=f"msk{glob}_{H}")
                    b.op("V", lambda nc, s2=sca, h=ssl:
                         nc.vector.reduce_max(self.mxr[:, h:h + 1], s2,
                                              axis=AX.X, negate=True),
                         rd=[scr], wr=[self.R(self.mxr, 0, 128, ssl,
                                              ssl + 1)],
                         tag=f"mx{glob}_{H}")

                def sc_exp(H):
                    ssl = H % 4
                    sca = self.sc[:, ssl * 160:ssl * 160 + 140]
                    scr = self.R(self.sc, 0, 128, ssl * 160, ssl * 160 + 140)
                    ata = self.attn[:, ssl * 160:ssl * 160 + 140]
                    atr = self.R(self.attn, 0, 128, ssl * 160,
                                 ssl * 160 + 140)
                    b.op("S", lambda nc, d=ata, s2=sca, h=ssl:
                         nc.scalar.activation(d, s2, AF.Exp,
                                              bias=self.mxr[:, h:h + 1],
                                              accum_out=self.smr[:, h:h + 1]),
                         rd=[scr, self.R(self.mxr, 0, 128, ssl, ssl + 1)],
                         wr=[atr, self.R(self.smr, 0, 128, ssl, ssl + 1)],
                         tag=f"ex{glob}_{H}")

                def sc_norm(H):
                    ssl = H % 4
                    ata = self.attn[:, ssl * 160:ssl * 160 + 140]
                    atr = self.R(self.attn, 0, 128, ssl * 160,
                                 ssl * 160 + 140)
                    b.op("V", lambda nc, h=ssl: nc.vector.reciprocal(
                        self.rcr[:, h:h + 1], self.smr[:, h:h + 1]),
                        rd=[self.R(self.smr, 0, 128, ssl, ssl + 1)],
                        wr=[self.R(self.rcr, 0, 128, ssl, ssl + 1)],
                        tag=f"rc{glob}_{H}")
                    b.op("V", lambda nc, d=ata, h=ssl:
                         nc.vector.tensor_scalar_mul(d, d,
                                                     self.rcr[:, h:h + 1]),
                         rd=[atr, self.R(self.rcr, 0, 128, ssl, ssl + 1)],
                         wr=[atr], tag=f"nm{glob}_{H}")

                def sc_trans(H):
                    ssl = H % 4
                    pbx = self.pb0 if H % 2 == 0 else self.pb1
                    pcol = (ssl // 2) * 384
                    atr = self.R(self.attn, 0, 128, ssl * 160,
                                 ssl * 160 + 140)
                    self.trans(pbx, pbx[:, pcol:pcol + 128], self.attn,
                               self.attn[:, ssl * 160:ssl * 160 + 128],
                               atr, tag=f"t1{glob}_{H}")
                    self.trans(pbx, pbx[0:8, pcol + 128:pcol + 256],
                               self.attn,
                               self.attn[:, ssl * 160 + 128:ssl * 160 + 136],
                               atr, tag=f"t2{glob}_{H}")
                    self.trans(pbx, pbx[0:4, pcol + 256:pcol + 384],
                               self.attn,
                               self.attn[:, ssl * 160 + 136:ssl * 160 + 140],
                               atr, tag=f"t3{glob}_{H}")

                def sc_evac(H):
                    ssl = H % 4
                    pbx = self.pb0 if H % 2 == 0 else self.pb1
                    pcol = (ssl // 2) * 384
                    t1a = self.T1sb[:, ssl * 128:ssl * 128 + 128]
                    t1r = self.R(self.T1sb, 0, 128, ssl * 128,
                                 ssl * 128 + 128)
                    self.copy("S", self.T1sb, t1a, t1r,
                              pbx, pbx[:, pcol:pcol + 128], self.RF(pbx),
                              tag=f"t1e{glob}_{H}")
                    self.copy("S", self.T2sb,
                              self.T2sb[0:8, ssl * 256:ssl * 256 + 128],
                              self.R(self.T2sb, 0, 8, ssl * 256,
                                     ssl * 256 + 128),
                              pbx, pbx[0:8, pcol + 128:pcol + 256],
                              self.RF(pbx), tag=f"t2e{glob}_{H}")
                    self.copy("S", self.T2sb,
                              self.T2sb[0:4, ssl * 256 + 128:ssl * 256 + 256],
                              self.R(self.T2sb, 0, 4, ssl * 256 + 128,
                                     ssl * 256 + 256),
                              pbx, pbx[0:4, pcol + 256:pcol + 384],
                              self.RF(pbx), tag=f"t3e{glob}_{H}")

                def sc_av(H):
                    kv = H // c.GQ
                    ssl = H % 4
                    avp = self.ps2 if H % 2 == 0 else self.ps3
                    acol = (ssl // 2) * 256
                    t1a = self.T1sb[:, ssl * 128:ssl * 128 + 128]
                    t1r = self.R(self.T1sb, 0, 128, ssl * 128,
                                 ssl * 128 + 128)
                    self.mm(avp, avp[:, acol:acol + 128], self.vna,
                            self.vna[:, cc * c.KV1 + kv * 128:
                                     cc * c.KV1 + (kv + 1) * 128],
                            self.T1sb, t1a, True, False,
                            lr=self.R(self.vna, 0, 128,
                                      cc * c.KV1 + kv * 128,
                                      cc * c.KV1 + (kv + 1) * 128),
                            rr=t1r, tag=f"av{glob}_{H}")
                    if cc == 0:
                        self.mm(avp, avp[:, acol:acol + 128], self.vst8,
                                self.vst8[0:8, kv * 128:(kv + 1) * 128],
                                self.T2sb,
                                self.T2sb[0:8, ssl * 256:ssl * 256 + 128],
                                False, False,
                                lr=self.R(self.vst8, 0, 8, kv * 128,
                                          (kv + 1) * 128),
                                rr=self.R(self.T2sb, 0, 8, ssl * 256,
                                          ssl * 256 + 128),
                                tag=f"av8{glob}_{H}")
                    else:
                        self.mm(avp, avp[:, acol:acol + 128], self.vroll,
                                self.vroll[0:8, (cc - 1) * c.KV1 + kv * 128:
                                           (cc - 1) * c.KV1
                                           + (kv + 1) * 128],
                                self.T2sb,
                                self.T2sb[0:8, ssl * 256:ssl * 256 + 128],
                                False, False,
                                lr=self.R(self.vroll, 0, 8,
                                          (cc - 1) * c.KV1 + kv * 128,
                                          (cc - 1) * c.KV1 + (kv + 1) * 128),
                                rr=self.R(self.T2sb, 0, 8, ssl * 256,
                                          ssl * 256 + 128),
                                tag=f"av8{glob}_{H}")
                    self.mm(avp, avp[:, acol:acol + 128], self.vst4,
                            self.vst4[0:4, kv * 128:(kv + 1) * 128],
                            self.T2sb,
                            self.T2sb[0:4, ssl * 256 + 128:ssl * 256 + 256],
                            False, True,
                            lr=self.R(self.vst4, 0, 4, kv * 128,
                                      (kv + 1) * 128),
                            rr=self.R(self.T2sb, 0, 4, ssl * 256 + 128,
                                      ssl * 256 + 256),
                            tag=f"av4{glob}_{H}")

                def sc_ye(H):
                    ssl = H % 4
                    avp = self.ps2 if H % 2 == 0 else self.ps3
                    acol = (ssl // 2) * 256
                    ya = self.yT[:, H * c.MACRO + cc * 128:
                                 H * c.MACRO + cc * 128 + 128]
                    yr = self.R(self.yT, 0, 128, H * c.MACRO + cc * 128,
                                H * c.MACRO + cc * 128 + 128)
                    self.copy("S", self.yT, ya, yr, avp,
                              avp[:, acol:acol + 128], self.RF(avp),
                              tag=f"ye{glob}_{H}")

                for g0 in range(0, c.L1H, 4):
                    hs = range(g0, min(g0 + 4, c.L1H))
                    for H in hs:
                        sc_mm(H)
                    for H in hs:
                        sc_soft(H)
                    for H in hs:
                        sc_exp(H)
                    for H in hs:
                        sc_norm(H)
                    for H in hs:
                        sc_trans(H)
                    for H in hs:
                        sc_evac(H)
                    for H in hs:
                        sc_av(H)
                    for H in hs:
                        sc_ye(H)
            if m < c.NM - 1:
                for j in range(nkv):
                    sa = self.kT[:, j * c.MACRO + c.MACRO - 8:
                                 j * c.MACRO + c.MACRO]
                    self.copy("V", self.stK8, self.stK8[:, j * 8:j * 8 + 8],
                              self.R(self.stK8, 0, 128, j * 8, j * 8 + 8),
                              self.kT, sa,
                              self.R(self.kT, 0, 128,
                                     j * c.MACRO + c.MACRO - 8,
                                     j * c.MACRO + c.MACRO), tag="su")
                b.dma("G", lambda nc: nc.gpsimd.dma_start(
                    self.vst8[:, :],
                    self.vna[120:128, (c.CPM - 1) * c.KV1:c.CPM * c.KV1]),
                    rd=[self.RF(self.vna)], wr=[self.RF(self.vst8)],
                    tag="su2")
            self.proj_T(self.i_wo1, c.D,
                        [dict(src=self.yT, sstride=c.MACRO, sc0=0,
                              ntok=c.MACRO,
                              epi=self.epi_radd(self.xT, c.T, m * c.MACRO))],
                        tag=f"1o{m}")
            for w0 in range(0, c.MACRO, 512):
                self.ln_T(self.xT, c.T, m * c.MACRO + w0,
                          min(512, c.MACRO - w0), tag=f"1n{m}_{w0}")

    def stage_moe(self):
        c = self.cfg
        b = self.b
        s = self.i_wr.rearrange("(bb p) e -> p bb e", p=128)
        d = self.wrb[:, :].rearrange("p (bb e) -> p bb e", e=c.E)
        b.dma("G", lambda nc, ss=s, dd=d: nc.gpsimd.dma_start(dd, ss),
              wr=[self.RF(self.wrb)], tag="rw")
        s2 = self.i_wsg.rearrange("(bb p) e -> p bb e", p=128)
        d2 = self.wsgb[:, :].rearrange("p (bb e) -> p bb e", e=1)
        b.dma("G", lambda nc, ss=s2, dd=d2: nc.gpsimd.dma_start(
            dd, ss, allow_slow_non_contiguous=True),
            wr=[self.RF(self.wsgb)], tag="sgw")
        for tt in range(c.TT):
            ps = self.ps4
            for kb in range(c.DB):
                self.mm(ps, ps[:, 0:c.E], self.xT,
                        self.xT[:, kb * c.T + tt * 128:
                                kb * c.T + tt * 128 + 128],
                        self.wrb, self.wrb[:, kb * c.E:(kb + 1) * c.E],
                        kb == 0, kb == c.DB - 1,
                        lr=self.R(self.xT, 0, 128, kb * c.T + tt * 128,
                                  kb * c.T + tt * 128 + 128),
                        rr=self.R(self.wrb, 0, 128, kb * c.E, (kb + 1) * c.E),
                        tag=f"rt{tt}")
            for kb in range(c.DB):
                self.mm(ps, ps[:, c.E:c.E + 1], self.xT,
                        self.xT[:, kb * c.T + tt * 128:
                                kb * c.T + tt * 128 + 128],
                        self.wsgb, self.wsgb[:, kb:kb + 1],
                        kb == 0, kb == c.DB - 1,
                        lr=self.R(self.xT, 0, 128, kb * c.T + tt * 128,
                                  kb * c.T + tt * 128 + 128),
                        rr=self.R(self.wsgb, 0, 128, kb, kb + 1),
                        tag=f"sg{tt}")
            b.op("V", lambda nc: nc.vector.reduce_max(
                self.m1[:, 0:1], ps[:, 0:c.E], axis=AX.X, negate=True),
                rd=[self.RF(ps)], wr=[self.RF(self.m1)], tag=f"r1{tt}")
            b.op("S", lambda nc: nc.scalar.activation(
                self.prb[:, :], ps[:, 0:c.E], AF.Exp,
                bias=self.m1[:, 0:1], accum_out=self.m2[:, 0:1]),
                rd=[self.RF(ps), self.RF(self.m1)],
                wr=[self.RF(self.prb), self.RF(self.m2)], tag=f"r2{tt}")
            b.op("V", lambda nc: nc.vector.reciprocal(
                self.m1[:, 0:1], self.m2[:, 0:1]),
                rd=[self.RF(self.m2)], wr=[self.RF(self.m1)], tag=f"r3{tt}")
            b.op("V", lambda nc: nc.vector.tensor_scalar_mul(
                self.prb[:, :], self.prb[:, :], self.m1[:, 0:1]),
                rd=[self.RF(self.prb), self.RF(self.m1)],
                wr=[self.RF(self.prb)], tag=f"r4{tt}")
            b.op("V", lambda nc: nc.vector.reduce_max(
                self.m1[:, 0:1], self.prb[:, :], axis=AX.X),
                rd=[self.RF(self.prb)], wr=[self.RF(self.m1)], tag=f"r5{tt}")
            b.op("V", lambda nc: nc.vector.tensor_scalar(
                self.eqm[:, :], self.prb[:, :], self.m1[:, 0:1], None,
                ALU.is_ge), rd=[self.RF(self.prb), self.RF(self.m1)],
                wr=[self.RF(self.eqm)], tag=f"r6{tt}")
            b.op("V", lambda nc: nc.vector.scalar_tensor_tensor(
                self.eqm[:, :], self.eqm[:, :], -2.0, self.prb[:, :],
                ALU.mult, ALU.add),
                rd=[self.RF(self.eqm), self.RF(self.prb)],
                wr=[self.RF(self.eqm)], tag=f"r7{tt}")
            b.op("V", lambda nc: nc.vector.reduce_max(
                self.m2[:, 0:1], self.eqm[:, :], axis=AX.X),
                rd=[self.RF(self.eqm)], wr=[self.RF(self.m2)], tag=f"r8{tt}")
            b.op("V", lambda nc: nc.vector.tensor_scalar(
                self.eqm[:, :], self.prb[:, :], self.m2[:, 0:1], None,
                ALU.is_ge), rd=[self.RF(self.prb), self.RF(self.m2)],
                wr=[self.RF(self.eqm)], tag=f"r9{tt}")
            ca = self.comb[:, tt * c.E:(tt + 1) * c.E]
            cr = self.R(self.comb, 0, 128, tt * c.E, (tt + 1) * c.E)
            b.op("V", lambda nc, d=ca: nc.vector.tensor_mul(
                d, self.prb[:, :], self.eqm[:, :]),
                rd=[self.RF(self.prb), self.RF(self.eqm)], wr=[cr],
                tag=f"rA{tt}")
            self.copy("V", self.gatec, self.gatec[:, tt:tt + 1],
                      self.R(self.gatec, 0, 128, tt, tt + 1),
                      ps, ps[:, c.E:c.E + 1], self.RF(ps), tag=f"ge{tt}")
        b.op("S", lambda nc: nc.scalar.activation(
            self.gatec[:, :], self.gatec[:, :], AF.Sigmoid),
            rd=[self.RF(self.gatec)], wr=[self.RF(self.gatec)], tag="gs")

        TH = c.TH
        HTOK = c.HTOK
        fin_pending = []
        for half in range(TH):
            for pe in range(c.NPE):
                shared = pe < c.NSH
                e = pe - c.NSH
                for ib in range(c.NIB):
                    if shared:
                        wsrc, jj = self.i_wsgu, pe * c.NIB + ib
                    else:
                        wsrc, jj = self.i_wegu, e * c.NIB + ib
                    wbuf = self.wbm[ib % 2]
                    self.load_w(wsrc, jj, c.DB * 256, wbuf, tag=f"eg{pe}")
                    for tb in range((HTOK + 511) // 512):
                        nt = min(512, HTOK - tb * 512)
                        t0 = half * HTOK + tb * 512
                        psg, psu = self.ps0, self.ps1
                        for kb in range(c.DB):
                            self.mm(psg, psg[:, 0:nt], wbuf,
                                    wbuf[:, kb * 256:kb * 256 + 128],
                                    self.xT,
                                    self.xT[:, kb * c.T + t0:
                                            kb * c.T + t0 + nt],
                                    kb == 0, kb == c.DB - 1,
                                    lr=self.R(wbuf, 0, 128, kb * 256,
                                              kb * 256 + 128),
                                    rr=self.R(self.xT, 0, 128, kb * c.T + t0,
                                              kb * c.T + t0 + nt),
                                    tag=f"gg{pe}")
                        for kb in range(c.DB):
                            self.mm(psu, psu[:, 0:nt], wbuf,
                                    wbuf[:, kb * 256 + 128:kb * 256 + 256],
                                    self.xT,
                                    self.xT[:, kb * c.T + t0:
                                            kb * c.T + t0 + nt],
                                    kb == 0, kb == c.DB - 1,
                                    lr=self.R(wbuf, 0, 128, kb * 256 + 128,
                                              kb * 256 + 256),
                                    rr=self.R(self.xT, 0, 128, kb * c.T + t0,
                                              kb * c.T + t0 + nt),
                                    tag=f"gu{pe}")
                        sga = self.sg_scr[:, (tb % 2) * 512:
                                          (tb % 2) * 512 + nt]
                        sgr = self.R(self.sg_scr, 0, 128, (tb % 2) * 512,
                                     (tb % 2) * 512 + nt)
                        b.op("S", lambda nc, d=sga, p=psg, n=nt:
                             nc.scalar.activation(d, p[:, 0:n], AF.Silu),
                             rd=[self.RF(psg)], wr=[sgr], tag=f"sl{pe}")
                        ha = self.hT[:, ib * HTOK + tb * 512:
                                     ib * HTOK + tb * 512 + nt]
                        hr = self.R(self.hT, 0, 128, ib * HTOK + tb * 512,
                                    ib * HTOK + tb * 512 + nt)
                        b.op("V", lambda nc, d=ha, s3=sga, p=psu, n=nt:
                             nc.vector.tensor_mul(d, s3, p[:, 0:n]),
                             rd=[sgr, self.RF(psu)], wr=[hr], tag=f"hm{pe}")
                    if fin_pending:
                        self.moe_finalize_tile(fin_pending.pop(0))
                nd4 = (c.D + 511) // 512
                for d4 in range(nd4):
                    nd = min(512, c.D - d4 * 512)
                    wdb = self.wd[d4 % 2]
                    jj = (pe if shared else e) * nd4 + d4
                    self.load_w(self.i_wsdn if shared else self.i_wedn,
                                jj, c.NIB * nd, wdb, tag=f"dw{pe}")
                    for tl in range(HTOK // 128):
                        ttg = half * (HTOK // 128) + tl
                        psd = self.ps2 if tl % 2 == 0 else self.ps3
                        for ib in range(c.NIB):
                            self.mm(psd, psd[:, 0:nd], self.hT,
                                    self.hT[:, ib * HTOK + tl * 128:
                                            ib * HTOK + tl * 128 + 128],
                                    wdb,
                                    wdb[:, ib * nd:(ib + 1) * nd],
                                    ib == 0, ib == c.NIB - 1,
                                    lr=self.R(self.hT, 0, 128,
                                              ib * HTOK + tl * 128,
                                              ib * HTOK + tl * 128 + 128),
                                    rr=self.R(wdb, 0, 128, ib * nd,
                                              (ib + 1) * nd),
                                    tag=f"dn{pe}")
                        aa = self.acc[:, ttg * c.D + d4 * 512:
                                      ttg * c.D + d4 * 512 + nd]
                        ar = self.R(self.acc, 0, 128, ttg * c.D + d4 * 512,
                                    ttg * c.D + d4 * 512 + nd)
                        if shared:
                            ccol = self.gatec[:, ttg:ttg + 1]
                            creg = self.R(self.gatec, 0, 128, ttg, ttg + 1)
                        else:
                            ccol = self.comb[:, ttg * c.E + e:
                                             ttg * c.E + e + 1]
                            creg = self.R(self.comb, 0, 128, ttg * c.E + e,
                                          ttg * c.E + e + 1)
                        if pe == 0:
                            b.op("V", lambda nc, d=aa, p=psd, cc2=ccol, n=nd:
                                 nc.vector.tensor_scalar_mul(
                                     d, p[:, 0:n], cc2),
                                 rd=[self.RF(psd), creg], wr=[ar],
                                 tag=f"ac{pe}")
                        else:
                            b.op("V", lambda nc, d=aa, p=psd, cc2=ccol, n=nd:
                                 nc.vector.scalar_tensor_tensor(
                                     d, p[:, 0:n], cc2, d,
                                     ALU.mult, ALU.add),
                                 rd=[self.RF(psd), creg, ar], wr=[ar],
                                 tag=f"ac{pe}")
                        if pe == c.NPE - 1 and d4 == nd4 - 1:
                            if half < TH - 1:
                                fin_pending.append(ttg)
                            else:
                                self.moe_finalize_tile(ttg)
        while fin_pending:
            self.moe_finalize_tile(fin_pending.pop(0))

    def moe_finalize_tile(self, tt):
        c = self.cfg
        b = self.b
        HD = c.D // 2
        lr = self.RF(self.lnr)
        for kb in range(c.DB):
            pbx = self.pb0 if kb % 2 == 0 else self.pb1
            pcol = (kb // 2) * 128
            self.trans(pbx, pbx[:, pcol:pcol + 128], self.xT,
                       self.xT[:, kb * c.T + tt * 128:
                               kb * c.T + tt * 128 + 128],
                       self.R(self.xT, 0, 128, kb * c.T + tt * 128,
                              kb * c.T + tt * 128 + 128), tag=f"tx{tt}")
        for kb in range(c.DB):
            pbx = self.pb0 if kb % 2 == 0 else self.pb1
            pcol = (kb // 2) * 128
            aa = self.acc[:, tt * c.D + kb * 128:tt * c.D + kb * 128 + 128]
            ar = self.R(self.acc, 0, 128, tt * c.D + kb * 128,
                        tt * c.D + kb * 128 + 128)
            b.op("V", lambda nc, d=aa, p=pbx, o=pcol: nc.vector.tensor_add(
                d, d, p[:, o:o + 128]),
                rd=[ar, self.RF(pbx)], wr=[ar], tag=f"ra{tt}")
        ta = self.acc[:, tt * c.D:(tt + 1) * c.D]
        tr = self.R(self.acc, 0, 128, tt * c.D, (tt + 1) * c.D)
        b.op("V", lambda nc, s3=ta: nc.vector.reduce_sum(
            self.lnr[:, 0:1], s3, axis=AX.X),
            rd=[tr], wr=[lr], tag=f"ls{tt}")
        for ch in range(2):
            ca = self.acc[:, tt * c.D + ch * HD:tt * c.D + (ch + 1) * HD]
            b.op("S", lambda nc, s3=ca: nc.scalar.square(self.o32[:, :], s3),
                 rd=[tr], wr=[self.RF(self.o32)], tag=f"lq{tt}")
            b.op("V", lambda nc, cc2=ch: nc.vector.reduce_sum(
                self.lnr[:, 1 + 3 * cc2:2 + 3 * cc2], self.o32[:, :],
                axis=AX.X),
                rd=[self.RF(self.o32)], wr=[lr], tag=f"lqs{tt}")
        b.op("V", lambda nc: nc.vector.tensor_add(
            self.lnr[:, 1:2], self.lnr[:, 1:2], self.lnr[:, 4:5]),
            rd=[lr], wr=[lr], tag=f"lqa{tt}")
        b.op("V", lambda nc: nc.vector.tensor_scalar_mul(
            self.lnr[:, 2:4], self.lnr[:, 0:2], 1.0 / c.D),
            rd=[lr], wr=[lr], tag=f"lm{tt}")
        b.op("V", lambda nc: nc.vector.tensor_mul(
            self.lnr[:, 4:5], self.lnr[:, 2:3], self.lnr[:, 2:3]),
            rd=[lr], wr=[lr], tag=f"lm2{tt}")
        b.op("V", lambda nc: nc.vector.tensor_sub(
            self.lnr[:, 5:6], self.lnr[:, 3:4], self.lnr[:, 4:5]),
            rd=[lr], wr=[lr], tag=f"lv{tt}")
        b.op("V", lambda nc: nc.vector.tensor_scalar_add(
            self.lnr[:, 5:6], self.lnr[:, 5:6], EPS_LN),
            rd=[lr], wr=[lr], tag=f"lve{tt}")
        b.op("S", lambda nc: nc.scalar.activation(
            self.lnr[:, 6:7], self.lnr[:, 5:6], AF.Ln),
            rd=[lr], wr=[lr], tag=f"ll{tt}")
        b.op("V", lambda nc: nc.vector.tensor_scalar_mul(
            self.lnr[:, 6:7], self.lnr[:, 6:7], -0.5),
            rd=[lr], wr=[lr], tag=f"lvh{tt}")
        b.op("S", lambda nc: nc.scalar.activation(
            self.lnr[:, 7:8], self.lnr[:, 6:7], AF.Exp),
            rd=[lr], wr=[lr], tag=f"le{tt}")
        b.op("V", lambda nc: nc.vector.tensor_mul(
            self.lnr[:, 3:4], self.lnr[:, 2:3], self.lnr[:, 7:8]),
            rd=[lr], wr=[lr], tag=f"lmr{tt}")
        b.op("V", lambda nc: nc.vector.tensor_scalar_mul(
            self.lnr[:, 3:4], self.lnr[:, 3:4], -1.0),
            rd=[lr], wr=[lr], tag=f"lng{tt}")
        for ch in range(2):
            ca = self.acc[:, tt * c.D + ch * HD:tt * c.D + (ch + 1) * HD]
            b.op("S", lambda nc, s3=ca: nc.scalar.activation(
                self.o32[:, :], s3, AF.Identity, bias=self.lnr[:, 3:4],
                scale=self.lnr[:, 7:8]),
                rd=[tr, lr], wr=[self.RF(self.o32)], tag=f"la{tt}")
            b.dma("Q", lambda nc, t=tt, cc2=ch: nc.sync.dma_start(
                self.o_out[t * 128:(t + 1) * 128, cc2 * HD:(cc2 + 1) * HD],
                self.o32[:, :]),
                rd=[self.RF(self.o32)], tag=f"so{tt}")


def _k4own(mk, c):
    t = mk.kT.tensor if hasattr(mk.kT, "tensor") else mk.kT
    return bass.AP(tensor=t, offset=0,
                   ap=[[mk.kT.shape[1], 128], [c.MACRO, c.L1KV], [1, 4]])


STAGE_GROUPS = [["l0a", "l0f", "l1a", "moe"]]


def _swz(WT, cw=128):
    WT = np.asarray(WT, dtype=np.float32)
    din, dout = WT.shape
    cw = min(cw, dout)
    nb, nj = din // 128, dout // cw
    a = WT.reshape(nb, 128, nj, cw).transpose(2, 1, 0, 3)
    return np.ascontiguousarray(a.reshape(nj, 128, nb * cw))


def _swz_pair(Wg, Wu):
    din, I = Wg.shape
    nb, nj = din // 128, I // 128
    g = Wg.reshape(nb, 128, nj, 128).transpose(2, 1, 0, 3)
    u = Wu.reshape(nb, 128, nj, 128).transpose(2, 1, 0, 3)
    a = np.concatenate([g[..., None, :], u[..., None, :]], axis=3)
    return np.ascontiguousarray(a.reshape(nj, 128, nb * 256))


def _prep_weights(inp, cfg):
    c = cfg
    w = {}
    t = lambda a: np.ascontiguousarray(np.asarray(a, dtype=np.float32).T)
    w["l0WqT"] = _swz(t(inp["L0_attn_Wq"]))
    w["l0WkT"] = _swz(t(inp["L0_attn_Wk"]))
    w["l0WvT"] = _swz(t(inp["L0_attn_Wv"]))
    w["l0WoT"] = _swz(t(inp["L0_attn_Wo"]))
    wfq = np.zeros((128, c.NPAIR * 128), np.float32)
    wfk = np.zeros((128, c.NPAIR * 128), np.float32)
    for pr in range(c.NPAIR):
        wfq[0:64, pr * 128:pr * 128 + 64] = inp["L0_attn_Wfq"][2 * pr]
        wfq[64:128, pr * 128 + 64:pr * 128 + 128] = \
            inp["L0_attn_Wfq"][2 * pr + 1]
        wfk[0:64, pr * 128:pr * 128 + 64] = inp["L0_attn_Wfk"][2 * pr]
        wfk[64:128, pr * 128 + 64:pr * 128 + 128] = \
            inp["L0_attn_Wfk"][2 * pr + 1]
    w["l0Wfq"] = wfq
    w["l0Wfk"] = wfk
    w["l0fWT"] = _swz(t(inp["L0_ffn_weight"]))
    w["l0fscale"] = np.ascontiguousarray(inp["L0_ffn_scale_weight"],
                                         dtype=np.float32)
    w["l0fbias"] = np.ascontiguousarray(inp["L0_ffn_bias"], dtype=np.float32)
    w["l1WqT"] = _swz(t(inp["L1_attn_Wq"]))
    w["l1WkT"] = _swz(t(inp["L1_attn_Wk"]))
    w["l1WvT"] = _swz(t(inp["L1_attn_Wv"]))
    w["l1WoT"] = _swz(t(inp["L1_attn_Wo"]))
    w["rWT"] = t(inp["L1_ffn_W_router"])
    w["sgWT"] = t(inp["L1_ffn_W_sgate"])
    sgu = np.asarray(inp["L1_ffn_Ws_gate_up"], dtype=np.float32)
    w["WsGU"] = _swz_pair(sgu[:, :c.ISH], sgu[:, c.ISH:])
    nd4 = (c.D + 511) // 512
    sdn = np.asarray(inp["L1_ffn_Ws_down"], dtype=np.float32)
    w["WsDN"] = np.concatenate(
        [_swz(sdn[pe * c.IM:(pe + 1) * c.IM, :], cw=512)
         for pe in range(c.NSH)], axis=0)
    egu = np.asarray(inp["L1_ffn_We_gate_up"], dtype=np.float32)
    w["WeGU"] = np.concatenate(
        [_swz_pair(egu[e][:, :c.IM], egu[e][:, c.IM:]) for e in range(c.E)],
        axis=0)
    edn = np.asarray(inp["L1_ffn_We_down"], dtype=np.float32)
    w["WeDN"] = np.concatenate(
        [_swz(edn[e], cw=512) for e in range(c.E)], axis=0)
    return w


def _masks(cfg, second_half):
    c = cfg
    m = np.zeros((2, 128, 140), np.float32)
    qi = np.arange(128)[:, None]
    ki = np.arange(128)[None, :]
    band = (ki <= qi) & (qi - ki < c.NL)
    m[1, :, 0:128] = np.where(band, 0.0, NEG)
    kp = np.arange(8)[None, :]
    prev_ok = kp > qi
    m[1, :, 128:136] = np.where(prev_ok, 0.0, NEG)
    m[1, :, 136:140] = 0.0
    if second_half:
        m[0] = m[1]
    else:
        ki = np.arange(128)[None, :]
        allowed = (ki <= qi) & ((ki < c.NI) | (qi - ki < c.NL))
        m[0, :, 0:128] = np.where(allowed, 0.0, NEG)
        m[0, :, 128:140] = NEG
    return m


_PROG_CACHE = {}


def _get_prog(key, cfg, stages, first, last):
    if key not in _PROG_CACHE:
        _PROG_CACHE[key] = MK(cfg, stages, first, last).build()
    return _PROG_CACHE[key]


def kernel(**inputs):
    cfg = Cfg(False)
    c = cfg
    x = np.asarray(inputs["L0_attn_hidden_states"], dtype=np.float32)
    B, S, D = x.shape
    w = _prep_weights(inputs, cfg)
    ident = np.eye(128, dtype=np.float32)
    tri = np.triu(np.ones((128, 128), np.float32))
    ones = np.ones((128, 1), np.float32)

    halves = S // c.T
    core_meta = []
    for core in range(N_CORES):
        bb, hh = divmod(core, halves)
        xs = np.ascontiguousarray(x[bb, hh * c.T:(hh + 1) * c.T].T)
        xp = np.ascontiguousarray(x[bb, 0:c.T].T)
        flag = np.full((128, 1), 1.0 if hh > 0 else 0.0, np.float32)
        mask = _masks(cfg, hh > 0)
        core_meta.append(dict(xT=xs, xpreT=xp, c_flag=flag, l1mask=mask))

    consts = dict(c_ident=ident, c_tri=tri, c_ones=ones)
    state = [dict() for _ in range(N_CORES)]
    n_groups = len(STAGE_GROUPS)
    out = None
    for gi, stages in enumerate(STAGE_GROUPS):
        first = gi == 0
        last = gi == n_groups - 1
        mk = _get_prog((tuple(stages), first, last), cfg, stages, first, last)
        in_names = set()
        for alloc in mk.nc.m.functions[0].allocations:
            if getattr(alloc, "kind", None) == "ExternalInput":
                n = alloc.memorylocations[0].name
                if n != "partition_id":
                    in_names.add(n)
        in_maps = []
        for core in range(N_CORES):
            im = {}
            for name in in_names:
                if name in w:
                    im[name] = w[name]
                elif name in consts:
                    im[name] = consts[name]
                elif name in core_meta[core]:
                    im[name] = core_meta[core][name]
                elif name in state[core]:
                    im[name] = state[core][name]
                else:
                    raise KeyError(f"missing input {name}")
            in_maps.append(im)
        res = run_bass_kernel_spmd(mk.nc, in_maps,
                                   core_ids=list(range(N_CORES)))
        if last:
            out = np.empty((B, S, D), np.float32)
            for core in range(N_CORES):
                bb, hh = divmod(core, halves)
                out[bb, hh * c.T:(hh + 1) * c.T] = res.results[core]["out"]
        else:
            for core in range(N_CORES):
                state[core] = {"xstate": res.results[core]["xstate_o"],
                               "halostate": res.results[core]["halostate_o"]}
    return out


# version 18
# speedup vs baseline: 1.1336x; 1.1336x over previous
import numpy as np
import ml_dtypes
from contextlib import ExitStack

import concourse.bass as bass
import concourse.mybir as mybir
from concourse.bass_utils import run_bass_kernel_spmd

F32 = mybir.dt.float32
BF16 = mybir.dt.bfloat16
AF = mybir.ActivationFunctionType
ALU = mybir.AluOpType
AX = mybir.AxisListType

NEG = -1.0e38
EPS_LN = 1.0e-5
EPS_LA = 1.0e-12

N_CORES = 8


class Cfg:
    def __init__(self, small=False):
        if small:
            self.D, self.T, self.MACRO = 256, 256, 128
            self.L0H, self.L1H, self.L1KV = 4, 2, 1
            self.IM, self.ISH, self.E = 256, 512, 8
        else:
            self.D, self.T, self.MACRO = 2048, 2048, 512
            self.L0H, self.L1H, self.L1KV = 32, 16, 8
            self.IM, self.ISH, self.E = 1408, 5632, 8
        self.FD = 64
        self.HD0 = 64
        self.HD1 = 128
        self.NI = 4
        self.NL = 8
        self.PRE = self.T
        self.DB = self.D // 128
        self.NM = self.T // self.MACRO
        self.CPM = self.MACRO // 128
        self.NC = self.T // 128
        self.NPAIR = self.L0H // 2
        self.TT = self.T // 128
        self.NIB = self.IM // 128
        self.NSH = self.ISH // self.IM
        self.NPE = self.NSH + self.E
        self.VW = self.L0H * 65
        self.KV1 = self.L1KV * self.HD1
        self.GQ = self.L1H // self.L1KV
        self.TH = 2 if self.T >= 2048 else 1
        self.HTOK = self.T // self.TH


class Op:
    __slots__ = ("idx", "eng", "fn", "rd", "wr", "deps", "need_inc", "inc_val",
                 "is_dma", "chan", "chan_cnt", "tag")

    def __init__(self, idx, eng, fn, rd, wr, is_dma=False, tag=""):
        self.idx, self.eng, self.fn = idx, eng, fn
        self.rd, self.wr = rd, wr
        self.deps = set()
        self.need_inc = False
        self.inc_val = None
        self.is_dma = is_dma
        self.chan = None
        self.chan_cnt = None
        self.tag = tag


def _overlap(a, b):
    return (a[0] == b[0] and a[1] < b[2] and b[1] < a[2]
            and a[3] < b[4] and b[3] < a[4])


def _covers(a, b):
    return (a[0] == b[0] and a[1] <= b[1] and a[2] >= b[2]
            and a[3] <= b[3] and a[4] >= b[4])


class Bld:
    NCH = 24

    def __init__(self, nc, dbg=False):
        self.nc = nc
        self.dbg = dbg
        self.ops = []
        self.w_hist = {}
        self.r_hist = {}
        self.bars = []
        self.chan_rr = {"Q": 0, "G": 0}
        self.psum_names = set()

    def reg(self, t, p0, p1, f0, f1):
        name = t.name
        if name in self.psum_names:
            return (name, 0, 128, 0, 1 << 30)
        return (name, p0, p1, f0, f1)

    def _resolve(self, op):
        for r in op.rd:
            lst = self.w_hist.get(r[0])
            if lst:
                for (wreg, widx) in reversed(lst):
                    if _overlap(wreg, r):
                        op.deps.add(widx)
                        if _covers(wreg, r):
                            break
        for w in op.wr:
            lst = self.w_hist.get(w[0])
            if lst:
                for (wreg, widx) in reversed(lst):
                    if _overlap(wreg, w):
                        op.deps.add(widx)
                        if _covers(wreg, w):
                            break
            lst = self.r_hist.get(w[0])
            if lst:
                for (rreg, ridx) in lst:
                    if _overlap(rreg, w):
                        op.deps.add(ridx)
        for r in op.rd:
            self.r_hist.setdefault(r[0], []).append((r, op.idx))
        for w in op.wr:
            wl = self.w_hist.setdefault(w[0], [])
            wl[:] = [e for e in wl if not _covers(w, e[0])]
            wl.append((w, op.idx))
            rl = self.r_hist.get(w[0])
            if rl:
                rl[:] = [e for e in rl if not _covers(w, e[0])]

    def op(self, eng, fn, rd=(), wr=(), tag=""):
        o = Op(len(self.ops), eng, fn, list(rd), list(wr), tag=tag)
        self._resolve(o)
        self.ops.append(o)
        return o

    def dma(self, q, fn, rd=(), wr=(), tag=""):
        o = Op(len(self.ops), q, fn, list(rd), list(wr), is_dma=True, tag=tag)
        o.chan = self.chan_rr[q]
        self.chan_rr[q] = (self.chan_rr[q] + 1) % self.NCH
        self._resolve(o)
        self.ops.append(o)
        return o

    def bar(self, tag="bar"):
        o = Op(len(self.ops), "BAR", None, [], [], tag=tag)
        self.ops.append(o)
        self.bars.append(o.idx)
        self.w_hist.clear()
        self.r_hist.clear()
        return o

    def emit(self, ctx):
        nc = self.nc
        engines = {"T": nc.tensor, "V": nc.vector, "S": nc.scalar,
                   "G": nc.gpsimd, "Q": nc.sync}
        ops = self.ops
        for o in ops:
            if o.eng == "BAR":
                continue
            for d in o.deps:
                p = ops[d]
                if p.is_dma:
                    continue
                if p.eng == o.eng and p.eng == "T":
                    continue
                p.need_inc = True
        for bidx in self.bars:
            seen = set()
            i = bidx - 1
            while i >= 0 and len(seen) < 5:
                o = ops[i]
                if o.eng == "BAR":
                    break
                if not o.is_dma and o.eng not in seen:
                    o.need_inc = True
                    seen.add(o.eng)
                i -= 1

        esem = {k: ctx.enter_context(nc.semaphore(f"se_{k}")) for k in "TVSGQ"}
        csem = {(q, i): ctx.enter_context(nc.semaphore(f"sc_{q}{i}"))
                for q in "QG" for i in range(self.NCH)}
        einc = {k: 0 for k in "TVSGQ"}
        ccnt = {(q, i): 0 for q in "QG" for i in range(self.NCH)}
        water = {k: {} for k in "TVSGQ"}
        nwaits = 0
        for o in ops:
            if o.eng == "BAR":
                targets = [(("E", k), esem[k], einc[k])
                           for k in "TVSGQ" if einc[k] > 0]
                targets += [(("C", c), csem[c], ccnt[c])
                            for c in ccnt if ccnt[c] > 0]
                for k in "TVSGQ":
                    eng = engines[k]
                    for key, sem, val in targets:
                        if key == ("E", k):
                            continue
                        if water[k].get(key, 0) >= val:
                            continue
                        eng.wait_ge(sem, val)
                        water[k][key] = val
                        nwaits += 1
                continue
            eng = engines[o.eng]
            for d in sorted(o.deps):
                p = ops[d]
                if p.is_dma:
                    key = ("C", (p.eng, p.chan))
                    assert p.chan_cnt is not None, \
                        f"dma dep not yet emitted op{p.idx}"
                    val = ccnt[(p.eng, p.chan)]
                    sem = csem[(p.eng, p.chan)]
                else:
                    if p.eng == o.eng and p.eng == "T":
                        continue
                    val = p.inc_val
                    if val is None:
                        continue
                    key = ("E", p.eng)
                    sem = esem[p.eng]
                if water[o.eng].get(key, 0) >= val:
                    continue
                eng.wait_ge(sem, val)
                water[o.eng][key] = val
                nwaits += 1
            inst = o.fn(nc)
            if o.is_dma:
                c = (o.eng, o.chan)
                ccnt[c] += 16
                o.chan_cnt = ccnt[c]
                inst.then_inc(csem[c], 16)
            elif o.need_inc:
                einc[o.eng] += 1
                o.inc_val = einc[o.eng]
                inst.then_inc(esem[o.eng], 1)
        if self.dbg:
            print(f"[bld] ops={len(ops)} waits={nwaits} "
                  f"incs={sum(einc.values())}")


class MK:
    def __init__(self, cfg, stages, first, last, dbg=False):
        self.cfg = cfg
        self.stages = stages
        self.first = first
        self.last = last
        self.dbg = dbg
        self.nc = bass.Bass("TRN2", target_bir_lowering=False)
        self.b = Bld(self.nc, dbg=dbg)
        self.ctx = ExitStack()

    def sbuf(self, name, shape, dt, ctx=None):
        return (ctx or self.ctx).enter_context(
            self.nc.sbuf_tensor(name, shape, dt))

    def psumt(self, name, shape, dt):
        t = self.ctx.enter_context(self.nc.psum_tensor(name, shape, dt))
        self.b.psum_names.add(name)
        return t

    def R(self, t, p0, p1, f0, f1):
        return self.b.reg(t, p0, p1, f0, f1)

    def RF(self, t):
        return self.b.reg(t, 0, 128, 0, 1 << 30)

    def mm(self, out_t, out_ap, lhs_t, lhs_ap, rhs_t, rhs_ap, start, stop,
           lr=None, rr=None, tag="mm"):
        lreg = lr or self.RF(lhs_t)
        rreg = rr or self.RF(rhs_t)
        self.b.op("T", lambda nc, o=out_ap, l=lhs_ap, r=rhs_ap, s1=start,
                  s2=stop: nc.tensor.matmul(o, l, r, start=s1, stop=s2),
                  rd=[lreg, rreg], wr=[self.RF(out_t)], tag=tag)

    def trans(self, out_t, out_ap, in_t, in_ap, in_reg, tag="tr"):
        self.b.op("T", lambda nc, o=out_ap, i=in_ap:
                  nc.tensor.transpose(o, i, self.ident[:, :]),
                  rd=[in_reg, self.RF(self.ident)], wr=[self.RF(out_t)],
                  tag=tag)

    def copy(self, eng, dst_t, dst_ap, dst_reg, src_t, src_ap, src_reg,
             tag="cp"):
        if eng == "S":
            self.b.op("S", lambda nc, d=dst_ap, s=src_ap:
                      nc.scalar.copy(d, s), rd=[src_reg], wr=[dst_reg],
                      tag=tag)
        else:
            self.b.op("V", lambda nc, d=dst_ap, s=src_ap:
                      nc.vector.tensor_copy(d, s), rd=[src_reg], wr=[dst_reg],
                      tag=tag)

    def load_w(self, whbm_sw, j, width, dst, tag="w"):
        self.b.dma("G", lambda nc, s=whbm_sw, dd=dst, jj=j, w=width:
                   nc.gpsimd.dma_start(dd[:, 0:w], s[jj, :, :]),
                   wr=[self.R(dst, 0, 128, 0, width)], tag=tag)

    def load_half(self, whbm_sw, h, tag="w"):
        W = self.cfg.DB * 64
        j, hh = divmod(h, 2)
        dst = self.wb[h % 4]
        self.b.dma("G", lambda nc, s=whbm_sw, dd=dst, jj=j, h0=hh * W, w=W:
                   nc.gpsimd.dma_start(dd[:, 0:w], s[jj, :, h0:h0 + w]),
                   wr=[self.R(dst, 0, 128, 0, W)], tag=tag)

    def ln_T(self, buf, stride, c0, W, tag="ln"):
        c = self.cfg
        b = self.b
        psA, psB = self.ps2, self.ps3
        for kb in range(c.DB):
            sqa = self.sq[:, 0:W]
            sqr = self.R(self.sq, 0, 128, 0, W)
            xa = buf[:, kb * stride + c0: kb * stride + c0 + W]
            xr = self.R(buf, 0, 128, kb * stride + c0, kb * stride + c0 + W)
            b.op("S", lambda nc, d=sqa, s=xa: nc.scalar.square(d, s),
                 rd=[xr], wr=[sqr], tag=tag + "sq")
            self.mm(psA, psA[0:1, 0:W], self.onesc, self.onesc[:, :],
                    buf, xa, kb == 0, kb == c.DB - 1, rr=xr, tag=tag + "s")
            self.mm(psB, psB[0:1, 0:W], self.onesc, self.onesc[:, :],
                    self.sq, sqa, kb == 0, kb == c.DB - 1, rr=sqr,
                    tag=tag + "q")
        R_ = self.rows
        inv = 1.0 / c.D
        b.op("V", lambda nc, w=W: nc.vector.tensor_scalar_mul(
            R_[0:1, 0:w], psA[0:1, 0:w], inv),
            rd=[self.RF(psA)], wr=[self.R(R_, 0, 1, 0, 512)], tag=tag + "m")
        b.op("V", lambda nc, w=W: nc.vector.tensor_scalar_mul(
            R_[0:1, 512:512 + w], psB[0:1, 0:w], inv),
            rd=[self.RF(psB)], wr=[self.R(R_, 0, 1, 512, 1024)],
            tag=tag + "m2")
        Rb = self.rowsb
        b.op("V", lambda nc, w=W: nc.vector.tensor_copy(
            Rb[0:1, 0:w], R_[0:1, 0:w]),
            rd=[self.R(R_, 0, 1, 0, 512)], wr=[self.R(Rb, 0, 1, 0, 512)],
            tag=tag + "cb")
        b.op("V", lambda nc, w=W: nc.vector.tensor_mul(
            R_[0:1, 0:w], R_[0:1, 0:w], R_[0:1, 0:w]),
            rd=[self.R(R_, 0, 1, 0, 512)], wr=[self.R(R_, 0, 1, 0, 512)],
            tag=tag + "mm")
        b.op("V", lambda nc, w=W: nc.vector.tensor_sub(
            R_[0:1, 512:512 + w], R_[0:1, 512:512 + w], R_[0:1, 0:w]),
            rd=[self.R(R_, 0, 1, 512, 1024), self.R(R_, 0, 1, 0, 512)],
            wr=[self.R(R_, 0, 1, 512, 1024)], tag=tag + "var")
        b.op("V", lambda nc, w=W: nc.vector.tensor_scalar_add(
            R_[0:1, 512:512 + w], R_[0:1, 512:512 + w], EPS_LN),
            rd=[self.R(R_, 0, 1, 512, 1024)], wr=[self.R(R_, 0, 1, 512, 1024)],
            tag=tag + "ve")
        b.op("S", lambda nc, w=W: nc.scalar.activation(
            R_[0:1, 0:w], R_[0:1, 512:512 + w], AF.Ln),
            rd=[self.R(R_, 0, 1, 512, 1024)], wr=[self.R(R_, 0, 1, 0, 512)],
            tag=tag + "ln")
        b.op("V", lambda nc, w=W: nc.vector.tensor_scalar_mul(
            R_[0:1, 0:w], R_[0:1, 0:w], -0.5),
            rd=[self.R(R_, 0, 1, 0, 512)], wr=[self.R(R_, 0, 1, 0, 512)],
            tag=tag + "vh")
        b.op("S", lambda nc, w=W: nc.scalar.activation(
            R_[0:1, 512:512 + w], R_[0:1, 0:w], AF.Exp),
            rd=[self.R(R_, 0, 1, 0, 512)], wr=[self.R(R_, 0, 1, 512, 1024)],
            tag=tag + "ex")
        b.op("V", lambda nc, w=W: nc.vector.tensor_copy(
            Rb[0:1, 512:512 + w], R_[0:1, 512:512 + w]),
            rd=[self.R(R_, 0, 1, 512, 1024)], wr=[self.R(Rb, 0, 1, 512, 1024)],
            tag=tag + "cb2")
        self.mm(psA, psA[0:128, 0:W], self.ones1, self.ones1[0:1, 0:128],
                Rb, Rb[0:1, 0:W], True, True,
                rr=self.R(Rb, 0, 1, 0, 512), tag=tag + "bc")
        self.mm(psB, psB[0:128, 0:W], self.ones1, self.ones1[0:1, 0:128],
                Rb, Rb[0:1, 512:512 + W], True, True,
                rr=self.R(Rb, 0, 1, 512, 1024), tag=tag + "bc2")
        self.copy("S", self.mb, self.mb[:, 0:W], self.R(self.mb, 0, 128, 0, W),
                  psA, psA[:, 0:W], self.RF(psA), tag=tag + "eb")
        self.copy("S", self.rb, self.rb[:, 0:W], self.R(self.rb, 0, 128, 0, W),
                  psB, psB[:, 0:W], self.RF(psB), tag=tag + "eb2")
        for kb in range(c.DB):
            xa = buf[:, kb * stride + c0: kb * stride + c0 + W]
            xr = self.R(buf, 0, 128, kb * stride + c0, kb * stride + c0 + W)
            b.op("V", lambda nc, x=xa, w=W: nc.vector.tensor_sub(
                x, x, self.mb[:, 0:w]),
                rd=[xr, self.R(self.mb, 0, 128, 0, W)], wr=[xr],
                tag=tag + "ap1")
            b.op("V", lambda nc, x=xa, w=W: nc.vector.tensor_mul(
                x, x, self.rb[:, 0:w]),
                rd=[xr, self.R(self.rb, 0, 128, 0, W)], wr=[xr],
                tag=tag + "ap2")

    def proj_T(self, whbm, wcols, jobs, tag="pj"):
        c = self.cfg
        nj = wcols // 128
        nh = 2 * nj
        hb = c.DB // 2
        for h in range(min(3, nh)):
            self.load_half(whbm, h, tag + "w")
        for jj in range(nj):
            for job in jobs:
                src, sstride, sc0 = job["src"], job["sstride"], job["sc0"]
                ntok = job["ntok"]
                t0 = 0
                while t0 < ntok:
                    nt = min(512, ntok - t0)
                    ps = self.ps0 if (jj + t0 // 512) % 2 == 0 else self.ps1
                    for kb in range(c.DB):
                        wbuf = self.wb[(2 * jj + kb // hb) % 4]
                        kof = kb % hb
                        self.mm(ps, ps[:, 0:nt],
                                wbuf, wbuf[:, kof * 128:(kof + 1) * 128],
                                src,
                                src[:, kb * sstride + sc0 + t0:
                                    kb * sstride + sc0 + t0 + nt],
                                kb == 0, kb == c.DB - 1,
                                lr=self.R(wbuf, 0, 128, kof * 128,
                                          (kof + 1) * 128),
                                rr=self.R(src, 0, 128, kb * sstride + sc0 + t0,
                                          kb * sstride + sc0 + t0 + nt),
                                tag=tag)
                    job["epi"](jj, t0, nt, ps)
                    t0 += nt
            for h in (2 * jj + 3, 2 * jj + 4):
                if h < nh:
                    self.load_half(whbm, h, tag + "w")

    def epi_copy(self, dst, dstride, dc0):
        def epi(j, t0, nt, ps):
            da = dst[:, j * dstride + dc0 + t0: j * dstride + dc0 + t0 + nt]
            dr = self.R(dst, 0, 128, j * dstride + dc0 + t0,
                        j * dstride + dc0 + t0 + nt)
            self.copy("S", dst, da, dr, ps, ps[:, 0:nt], self.RF(ps))
        return epi

    def epi_radd(self, dst, dstride, dc0):
        def epi(j, t0, nt, ps):
            da = dst[:, j * dstride + dc0 + t0: j * dstride + dc0 + t0 + nt]
            dr = self.R(dst, 0, 128, j * dstride + dc0 + t0,
                        j * dstride + dc0 + t0 + nt)
            self.b.op("V", lambda nc, d=da, p=ps, n=nt:
                      nc.vector.tensor_add(d, d, p[:, 0:n]),
                      rd=[dr, self.RF(ps)], wr=[dr], tag="radd")
        return epi

    def proj_N(self, whbm, wcols, src, sstride, sc0, ntok, epi, tag="pn"):
        c = self.cfg
        nj = wcols // 128
        ntt = ntok // 128
        nh = 2 * nj
        hb = c.DB // 2
        for h in range(min(3, nh)):
            self.load_half(whbm, h, tag + "w")
        for jj in range(nj):
            for tt in range(ntt):
                ps = self.ps0 if (jj + tt) % 2 == 0 else self.ps1
                for kb in range(c.DB):
                    wbuf = self.wb[(2 * jj + kb // hb) % 4]
                    kof = kb % hb
                    sa = src[:, kb * sstride + sc0 + tt * 128:
                             kb * sstride + sc0 + tt * 128 + 128]
                    self.mm(ps, ps[:, 0:128], src, sa,
                            wbuf, wbuf[:, kof * 128:(kof + 1) * 128],
                            kb == 0, kb == c.DB - 1,
                            lr=self.R(src, 0, 128, kb * sstride + sc0
                                      + tt * 128,
                                      kb * sstride + sc0 + tt * 128 + 128),
                            rr=self.R(wbuf, 0, 128, kof * 128,
                                      (kof + 1) * 128),
                            tag=tag)
                epi(jj, tt, ps)
            for h in (2 * jj + 3, 2 * jj + 4):
                if h < nh:
                    self.load_half(whbm, h, tag + "w")

    def build(self):
        c = self.cfg
        nc = self.nc
        b = self.b
        st = self.stages

        if self.first:
            self.i_xT = nc.dram_tensor("xT", [c.D, c.T], F32,
                                       kind="ExternalInput").ap()
        else:
            self.i_xst = nc.dram_tensor("xstate", [128, c.DB * c.T], BF16,
                                        kind="ExternalInput").ap()
            self.i_hst = nc.dram_tensor("halostate", [128, c.DB * 256], BF16,
                                        kind="ExternalInput").ap()
        if "l0a" in st:
            self.i_preT = nc.dram_tensor("xpreT", [c.D, c.PRE], F32,
                                         kind="ExternalInput").ap()
            self.i_wq0 = nc.dram_tensor("l0WqT", [c.D // 128, 128, c.DB * 128], F32,
                                        kind="ExternalInput").ap()
            self.i_wk0 = nc.dram_tensor("l0WkT", [c.D // 128, 128, c.DB * 128], F32,
                                        kind="ExternalInput").ap()
            self.i_wv0 = nc.dram_tensor("l0WvT", [c.D // 128, 128, c.DB * 128], F32,
                                        kind="ExternalInput").ap()
            self.i_wo0 = nc.dram_tensor("l0WoT", [c.D // 128, 128, c.DB * 128], F32,
                                        kind="ExternalInput").ap()
            self.i_wfq = nc.dram_tensor("l0Wfq", [128, c.NPAIR * 128], F32,
                                        kind="ExternalInput").ap()
            self.i_wfk = nc.dram_tensor("l0Wfk", [128, c.NPAIR * 128], F32,
                                        kind="ExternalInput").ap()
        if "l0f" in st:
            self.i_fw = nc.dram_tensor("l0fWT", [c.D // 128, 128, c.DB * 128], F32,
                                       kind="ExternalInput").ap()
            self.i_fs = nc.dram_tensor("l0fscale", [c.D], F32,
                                       kind="ExternalInput").ap()
            self.i_fb = nc.dram_tensor("l0fbias", [c.D], F32,
                                       kind="ExternalInput").ap()
        if "l1a" in st:
            self.i_wq1 = nc.dram_tensor("l1WqT", [c.D // 128, 128, c.DB * 128], F32,
                                        kind="ExternalInput").ap()
            self.i_wk1 = nc.dram_tensor("l1WkT", [c.KV1 // 128, 128, c.DB * 128], F32,
                                        kind="ExternalInput").ap()
            self.i_wv1 = nc.dram_tensor("l1WvT", [c.KV1 // 128, 128, c.DB * 128], F32,
                                        kind="ExternalInput").ap()
            self.i_wo1 = nc.dram_tensor("l1WoT", [c.D // 128, 128, c.DB * 128], F32,
                                        kind="ExternalInput").ap()
            self.i_mask = nc.dram_tensor("l1mask", [2, 128, 140], F32,
                                         kind="ExternalInput").ap()
        if "moe" in st:
            self.i_wr = nc.dram_tensor("rWT", [c.D, c.E], F32,
                                       kind="ExternalInput").ap()
            self.i_wsg = nc.dram_tensor("sgWT", [c.D, 1], F32,
                                        kind="ExternalInput").ap()
            self.i_wsgu = nc.dram_tensor("WsGU", [c.NSH * c.NIB, 128, c.DB * 256], F32,
                                         kind="ExternalInput").ap()
            self.i_wsdn = nc.dram_tensor("WsDN", [c.NSH * ((c.D + 511) // 512), 128, c.NIB * min(512, c.D)], F32,
                                         kind="ExternalInput").ap()
            self.i_wegu = nc.dram_tensor("WeGU", [c.E * c.NIB, 128, c.DB * 256], F32,
                                         kind="ExternalInput").ap()
            self.i_wedn = nc.dram_tensor("WeDN", [c.E * ((c.D + 511) // 512), 128, c.NIB * min(512, c.D)], F32,
                                         kind="ExternalInput").ap()
        self.i_ident = nc.dram_tensor("c_ident", [128, 128], F32,
                                      kind="ExternalInput").ap()
        self.i_tri = nc.dram_tensor("c_tri", [128, 128], F32,
                                    kind="ExternalInput").ap()
        self.i_ones = nc.dram_tensor("c_ones", [128, 1], F32,
                                     kind="ExternalInput").ap()
        self.i_flag = nc.dram_tensor("c_flag", [128, 1], F32,
                                     kind="ExternalInput").ap()
        if self.last:
            self.o_out = nc.dram_tensor("out", [c.T, c.D], F32,
                                        kind="ExternalOutput").ap()
        if not self.last:
            self.o_xst = nc.dram_tensor("xstate_o", [128, c.DB * c.T], BF16,
                                        kind="ExternalOutput").ap()
            self.o_hst = nc.dram_tensor("halostate_o", [128, c.DB * 256],
                                        BF16, kind="ExternalOutput").ap()

        self.xT = self.sbuf("xT_sb", [128, c.DB * c.T], BF16)
        self.wb = [self.sbuf(f"wb{i}", [128, c.DB * 64], BF16)
                   for i in range(4)]
        self.ident = self.sbuf("ident", [128, 128], BF16)
        self.ones1 = self.sbuf("ones1", [1, 128], BF16)
        self.onesc = self.sbuf("onesc", [128, 1], BF16)
        self.tri = self.sbuf("tri", [128, 128], BF16)
        self.flag = self.sbuf("flag", [128, 1], F32)

        self.ps0 = self.psumt("ps0", [128, 512], F32)
        self.ps1 = self.psumt("ps1", [128, 512], F32)
        self.ps2 = self.psumt("ps2", [128, 512], F32)
        self.ps3 = self.psumt("ps3", [128, 512], F32)
        self.ps4 = self.psumt("ps4", [128, 512], F32)
        self.ps5 = self.psumt("ps5", [128, 512], F32)
        self.pb0 = self.psumt("pb0", [128, 1024], BF16)
        self.pb1 = self.psumt("pb1", [128, 1024], BF16)

        for (dst, src) in ((self.ident, self.i_ident), (self.tri, self.i_tri),
                           (self.onesc, self.i_ones)):
            b.dma("G", lambda nc, d=dst, s=src: nc.gpsimd.dma_start(
                d[:, :], s[:, :]), wr=[self.RF(dst)], tag="c")
        b.dma("G", lambda nc: nc.gpsimd.dma_start(
            self.ones1[0:1, :], self.i_ones[:, :].rearrange("p c -> c p")),
            wr=[self.RF(self.ones1)], tag="c1")
        b.dma("Q", lambda nc: nc.sync.dma_start(
            self.flag[:, :], self.i_flag[:, :]), wr=[self.RF(self.flag)],
            tag="cf")

        if self.first:
            src = self.i_xT.rearrange("(bb p) t -> p bb t", p=128)
            dst = self.xT[:, :].rearrange("p (bb t) -> p bb t", t=c.T)
            b.dma("G", lambda nc, s=src, d=dst: nc.gpsimd.dma_start(d, s),
                  wr=[self.RF(self.xT)], tag="ldx")
        else:
            b.dma("Q", lambda nc: nc.sync.dma_start(
                self.xT[:, :], self.i_xst[:, :]), wr=[self.RF(self.xT)],
                tag="ldxs")

        if "l0a" in st or "l0f" in st or "l1a" in st:
            g1 = ExitStack()
            self.halo = self.sbuf("haloX", [128, c.DB * 256], BF16, g1)
            if not self.first:
                b.dma("Q", lambda nc: nc.sync.dma_start(
                    self.halo[:, :], self.i_hst[:, :]),
                    wr=[self.RF(self.halo)], tag="ldhs")
            self.sq = self.sbuf("sq", [128, 512], BF16, g1)
            self.rows = self.sbuf("rows", [1, 1024], F32, g1)
            self.rowsb = self.sbuf("rowsb", [1, 1024], BF16, g1)
            self.mb = self.sbuf("mbc", [128, 512], BF16, g1)
            self.rb = self.sbuf("rbc", [128, 512], BF16, g1)
            self.qT = self.sbuf("qT", [128, c.DB * c.MACRO], BF16, g1)
            self.kT = self.sbuf("kT", [128, c.DB * c.MACRO], BF16, g1)
            self.vna = self.sbuf("vna", [128, c.CPM * c.VW], BF16, g1)
            self.yT = self.sbuf("yT", [128, c.DB * c.MACRO], BF16, g1)
            self.haloY = self.sbuf("haloY", [128, c.DB * 256], BF16, g1)
            self.wfq = self.sbuf("wfq", [128, c.NPAIR * 128], BF16, g1)
            self.wfk = self.sbuf("wfk", [128, c.NPAIR * 128], BF16, g1)
            self.fq2 = self.sbuf("fq2", [128, 512], BF16, g1)
            self.fk2 = self.sbuf("fk2", [128, 512], BF16, g1)
            self.fqT = self.sbuf("fqT", [128, 512], BF16, g1)
            self.fkT = self.sbuf("fkT", [128, 512], BF16, g1)
            self.ATm = self.sbuf("ATm", [128, 1024], BF16, g1)
            self.S32 = self.sbuf("S32", [128, c.NPAIR * 65], F32, g1)
            self.Sbf = self.sbuf("Sbf", [128, c.NPAIR * 65], BF16, g1)
            self.ysc = self.sbuf("ysc", [128, 1024], BF16, g1)
            self.dcol = self.sbuf("dcol", [128, 32], F32, g1)
            self.s2 = self.sbuf("s2", [128, 16], F32, g1)
            self.s2r = self.sbuf("s2r", [128, 16], F32, g1)
            self.mask_sb = self.sbuf("mask_sb", [128, 280], F32, g1)
            self.sc = self.sbuf("scsc", [128, 640], F32, g1)
            self.attn = self.sbuf("attnb", [128, 640], BF16, g1)
            self.T1sb = self.sbuf("T1sb", [128, 512], BF16, g1)
            self.T2sb = self.sbuf("T2sb", [8, 1024], BF16, g1)
            self.stK8 = self.sbuf("stK8", [128, c.L1KV * 8], BF16, g1)
            self.stK4 = self.sbuf("stK4", [128, c.L1KV * 4], BF16, g1)
            self.vst8 = self.sbuf("vst8", [8, c.KV1], BF16, g1)
            self.vst4 = self.sbuf("vst4", [4, c.KV1], BF16, g1)
            self.vroll = self.sbuf("vroll", [8, max(1, c.CPM - 1) * c.KV1], BF16, g1)
            self.khal8 = self.sbuf("khal8", [128, c.L1KV * 8], BF16, g1)
            self.khal4 = self.sbuf("khal4", [128, c.L1KV * 4], BF16, g1)
            self.vhal8 = self.sbuf("vhal8", [8, c.KV1], BF16, g1)
            self.vhal4 = self.sbuf("vhal4", [4, c.KV1], BF16, g1)
            self.mxr = self.sbuf("mxr", [128, 4], F32, g1)
            self.smr = self.sbuf("smr", [128, 4], F32, g1)
            self.rcr = self.sbuf("rcr", [128, 4], F32, g1)
            self.fsb = self.sbuf("fsb", [128, c.DB], F32, g1)
            self.fbb = self.sbuf("fbb", [128, c.DB], F32, g1)

            if "l0a" in st:
                self.stage_l0a()
            if "l0f" in st:
                self.stage_l0f()
            if "l1a" in st:
                self.stage_l1a()
                b.bar("l1a")
            if not self.last:
                b.dma("Q", lambda nc: nc.sync.dma_start(
                    self.o_xst[:, :], self.xT[:, :]), rd=[self.RF(self.xT)],
                    tag="stx")
                b.dma("Q", lambda nc: nc.sync.dma_start(
                    self.o_hst[:, :], self.halo[:, :]),
                    rd=[self.RF(self.halo)], tag="sth")
                b.bar("ststate")
            g1.close()

        if "moe" in st:
            g2 = ExitStack()
            self.acc = self.sbuf("acc", [128, c.TT * c.D], BF16, g2)
            self.hT = self.sbuf("hT", [128, c.NIB * c.HTOK], BF16, g2)
            self.wd = [self.sbuf(f"wd{i}", [128, c.NIB * 512], BF16,
                                 g2) for i in range(2)]
            self.o32 = self.sbuf("o32", [128, c.D // 2], F32, g2)
            self.comb = self.sbuf("comb", [128, c.TT * c.E], F32, g2)
            self.gatec = self.sbuf("gatec", [128, c.TT], F32, g2)
            self.prb = self.sbuf("prb", [128, c.E], F32, g2)
            self.m1 = self.sbuf("m1r", [128, 1], F32, g2)
            self.m2 = self.sbuf("m2r", [128, 1], F32, g2)
            self.eqm = self.sbuf("eqm", [128, c.E], F32, g2)
            self.wbm = [self.sbuf(f"wbm{i}", [128, c.DB * 256], BF16,
                                  g2) for i in range(2)]
            self.wrb = self.sbuf("wrb", [128, c.DB * c.E], BF16, g2)
            self.wsgb = self.sbuf("wsgb", [128, c.DB], BF16, g2)
            self.sg_scr = self.sbuf("sg_scr", [128, 1024], BF16, g2)
            self.lnr = self.sbuf("lnr", [128, 8], F32, g2)
            self.stage_moe()
            b.bar("moe")
            g2.close()

        b.bar("end")
        self.b.emit(self.ctx)
        return self


    def l0_wave(self, w, nw, qsrc, qstride, qc0, kc0, vtt, ydst, ystride,
                yc0, with_state, do_y, tag):
        c = self.cfg
        b = self.b
        prs = [4 * w + i for i in range(nw)]
        zkb = lambda i: self.ps4 if i % 2 == 0 else self.ps5
        zqb = lambda i: self.ps2 if i % 2 == 0 else self.ps3
        atb = lambda i: self.ps0 if i % 2 == 0 else self.ps1
        dsl = (w % 2) * 16

        for i, pr in enumerate(prs):
            ps = zkb(i)
            col = (i // 2) * 128
            self.mm(ps, ps[:, col:col + 128], self.kT,
                    self.kT[:, pr * c.MACRO + kc0:pr * c.MACRO + kc0 + 128],
                    self.wfk, self.wfk[:, pr * 128:(pr + 1) * 128],
                    True, True,
                    lr=self.R(self.kT, 0, 128, pr * c.MACRO + kc0,
                              pr * c.MACRO + kc0 + 128),
                    rr=self.R(self.wfk, 0, 128, pr * 128, (pr + 1) * 128),
                    tag=tag + "zk")
        for i, pr in enumerate(prs):
            sl = pr % 4
            ps = zkb(i)
            col = (i // 2) * 128
            da = self.fk2[:, sl * 128:sl * 128 + 128]
            dr = self.R(self.fk2, 0, 128, sl * 128, sl * 128 + 128)
            b.op("S", lambda nc, d=da, p=ps, o=col: nc.scalar.activation(
                d, p[:, o:o + 128], AF.Exp),
                rd=[self.RF(ps)], wr=[dr], tag=tag + "ek")
        for i, pr in enumerate(prs):
            sl = pr % 4
            da = self.fk2[:, sl * 128:sl * 128 + 128]
            dr = self.R(self.fk2, 0, 128, sl * 128, sl * 128 + 128)
            d3 = da.rearrange("p (h f) -> p h f", f=64)
            s2a = self.s2[:, sl * 4:sl * 4 + 2]
            s2reg = self.R(self.s2, 0, 128, sl * 4, sl * 4 + 2)
            s2ra = self.s2r[:, sl * 4:sl * 4 + 2]
            s2rreg = self.R(self.s2r, 0, 128, sl * 4, sl * 4 + 2)
            b.op("V", lambda nc, s=d3, o=s2a: nc.vector.reduce_sum(
                o, s, axis=AX.X), rd=[dr], wr=[s2reg], tag=tag + "ks")
            b.op("V", lambda nc, ii=s2a, o=s2ra: nc.vector.reciprocal(o, ii),
                 rd=[s2reg], wr=[s2rreg], tag=tag + "kr")
        for i, pr in enumerate(prs):
            sl = pr % 4
            dr = self.R(self.fk2, 0, 128, sl * 128, sl * 128 + 128)
            s2rreg = self.R(self.s2r, 0, 128, sl * 4, sl * 4 + 2)
            for h2 in range(2):
                b.op("V", lambda nc, s0=sl * 128 + h2 * 64,
                     sc2=self.s2r[:, (pr % 4) * 4 + h2:(pr % 4) * 4 + h2 + 1]:
                     nc.vector.tensor_scalar_mul(
                         self.fk2[:, s0:s0 + 64], self.fk2[:, s0:s0 + 64],
                         sc2),
                     rd=[dr, s2rreg], wr=[dr], tag=tag + "kn")
        if do_y:
            for i, pr in enumerate(prs):
                ps = zqb(i)
                col = (i // 2) * 128
                self.mm(ps, ps[:, col:col + 128], qsrc,
                        qsrc[:, pr * qstride + qc0:pr * qstride + qc0 + 128],
                        self.wfq, self.wfq[:, pr * 128:(pr + 1) * 128],
                        True, True,
                        lr=self.R(qsrc, 0, 128, pr * qstride + qc0,
                                  pr * qstride + qc0 + 128),
                        rr=self.R(self.wfq, 0, 128, pr * 128,
                                  (pr + 1) * 128), tag=tag + "zq")
            for i, pr in enumerate(prs):
                sl = pr % 4
                ps = zqb(i)
                col = (i // 2) * 128
                da = self.fq2[:, sl * 128:sl * 128 + 128]
                dr = self.R(self.fq2, 0, 128, sl * 128, sl * 128 + 128)
                b.op("S", lambda nc, d=da, p=ps, o=col: nc.scalar.activation(
                    d, p[:, o:o + 128], AF.Exp),
                    rd=[self.RF(ps)], wr=[dr], tag=tag + "eq")
            for i, pr in enumerate(prs):
                sl = pr % 4
                self.trans(self.pb0, self.pb0[:, i * 128:i * 128 + 128],
                           self.fq2, self.fq2[:, sl * 128:sl * 128 + 128],
                           self.R(self.fq2, 0, 128, sl * 128, sl * 128 + 128),
                           tag=tag + "tq")
                self.trans(self.pb1, self.pb1[:, i * 128:i * 128 + 128],
                           self.fk2, self.fk2[:, sl * 128:sl * 128 + 128],
                           self.R(self.fk2, 0, 128, sl * 128, sl * 128 + 128),
                           tag=tag + "tk")
            for i, pr in enumerate(prs):
                sl = pr % 4
                b.op("S", lambda nc, s0=sl * 128, o=i * 128:
                     nc.scalar.copy(self.fqT[:, s0:s0 + 128],
                                    self.pb0[:, o:o + 128]),
                     rd=[self.RF(self.pb0)],
                     wr=[self.R(self.fqT, 0, 128, sl * 128, sl * 128 + 128)],
                     tag=tag + "evq")
                b.op("S", lambda nc, s0=sl * 128, o=i * 128:
                     nc.scalar.copy(self.fkT[:, s0:s0 + 128],
                                    self.pb1[:, o:o + 128]),
                     rd=[self.RF(self.pb1)],
                     wr=[self.R(self.fkT, 0, 128, sl * 128, sl * 128 + 128)],
                     tag=tag + "evk")
            for i, pr in enumerate(prs):
                sl = pr % 4
                for h2 in range(2):
                    p0 = h2 * 64
                    ps = atb(i + h2)
                    col = (i // 2) * 256 + h2 * 128
                    self.mm(ps, ps[:, col:col + 128], self.fkT,
                            self.fkT[p0:p0 + 64, sl * 128:sl * 128 + 128],
                            self.fqT,
                            self.fqT[p0:p0 + 64, sl * 128:sl * 128 + 128],
                            True, True,
                            lr=self.R(self.fkT, p0, p0 + 64, sl * 128,
                                      sl * 128 + 128),
                            rr=self.R(self.fqT, p0, p0 + 64, sl * 128,
                                      sl * 128 + 128), tag=tag + "at")
            for i, pr in enumerate(prs):
                for h2 in range(2):
                    ps = atb(i + h2)
                    col = (i // 2) * 256 + h2 * 128
                    ac = (i * 2 + h2) * 128
                    atr = self.R(self.ATm, 0, 128, ac, ac + 128)
                    b.op("V", lambda nc, p=ps, o=col, a=ac:
                         nc.vector.tensor_mul(
                             self.ATm[:, a:a + 128], p[:, o:o + 128],
                             self.tri[:, :]),
                         rd=[self.RF(ps), self.RF(self.tri)], wr=[atr],
                         tag=tag + "msk")
            for i, pr in enumerate(prs):
                sl = pr % 4
                for h2 in range(2):
                    h = 2 * pr + h2
                    p0 = h2 * 64
                    ps = zqb(i)
                    col = (i // 2) * 256 + h2 * 128
                    ac = (i * 2 + h2) * 128
                    va = self.vna[:, vtt * c.VW + h * 65:
                                  vtt * c.VW + h * 65 + 65]
                    vr = self.R(self.vna, 0, 128, vtt * c.VW + h * 65,
                                vtt * c.VW + h * 65 + 65)
                    self.mm(ps, ps[:, col:col + 65], self.ATm,
                            self.ATm[:, ac:ac + 128], self.vna, va,
                            True, not with_state,
                            lr=self.R(self.ATm, 0, 128, ac, ac + 128),
                            rr=vr, tag=tag + "y1")
                    if with_state:
                        sa = self.Sbf[p0:p0 + 64, pr * 65:pr * 65 + 65]
                        sr = self.R(self.Sbf, p0, p0 + 64, pr * 65,
                                    pr * 65 + 65)
                        self.mm(ps, ps[:, col:col + 65], self.fqT,
                                self.fqT[p0:p0 + 64, sl * 128:sl * 128 + 128],
                                self.Sbf, sa,
                                False, True,
                                lr=self.R(self.fqT, p0, p0 + 64, sl * 128,
                                          sl * 128 + 128),
                                rr=sr, tag=tag + "y2")
            for bi, ps in ((0, self.ps2), (1, self.ps3)):
                idxs = [i for i in range(nw) if i % 2 == bi]
                if not idxs:
                    continue
                nt = 2 * len(idxs)
                base = (idxs[0] // 2) * 256 + 64
                pt = ps.tensor if hasattr(ps, "tensor") else ps
                dsrc = bass.AP(tensor=pt, offset=base,
                               ap=[[512, 128], [128, nt], [1, 1]])
                dca = self.dcol[:, dsl + bi * 8:dsl + bi * 8 + nt]
                dcr = self.R(self.dcol, 0, 128, dsl + bi * 8,
                             dsl + bi * 8 + nt)
                b.op("V", lambda nc, s=dsrc, d=dca: nc.vector.tensor_scalar_add(
                    d, s, EPS_LA), rd=[self.RF(ps)], wr=[dcr],
                    tag=tag + "dn")
                dca2 = self.dcol[:, dsl + bi * 8 + 4:dsl + bi * 8 + 4 + nt]
                dcr2 = self.R(self.dcol, 0, 128, dsl + bi * 8 + 4,
                              dsl + bi * 8 + 4 + nt)
                b.op("V", lambda nc, ii=dca, o=dca2: nc.vector.reciprocal(
                    o, ii), rd=[dcr], wr=[dcr2], tag=tag + "rc")
            ybase = (w % 2) * 512
            for i, pr in enumerate(prs):
                for h2 in range(2):
                    ps = zqb(i)
                    col = (i // 2) * 256 + h2 * 128
                    bi = i % 2
                    ri = dsl + bi * 8 + 4 + (i // 2) * 2 + h2
                    yo = ybase + (i * 2 + h2) * 64
                    b.op("V", lambda nc, p=ps, o=col, r=ri, d=yo:
                         nc.vector.tensor_scalar_mul(
                             self.ysc[:, d:d + 64], p[:, o:o + 64],
                             self.dcol[:, r:r + 1]),
                         rd=[self.RF(ps),
                             self.R(self.dcol, 0, 128, ri, ri + 1)],
                         wr=[self.R(self.ysc, 0, 128, yo, yo + 64)],
                         tag=tag + "ys")
            for i, pr in enumerate(prs):
                for h2 in range(2):
                    yo = ybase + (i * 2 + h2) * 64
                    pb = self.pb0 if h2 == 0 else self.pb1
                    self.trans(pb, pb[0:64, 512 + i * 128:512 + i * 128 + 128],
                               self.ysc, self.ysc[:, yo:yo + 64],
                               self.R(self.ysc, 0, 128, yo, yo + 64),
                               tag=tag + "ty")
            for h2 in range(2):
                pb = self.pb0 if h2 == 0 else self.pb1
                ssrc = pb[0:64, 512:512 + nw * 128].rearrange(
                    "p (j r) -> p j r", r=128)
                ddst = ydst[h2 * 64:h2 * 64 + 64, :].rearrange(
                    "p (j r) -> p j r", r=ystride)[
                    :, prs[0]:prs[0] + nw, yc0:yc0 + 128]
                b.op("S", lambda nc, s=ssrc, d=ddst: nc.scalar.copy(d, s),
                     rd=[self.RF(pb)],
                     wr=[self.R(ydst, h2 * 64, h2 * 64 + 64,
                                prs[0] * ystride + yc0,
                                (prs[-1] + 1) * ystride)],
                     tag=tag + "ye")
        for i, pr in enumerate(prs):
            sl = pr % 4
            ps = zkb(i)
            col = 256 + (i // 2) * 128
            for h2 in range(2):
                h = 2 * pr + h2
                fa = self.fk2[:, sl * 128 + h2 * 64:sl * 128 + h2 * 64 + 64]
                fr = self.R(self.fk2, 0, 128, sl * 128 + h2 * 64,
                            sl * 128 + h2 * 64 + 64)
                va = self.vna[:, vtt * c.VW + h * 65:vtt * c.VW + h * 65 + 65]
                vr = self.R(self.vna, 0, 128, vtt * c.VW + h * 65,
                            vtt * c.VW + h * 65 + 65)
                self.mm(ps, ps[h2 * 64:h2 * 64 + 64, col:col + 65],
                        self.fk2, fa, self.vna, va, True, True,
                        lr=fr, rr=vr, tag=tag + "ds")
        for bi, ps in ((0, self.ps4), (1, self.ps5)):
            idxs = [i for i in range(nw) if i % 2 == bi]
            if not idxs:
                continue
            pt = ps.tensor if hasattr(ps, "tensor") else ps
            ssrc = bass.AP(tensor=pt, offset=256 + (idxs[0] // 2) * 128,
                           ap=[[512, 128], [128, len(idxs)], [1, 65]])
            st0 = self.S32.tensor if hasattr(self.S32, "tensor") else self.S32
            sdst = bass.AP(tensor=st0, offset=(prs[0] + bi) * 65,
                           ap=[[self.S32.shape[1], 128], [130, len(idxs)],
                               [1, 65]])
            sreg = self.R(self.S32, 0, 128, prs[0] * 65,
                          (prs[-1] + 1) * 65)
            b.op("V", lambda nc, s=ssrc, d=sdst: nc.vector.tensor_tensor(
                d, d, s, op=ALU.add), rd=[sreg, self.RF(ps)], wr=[sreg],
                tag=tag + "sa")
        ca = self.Sbf[:, prs[0] * 65:(prs[-1] + 1) * 65]
        cr = self.R(self.Sbf, 0, 128, prs[0] * 65, (prs[-1] + 1) * 65)
        b.op("S", lambda nc, d=ca: nc.scalar.copy(
            d, self.S32[:, prs[0] * 65:(prs[-1] + 1) * 65]),
            rd=[self.R(self.S32, 0, 128, prs[0] * 65, (prs[-1] + 1) * 65)],
            wr=[cr], tag=tag + "sc")

    def l0_featmap(self, srcT, c0, pr, sl, is_q, tag):
        c = self.cfg
        b = self.b
        wf = self.wfq if is_q else self.wfk
        dst = self.fq2 if is_q else self.fk2
        ps = self.ps4 if pr % 2 == 0 else self.ps5
        self.mm(ps, ps[:, 0:128], srcT,
                srcT[:, pr * c.MACRO + c0:pr * c.MACRO + c0 + 128],
                wf, wf[:, pr * 128:(pr + 1) * 128], True, True,
                lr=self.R(srcT, 0, 128, 0, 1 << 30),
                rr=self.R(wf, 0, 128, pr * 128, (pr + 1) * 128), tag=tag + "z")
        da = dst[:, sl * 128:sl * 128 + 128]
        dr = self.R(dst, 0, 128, sl * 128, sl * 128 + 128)
        b.op("S", lambda nc, d=da, p=ps: nc.scalar.activation(d, p[:, 0:128],
                                                              AF.Exp),
             rd=[self.RF(ps)], wr=[dr], tag=tag + "e")
        if not is_q:
            d3 = da.rearrange("p (h f) -> p h f", f=64)
            s2a = self.s2[:, sl * 4:sl * 4 + 2]
            s2reg = self.R(self.s2, 0, 128, sl * 4, sl * 4 + 2)
            s2ra = self.s2r[:, sl * 4:sl * 4 + 2]
            s2rreg = self.R(self.s2r, 0, 128, sl * 4, sl * 4 + 2)
            b.op("V", lambda nc, s=d3, o=s2a: nc.vector.reduce_sum(
                o, s, axis=AX.X),
                rd=[dr], wr=[s2reg], tag=tag + "s")
            b.op("V", lambda nc, i=s2a, o=s2ra: nc.vector.reciprocal(o, i),
                rd=[s2reg], wr=[s2rreg], tag=tag + "r")
            for h2 in range(2):
                b.op("V", lambda nc, d=dst, s0=sl * 128 + h2 * 64,
                     sc2=self.s2r[:, sl * 4 + h2:sl * 4 + h2 + 1]:
                     nc.vector.tensor_scalar_mul(
                         d[:, s0:s0 + 64], d[:, s0:s0 + 64], sc2),
                     rd=[dr, s2rreg], wr=[dr], tag=tag + "n")

    def l0_attn_chunk(self, glob_zero, vtt, ytag, ydst, yc0, with_state,
                      pr, sl):
        c = self.cfg
        b = self.b
        self.trans(self.pb0, self.pb0[:, 0:128], self.fq2,
                   self.fq2[:, sl * 128:sl * 128 + 128],
                   self.R(self.fq2, 0, 128, sl * 128, sl * 128 + 128),
                   tag=ytag + "tq")
        self.copy("S", self.fqT, self.fqT[:, sl * 128:sl * 128 + 128],
                  self.R(self.fqT, 0, 128, sl * 128, sl * 128 + 128),
                  self.pb0, self.pb0[:, 0:128], self.RF(self.pb0),
                  tag=ytag + "eq")
        self.trans(self.pb1, self.pb1[:, 0:128], self.fk2,
                   self.fk2[:, sl * 128:sl * 128 + 128],
                   self.R(self.fk2, 0, 128, sl * 128, sl * 128 + 128),
                   tag=ytag + "tk")
        self.copy("S", self.fkT, self.fkT[:, sl * 128:sl * 128 + 128],
                  self.R(self.fkT, 0, 128, sl * 128, sl * 128 + 128),
                  self.pb1, self.pb1[:, 0:128], self.RF(self.pb1),
                  tag=ytag + "ek")
        atp = self.ps0 if pr % 2 == 0 else self.ps1
        yp = self.ps2 if pr % 2 == 0 else self.ps3
        bcp = self.ps5 if pr % 2 == 0 else self.ps4
        dn0 = sl * 256
        for h2 in range(2):
            h = 2 * pr + h2
            p0 = h2 * 64
            self.mm(atp, atp[0:128, h2 * 128:h2 * 128 + 128],
                    self.fkT, self.fkT[p0:p0 + 64, sl * 128:sl * 128 + 128],
                    self.fqT, self.fqT[p0:p0 + 64, sl * 128:sl * 128 + 128],
                    True, True,
                    lr=self.R(self.fkT, p0, p0 + 64, sl * 128, sl * 128 + 128),
                    rr=self.R(self.fqT, p0, p0 + 64, sl * 128, sl * 128 + 128),
                    tag=ytag + "at")
            atm = self.ATm[:, sl * 128:sl * 128 + 128]
            atr = self.R(self.ATm, 0, 128, sl * 128, sl * 128 + 128)
            b.op("V", lambda nc, d=atm, p=atp, o=h2 * 128:
                 nc.vector.tensor_mul(
                     d, p[0:128, o:o + 128], self.tri[:, :]),
                 rd=[self.RF(atp), self.RF(self.tri)], wr=[atr],
                 tag=ytag + "msk")
            va = self.vna[:, vtt * c.VW + h * 65: vtt * c.VW + h * 65 + 65]
            vr = self.R(self.vna, 0, 128, vtt * c.VW + h * 65,
                        vtt * c.VW + h * 65 + 65)
            self.mm(yp, yp[0:65, h2 * 128:h2 * 128 + 128], self.vna, va,
                    self.ATm, atm, True, not with_state, lr=vr, rr=atr,
                    tag=ytag + "y1")
            if with_state:
                sa = self.Sbf[p0:p0 + 64, pr * 65:pr * 65 + 65]
                sr = self.R(self.Sbf, p0, p0 + 64, pr * 65, pr * 65 + 65)
                self.mm(yp, yp[0:65, h2 * 128:h2 * 128 + 128], self.Sbf, sa,
                        self.fqT,
                        self.fqT[p0:p0 + 64, sl * 128:sl * 128 + 128],
                        False, True, lr=sr,
                        rr=self.R(self.fqT, p0, p0 + 64, sl * 128,
                                  sl * 128 + 128),
                        tag=ytag + "y2")
            b.op("V", lambda nc, p=yp, o=h2 * 128, d0=dn0:
                 nc.vector.tensor_scalar_add(
                     self.rows[0:1, d0:d0 + 128], p[64:65, o:o + 128],
                     EPS_LA),
                 rd=[self.RF(yp)], wr=[self.R(self.rows, 0, 1, dn0,
                                              dn0 + 128)],
                 tag=ytag + "dn")
            b.op("V", lambda nc, d0=dn0: nc.vector.reciprocal(
                self.rows[0:1, d0 + 128:d0 + 256], self.rows[0:1, d0:d0 + 128]),
                rd=[self.R(self.rows, 0, 1, dn0, dn0 + 128)],
                wr=[self.R(self.rows, 0, 1, dn0 + 128, dn0 + 256)],
                tag=ytag + "rc")
            b.op("V", lambda nc, d0=dn0, s0=sl * 128: nc.vector.tensor_copy(
                self.dnR[0:1, s0:s0 + 128],
                self.rows[0:1, d0 + 128:d0 + 256]),
                rd=[self.R(self.rows, 0, 1, dn0 + 128, dn0 + 256)],
                wr=[self.R(self.dnR, 0, 1, sl * 128, sl * 128 + 128)],
                tag=ytag + "rcc")
            self.mm(bcp, bcp[0:64, h2 * 128:h2 * 128 + 128], self.ones1,
                    self.ones1[0:1, 0:64], self.dnR,
                    self.dnR[0:1, sl * 128:sl * 128 + 128],
                    True, True,
                    rr=self.R(self.dnR, 0, 1, sl * 128, sl * 128 + 128),
                    tag=ytag + "bc")
            self.copy("S", self.rb64, self.rb64[0:64, sl * 128:sl * 128 + 128],
                      self.R(self.rb64, 0, 64, sl * 128, sl * 128 + 128),
                      bcp, bcp[0:64, h2 * 128:h2 * 128 + 128], self.RF(bcp),
                      tag=ytag + "eb")
            ya = ydst[p0:p0 + 64, pr * (256 if ydst is self.haloY else c.MACRO)
                      + yc0: pr * (256 if ydst is self.haloY else c.MACRO)
                      + yc0 + 128]
            yr = self.R(ydst, p0, p0 + 64,
                        pr * (256 if ydst is self.haloY else c.MACRO) + yc0,
                        pr * (256 if ydst is self.haloY else c.MACRO)
                        + yc0 + 128)
            b.op("V", lambda nc, y=ya, p=yp, o=h2 * 128, s0=sl * 128:
                 nc.vector.tensor_mul(
                     y, p[0:64, o:o + 128],
                     self.rb64[0:64, s0:s0 + 128]),
                 rd=[self.RF(yp), self.R(self.rb64, 0, 64, sl * 128,
                                         sl * 128 + 128)],
                 wr=[yr], tag=ytag + "yo")

    def l0_state_update(self, vtt, pr, sl, do_cast, tag):
        c = self.cfg
        b = self.b
        dsp = self.ps3 if pr % 2 == 0 else self.ps2
        o65 = 65 if pr % 2 == 0 else 0
        for h2 in range(2):
            h = 2 * pr + h2
            fa = self.fk2[:, sl * 128 + h2 * 64: sl * 128 + h2 * 64 + 64]
            fr = self.R(self.fk2, 0, 128, sl * 128 + h2 * 64,
                        sl * 128 + h2 * 64 + 64)
            va = self.vna[:, vtt * c.VW + h * 65: vtt * c.VW + h * 65 + 65]
            vr = self.R(self.vna, 0, 128, vtt * c.VW + h * 65,
                        vtt * c.VW + h * 65 + 65)
            self.mm(dsp, dsp[h2 * 64:h2 * 64 + 64, 256 + o65:256 + o65 + 65],
                    self.fk2, fa, self.vna, va, True, True, lr=fr, rr=vr,
                    tag=tag + "ds")
        sa = self.S32[:, pr * 65:pr * 65 + 65]
        sr = self.R(self.S32, 0, 128, pr * 65, pr * 65 + 65)
        b.op("V", lambda nc, s=sa, p=dsp, o=256 + o65: nc.vector.tensor_add(
            s, s, p[0:128, o:o + 65]),
            rd=[sr, self.RF(dsp)], wr=[sr], tag=tag + "sa")
        if do_cast:
            ba = self.Sbf[:, pr * 65:pr * 65 + 65]
            br = self.R(self.Sbf, 0, 128, pr * 65, pr * 65 + 65)
            self.copy("S", self.Sbf, ba, br, self.S32, sa, sr, tag=tag + "sc")

    def l0_load_macro_kv(self, srcT, m, wsrcq=None, tag="pkv"):
        c = self.cfg
        b = self.b
        self.proj_T(self.i_wk0, c.D,
                    [dict(src=srcT, sstride=c.MACRO, sc0=0, ntok=c.MACRO,
                          epi=self.epi_copy(self.kT, c.MACRO, 0))],
                    tag=tag + "k")

        def vepi(j, tt, ps):
            da = bass.AP(tensor=self.vna.tensor if hasattr(self.vna, "tensor")
                         else self.vna,
                         offset=tt * c.VW + (2 * j) * 65,
                         ap=[[self.vna.shape[1], 128], [65, 2], [1, 64]])
            pa = ps[:, 0:128].rearrange("p (g f) -> p g f", f=64)
            self.copy("S", self.vna, da,
                      self.R(self.vna, 0, 128, tt * c.VW + 2 * j * 65,
                             tt * c.VW + 2 * j * 65 + 130),
                      ps, pa, self.RF(ps), tag=tag + "ve")
        self.proj_N(self.i_wv0, c.D, srcT, c.MACRO, 0, c.MACRO, vepi,
                    tag=tag + "v")
        ones_ap = bass.AP(tensor=self.vna.tensor if hasattr(self.vna, "tensor")
                          else self.vna, offset=64,
                          ap=[[self.vna.shape[1], 128],
                              [c.VW, c.CPM], [65, c.L0H]])
        b.op("V", lambda nc, a=ones_ap: nc.vector.memset(a, 1.0),
             rd=[], wr=[self.RF(self.vna)], tag=tag + "o")

    def stage_l0a(self):
        c = self.cfg
        b = self.b
        for (dst, src) in ((self.wfq, self.i_wfq), (self.wfk, self.i_wfk)):
            b.dma("G", lambda nc, ss=src, dd=dst: nc.gpsimd.dma_start(
                dd[:, :], ss[:, :]), wr=[self.RF(dst)], tag="wf")
        b.op("V", lambda nc: nc.vector.memset(self.S32[:, :], 0.0),
             rd=[], wr=[self.RF(self.S32)], tag="s0")
        b.op("V", lambda nc: nc.vector.memset(self.Sbf[:, :], 0.0),
             rd=[], wr=[self.RF(self.Sbf)], tag="s0b")

        for pm in range(c.NM):
            src = self.i_preT[:, pm * c.MACRO:(pm + 1) * c.MACRO].rearrange(
                "(bb p) t -> p bb t", p=128)
            dst = self.qT[:, :].rearrange("p (bb t) -> p bb t", t=c.MACRO)
            b.dma("G", lambda nc, s=src, d=dst: nc.gpsimd.dma_start(d, s),
                  wr=[self.RF(self.qT)], tag="ldpre")
            self.l0_load_macro_kv(self.qT, pm, tag=f"p{pm}")
            halo_c = {(0, 0): 0, (c.NM - 1, c.CPM - 1): 1}
            nwv = (c.NPAIR + 3) // 4
            for cc in range(c.CPM):
                glob = pm * c.CPM + cc
                hc = halo_c.get((pm, cc))
                if hc is not None:
                    self.proj_T(self.i_wq0, c.D,
                                [dict(src=self.qT, sstride=c.MACRO,
                                      sc0=cc * 128, ntok=128,
                                      epi=self.epi_copy(self.yT, c.MACRO,
                                                        0))],
                                tag=f"hq{hc}")
                    for kb in range(c.DB):
                        sa = self.qT[:, kb * c.MACRO + cc * 128:
                                     kb * c.MACRO + cc * 128 + 128]
                        da = self.halo[:, kb * 256 + hc * 128:
                                       kb * 256 + hc * 128 + 128]
                        self.copy("V", self.halo, da,
                                  self.R(self.halo, 0, 128,
                                         kb * 256 + hc * 128,
                                         kb * 256 + hc * 128 + 128),
                                  self.qT, sa,
                                  self.R(self.qT, 0, 128,
                                         kb * c.MACRO + cc * 128,
                                         kb * c.MACRO + cc * 128 + 128),
                                  tag="hx")
                for w2 in range(nwv):
                    nw = min(4, c.NPAIR - 4 * w2)
                    if hc is not None:
                        self.l0_wave(w2, nw, self.yT, c.MACRO, 0, cc * 128,
                                     cc, self.haloY, 256, hc * 128,
                                     with_state=(glob != 0), do_y=True,
                                     tag=f"hw{glob}_{w2}")
                    else:
                        self.l0_wave(w2, nw, None, 0, 0, cc * 128, cc,
                                     None, 0, 0, with_state=False,
                                     do_y=False, tag=f"pw{glob}_{w2}")
        b.op("V", lambda nc: nc.vector.tensor_scalar_mul(
            self.S32[:, :], self.S32[:, :], self.flag[:, 0:1]),
            rd=[self.RF(self.S32), self.RF(self.flag)],
            wr=[self.RF(self.S32)], tag="fsc")
        for pr in range(c.NPAIR):
            sa = self.S32[:, pr * 65:pr * 65 + 65]
            sr = self.R(self.S32, 0, 128, pr * 65, pr * 65 + 65)
            ba = self.Sbf[:, pr * 65:pr * 65 + 65]
            br = self.R(self.Sbf, 0, 128, pr * 65, pr * 65 + 65)
            self.copy("S", self.Sbf, ba, br, self.S32, sa, sr, tag="fcc")

        for m in range(c.NM):
            self.proj_T(self.i_wq0, c.D,
                        [dict(src=self.xT, sstride=c.T, sc0=m * c.MACRO,
                              ntok=c.MACRO,
                              epi=self.epi_copy(self.qT, c.MACRO, 0))],
                        tag=f"oq{m}")
            self.proj_T(self.i_wk0, c.D,
                        [dict(src=self.xT, sstride=c.T, sc0=m * c.MACRO,
                              ntok=c.MACRO,
                              epi=self.epi_copy(self.kT, c.MACRO, 0))],
                        tag=f"ok{m}")

            def vepi(j, tt, ps):
                da = bass.AP(tensor=self.vna.tensor
                             if hasattr(self.vna, "tensor") else self.vna,
                             offset=tt * c.VW + (2 * j) * 65,
                             ap=[[self.vna.shape[1], 128], [65, 2], [1, 64]])
                pa = ps[:, 0:128].rearrange("p (g f) -> p g f", f=64)
                self.copy("S", self.vna, da,
                          self.R(self.vna, 0, 128, tt * c.VW + 2 * j * 65,
                                 tt * c.VW + 2 * j * 65 + 130),
                          ps, pa, self.RF(ps), tag=f"ov{m}")
            self.proj_N(self.i_wv0, c.D, self.xT, c.T, m * c.MACRO, c.MACRO,
                        vepi, tag=f"ovv{m}")
            ones_ap = bass.AP(tensor=self.vna.tensor
                              if hasattr(self.vna, "tensor") else self.vna,
                              offset=64,
                              ap=[[self.vna.shape[1], 128],
                                  [c.VW, c.CPM], [65, c.L0H]])
            b.op("V", lambda nc, a=ones_ap: nc.vector.memset(a, 1.0),
                 rd=[], wr=[self.RF(self.vna)], tag=f"oo{m}")
            nwv = (c.NPAIR + 3) // 4
            for cc in range(c.CPM):
                glob = m * c.CPM + cc
                for w2 in range(nwv):
                    nw = min(4, c.NPAIR - 4 * w2)
                    self.l0_wave(w2, nw, self.qT, c.MACRO, cc * 128,
                                 cc * 128, cc, self.yT, c.MACRO, cc * 128,
                                 with_state=True, do_y=True,
                                 tag=f"ow{glob}_{w2}")
            jobs = [dict(src=self.yT, sstride=c.MACRO, sc0=0, ntok=c.MACRO,
                         epi=self.epi_radd(self.xT, c.T, m * c.MACRO))]
            if m == 0:
                jobs.append(dict(src=self.haloY, sstride=256, sc0=0, ntok=256,
                                 epi=self.epi_radd(self.halo, 256, 0)))
            self.proj_T(self.i_wo0, c.D, jobs, tag=f"wo{m}")
            for w0 in range(0, c.MACRO, 512):
                self.ln_T(self.xT, c.T, m * c.MACRO + w0,
                          min(512, c.MACRO - w0), tag=f"l0n{m}_{w0}")
            if m == 0:
                self.ln_T(self.halo, 256, 0, 256, tag="l0nh")

    def stage_l0f(self):
        c = self.cfg
        b = self.b
        for (dst, src) in ((self.fsb, self.i_fs), (self.fbb, self.i_fb)):
            s = src.rearrange("(bb p) -> p bb", p=128)
            b.dma("Q", lambda nc, ss=s, dd=dst: nc.sync.dma_start(
                dd[:, :], ss, allow_slow_non_contiguous=True),
                wr=[self.RF(dst)], tag="fs")

        def mk_epi(dstbuf, dstride):
            def epi(j, t0, nt, ps):
                da = dstbuf[:, j * dstride + t0: j * dstride + t0 + nt]
                dr = self.R(dstbuf, 0, 128, j * dstride + t0,
                            j * dstride + t0 + nt)
                b.op("S", lambda nc, d=da, p=ps, n=nt, jj=j:
                     nc.scalar.activation(d, p[:, 0:n], AF.Identity,
                                          bias=self.fbb[:, jj:jj + 1],
                                          scale=self.fsb[:, jj:jj + 1]),
                     rd=[self.RF(ps), self.RF(self.fsb), self.RF(self.fbb)],
                     wr=[dr], tag="fep")
            return epi

        for m in range(c.NM):
            jobs = [dict(src=self.xT, sstride=c.T, sc0=m * c.MACRO,
                         ntok=c.MACRO, epi=mk_epi(self.yT, c.MACRO))]
            if m == 0:
                jobs.append(dict(src=self.halo, sstride=256, sc0=0, ntok=256,
                                 epi=mk_epi(self.haloY, 256)))
            self.proj_T(self.i_fw, c.D, jobs, tag=f"ff{m}")
            for kb in range(c.DB):
                xa = self.xT[:, kb * c.T + m * c.MACRO:
                             kb * c.T + (m + 1) * c.MACRO]
                xr = self.R(self.xT, 0, 128, kb * c.T + m * c.MACRO,
                            kb * c.T + (m + 1) * c.MACRO)
                ya = self.yT[:, kb * c.MACRO:(kb + 1) * c.MACRO]
                yr = self.R(self.yT, 0, 128, kb * c.MACRO,
                            (kb + 1) * c.MACRO)
                b.op("V", lambda nc, d=xa, f=ya: nc.vector.tensor_add(
                    d, d, f), rd=[xr, yr], wr=[xr], tag="fra")
                if m == 0:
                    ha = self.halo[:, kb * 256:(kb + 1) * 256]
                    hr = self.R(self.halo, 0, 128, kb * 256, (kb + 1) * 256)
                    hy = self.haloY[:, kb * 256:(kb + 1) * 256]
                    hyr = self.R(self.haloY, 0, 128, kb * 256,
                                 (kb + 1) * 256)
                    b.op("V", lambda nc, d=ha, f=hy: nc.vector.tensor_add(
                        d, d, f), rd=[hr, hyr], wr=[hr], tag="fhra")
            for w0 in range(0, c.MACRO, 512):
                self.ln_T(self.xT, c.T, m * c.MACRO + w0,
                          min(512, c.MACRO - w0), tag=f"fn{m}_{w0}")
            if m == 0:
                self.ln_T(self.halo, 256, 0, 256, tag="fnh")

    def stage_l1a(self):
        c = self.cfg
        b = self.b
        s = self.i_mask.rearrange("s p f -> p s f")
        d = self.mask_sb[:, :].rearrange("p (s f) -> p s f", f=140)
        b.dma("Q", lambda nc, ss=s, dd=d: nc.sync.dma_start(dd, ss),
              wr=[self.RF(self.mask_sb)], tag="mk")
        for t in (self.stK8, self.stK4, self.vst8, self.vst4, self.vroll,
                  self.khal8, self.khal4, self.vhal8, self.vhal4):
            b.op("V", lambda nc, tt=t: nc.vector.memset(tt[:, :], 0.0),
                 rd=[], wr=[self.RF(t)], tag="z")

        nkv = c.KV1 // 128
        hb = c.DB // 2
        nkh = 2 * nkv
        for h in range(min(3, nkh)):
            self.load_half(self.i_wk1, h, tag="hwk")
        for j in range(nkv):
            for kb in range(c.DB):
                wbuf = self.wb[(2 * j + kb // hb) % 4]
                kof = kb % hb
                self.mm(self.ps0, self.ps0[:, 0:8],
                        wbuf, wbuf[:, kof * 128:(kof + 1) * 128],
                        self.halo, self.halo[:, kb * 256 + 248:kb * 256 + 256],
                        kb == 0, kb == c.DB - 1,
                        lr=self.R(wbuf, 0, 128, kof * 128, (kof + 1) * 128),
                        rr=self.R(self.halo, 0, 128, kb * 256 + 248,
                                  kb * 256 + 256), tag="hk8")
            for kb in range(c.DB):
                wbuf = self.wb[(2 * j + kb // hb) % 4]
                kof = kb % hb
                self.mm(self.ps0, self.ps0[:, 8:12],
                        wbuf, wbuf[:, kof * 128:(kof + 1) * 128],
                        self.halo, self.halo[:, kb * 256:kb * 256 + 4],
                        kb == 0, kb == c.DB - 1,
                        lr=self.R(wbuf, 0, 128, kof * 128, (kof + 1) * 128),
                        rr=self.R(self.halo, 0, 128, kb * 256, kb * 256 + 4),
                        tag="hk4")
            self.copy("S", self.khal8, self.khal8[:, j * 8:j * 8 + 8],
                      self.R(self.khal8, 0, 128, j * 8, j * 8 + 8),
                      self.ps0, self.ps0[:, 0:8], self.RF(self.ps0),
                      tag="hke")
            self.copy("S", self.khal4, self.khal4[:, j * 4:j * 4 + 4],
                      self.R(self.khal4, 0, 128, j * 4, j * 4 + 4),
                      self.ps0, self.ps0[:, 8:12], self.RF(self.ps0),
                      tag="hke4")
            for h in (2 * j + 3, 2 * j + 4):
                if h < nkh:
                    self.load_half(self.i_wk1, h, tag="hwk")
        for h in range(min(3, nkh)):
            self.load_half(self.i_wv1, h, tag="hwv")
        for j in range(nkv):
            for kb in range(c.DB):
                wbuf = self.wb[(2 * j + kb // hb) % 4]
                kof = kb % hb
                self.mm(self.ps1, self.ps1[0:8, 0:128],
                        self.halo, self.halo[:, kb * 256 + 248:kb * 256 + 256],
                        wbuf, wbuf[:, kof * 128:(kof + 1) * 128],
                        kb == 0, kb == c.DB - 1,
                        lr=self.R(self.halo, 0, 128, kb * 256 + 248,
                                  kb * 256 + 256),
                        rr=self.R(wbuf, 0, 128, kof * 128, (kof + 1) * 128),
                        tag="hv8")
            for kb in range(c.DB):
                wbuf = self.wb[(2 * j + kb // hb) % 4]
                kof = kb % hb
                self.mm(self.ps1, self.ps1[32:36, 0:128],
                        self.halo, self.halo[:, kb * 256:kb * 256 + 4],
                        wbuf, wbuf[:, kof * 128:(kof + 1) * 128],
                        kb == 0, kb == c.DB - 1,
                        lr=self.R(self.halo, 0, 128, kb * 256, kb * 256 + 4),
                        rr=self.R(wbuf, 0, 128, kof * 128, (kof + 1) * 128),
                        tag="hv4")
            self.copy("S", self.vhal8,
                      self.vhal8[0:8, j * 128:(j + 1) * 128],
                      self.R(self.vhal8, 0, 8, j * 128, (j + 1) * 128),
                      self.ps1, self.ps1[0:8, 0:128], self.RF(self.ps1),
                      tag="hve")
            self.copy("S", self.vhal4,
                      self.vhal4[0:4, j * 128:(j + 1) * 128],
                      self.R(self.vhal4, 0, 4, j * 128, (j + 1) * 128),
                      self.ps1, self.ps1[32:36, 0:128], self.RF(self.ps1),
                      tag="hve4")
            for h in (2 * j + 3, 2 * j + 4):
                if h < nkh:
                    self.load_half(self.i_wv1, h, tag="hwv")

        scale1 = 1.0 / np.sqrt(c.HD1)
        for m in range(c.NM):
            self.proj_T(self.i_wq1, c.D,
                        [dict(src=self.xT, sstride=c.T, sc0=m * c.MACRO,
                              ntok=c.MACRO,
                              epi=self.epi_copy(self.qT, c.MACRO, 0))],
                        tag=f"1q{m}")
            self.proj_T(self.i_wk1, c.KV1,
                        [dict(src=self.xT, sstride=c.T, sc0=m * c.MACRO,
                              ntok=c.MACRO,
                              epi=self.epi_copy(self.kT, c.MACRO, 0))],
                        tag=f"1k{m}")

            def vepi(j, tt, ps, m=m):
                da = self.vna[:, tt * c.KV1 + j * 128:
                              tt * c.KV1 + (j + 1) * 128]
                dr = self.R(self.vna, 0, 128, tt * c.KV1 + j * 128,
                            tt * c.KV1 + (j + 1) * 128)
                self.copy("S", self.vna, da, dr, ps, ps[:, 0:128],
                          self.RF(ps), tag=f"1v{m}")
            self.proj_N(self.i_wv1, c.KV1, self.xT, c.T, m * c.MACRO,
                        c.MACRO, vepi, tag=f"1vv{m}")
            nroll = max(1, c.CPM - 1)
            b.dma("G", lambda nc, nr=nroll: nc.gpsimd.dma_start(
                self.vroll[0:8, 0:nr * c.KV1],
                self.vna[120:128, 0:nr * c.KV1]),
                rd=[self.RF(self.vna)], wr=[self.RF(self.vroll)],
                tag=f"vr{m}")
            if m == 0:
                s0 = c.CPM * c.KV1
                scr = self.vna
                sr4 = self.R(scr, 0, 128, s0, s0 + c.L1KV * 4)
                vnr = self.R(self.vna, 0, 128, 0, c.CPM * c.KV1)
                b.op("V", lambda nc, s2=s0: nc.vector.tensor_sub(
                    self.vna[:, s2:s2 + c.L1KV * 4], self.khal4[:, :],
                    _k4own(self, c)),
                    rd=[self.RF(self.khal4),
                        self.R(self.kT, 0, 128, 0, c.L1KV * c.MACRO)],
                    wr=[sr4], tag="sel1")
                b.op("V", lambda nc, s2=s0: nc.vector.tensor_scalar_mul(
                    self.vna[:, s2:s2 + c.L1KV * 4],
                    self.vna[:, s2:s2 + c.L1KV * 4], self.flag[:, 0:1]),
                    rd=[sr4, self.RF(self.flag)], wr=[sr4], tag="sel2")
                b.op("V", lambda nc, s2=s0: nc.vector.tensor_add(
                    self.stK4[:, :], self.vna[:, s2:s2 + c.L1KV * 4],
                    _k4own(self, c)),
                    rd=[sr4, self.R(self.kT, 0, 128, 0, c.L1KV * c.MACRO)],
                    wr=[self.RF(self.stK4)], tag="sel3")
                v0 = s0 + c.L1KV * 4
                sv4 = self.R(scr, 0, 4, v0, v0 + c.KV1)
                b.op("V", lambda nc, s2=v0: nc.vector.tensor_sub(
                    self.vna[0:4, s2:s2 + c.KV1], self.vhal4[:, :],
                    self.vna[0:4, 0:c.KV1]),
                    rd=[self.RF(self.vhal4), vnr],
                    wr=[sv4], tag="sel4")
                b.op("V", lambda nc, s2=v0: nc.vector.tensor_scalar_mul(
                    self.vna[0:4, s2:s2 + c.KV1],
                    self.vna[0:4, s2:s2 + c.KV1], self.flag[0:4, 0:1]),
                    rd=[sv4, self.RF(self.flag)], wr=[sv4], tag="sel5")
                b.op("V", lambda nc, s2=v0: nc.vector.tensor_add(
                    self.vst4[:, :], self.vna[0:4, s2:s2 + c.KV1],
                    self.vna[0:4, 0:c.KV1]),
                    rd=[sv4, vnr],
                    wr=[self.RF(self.vst4)], tag="sel6")
                self.copy("V", self.stK8, self.stK8[:, :],
                          self.RF(self.stK8), self.khal8, self.khal8[:, :],
                          self.RF(self.khal8), tag="sel7")
                self.copy("V", self.vst8, self.vst8[:, :], self.RF(self.vst8),
                          self.vhal8, self.vhal8[:, :], self.RF(self.vhal8),
                          tag="sel8")
            for cc in range(c.CPM):
                glob = m * c.CPM + cc
                slot = 0 if glob == 0 else 1
                def sc_mm(H):
                    kv = H // c.GQ
                    ssl = H % 4
                    scp = self.ps4 if H % 2 == 0 else self.ps5
                    col0 = (ssl // 2) * 256
                    qa = self.qT[:, H * c.MACRO + cc * 128:
                                 H * c.MACRO + cc * 128 + 128]
                    qr = self.R(self.qT, 0, 128, H * c.MACRO + cc * 128,
                                H * c.MACRO + cc * 128 + 128)
                    self.mm(scp, scp[:, col0:col0 + 128], self.qT, qa,
                            self.kT,
                            self.kT[:, kv * c.MACRO + cc * 128:
                                    kv * c.MACRO + cc * 128 + 128],
                            True, True, lr=qr,
                            rr=self.R(self.kT, 0, 128,
                                      kv * c.MACRO + cc * 128,
                                      kv * c.MACRO + cc * 128 + 128),
                            tag=f"sc{glob}_{H}")
                    if cc == 0:
                        self.mm(scp, scp[:, col0 + 128:col0 + 136], self.qT,
                                qa, self.stK8,
                                self.stK8[:, kv * 8:(kv + 1) * 8],
                                True, True, lr=qr,
                                rr=self.R(self.stK8, 0, 128, kv * 8,
                                          (kv + 1) * 8),
                                tag=f"sp{glob}_{H}")
                    else:
                        self.mm(scp, scp[:, col0 + 128:col0 + 136], self.qT,
                                qa, self.kT,
                                self.kT[:, kv * c.MACRO + cc * 128 - 8:
                                        kv * c.MACRO + cc * 128],
                                True, True, lr=qr,
                                rr=self.R(self.kT, 0, 128,
                                          kv * c.MACRO + cc * 128 - 8,
                                          kv * c.MACRO + cc * 128),
                                tag=f"sp{glob}_{H}")
                    self.mm(scp, scp[:, col0 + 136:col0 + 140], self.qT, qa,
                            self.stK4, self.stK4[:, kv * 4:(kv + 1) * 4],
                            True, True, lr=qr,
                            rr=self.R(self.stK4, 0, 128, kv * 4,
                                      (kv + 1) * 4), tag=f"s4{glob}_{H}")

                def sc_soft(H):
                    ssl = H % 4
                    scp = self.ps4 if H % 2 == 0 else self.ps5
                    col0 = (ssl // 2) * 256
                    sca = self.sc[:, ssl * 160:ssl * 160 + 140]
                    scr = self.R(self.sc, 0, 128, ssl * 160, ssl * 160 + 140)
                    b.op("V", lambda nc, d=sca, sl2=slot, p=scp, o=col0:
                         nc.vector.scalar_tensor_tensor(
                             d, p[:, o:o + 140], scale1,
                             self.mask_sb[:, sl2 * 140:sl2 * 140 + 140],
                             ALU.mult, ALU.add),
                         rd=[self.RF(scp), self.RF(self.mask_sb)],
                         wr=[scr], tag=f"msk{glob}_{H}")
                    b.op("V", lambda nc, s2=sca, h=ssl:
                         nc.vector.reduce_max(self.mxr[:, h:h + 1], s2,
                                              axis=AX.X, negate=True),
                         rd=[scr], wr=[self.R(self.mxr, 0, 128, ssl,
                                              ssl + 1)],
                         tag=f"mx{glob}_{H}")

                def sc_exp(H):
                    ssl = H % 4
                    sca = self.sc[:, ssl * 160:ssl * 160 + 140]
                    scr = self.R(self.sc, 0, 128, ssl * 160, ssl * 160 + 140)
                    ata = self.attn[:, ssl * 160:ssl * 160 + 140]
                    atr = self.R(self.attn, 0, 128, ssl * 160,
                                 ssl * 160 + 140)
                    b.op("S", lambda nc, d=ata, s2=sca, h=ssl:
                         nc.scalar.activation(d, s2, AF.Exp,
                                              bias=self.mxr[:, h:h + 1],
                                              accum_out=self.smr[:, h:h + 1]),
                         rd=[scr, self.R(self.mxr, 0, 128, ssl, ssl + 1)],
                         wr=[atr, self.R(self.smr, 0, 128, ssl, ssl + 1)],
                         tag=f"ex{glob}_{H}")

                def sc_norm(H):
                    ssl = H % 4
                    ata = self.attn[:, ssl * 160:ssl * 160 + 140]
                    atr = self.R(self.attn, 0, 128, ssl * 160,
                                 ssl * 160 + 140)
                    b.op("V", lambda nc, h=ssl: nc.vector.reciprocal(
                        self.rcr[:, h:h + 1], self.smr[:, h:h + 1]),
                        rd=[self.R(self.smr, 0, 128, ssl, ssl + 1)],
                        wr=[self.R(self.rcr, 0, 128, ssl, ssl + 1)],
                        tag=f"rc{glob}_{H}")
                    b.op("V", lambda nc, d=ata, h=ssl:
                         nc.vector.tensor_scalar_mul(d, d,
                                                     self.rcr[:, h:h + 1]),
                         rd=[atr, self.R(self.rcr, 0, 128, ssl, ssl + 1)],
                         wr=[atr], tag=f"nm{glob}_{H}")

                def sc_trans(H):
                    ssl = H % 4
                    pbx = self.pb0 if H % 2 == 0 else self.pb1
                    pcol = (ssl // 2) * 384
                    atr = self.R(self.attn, 0, 128, ssl * 160,
                                 ssl * 160 + 140)
                    self.trans(pbx, pbx[:, pcol:pcol + 128], self.attn,
                               self.attn[:, ssl * 160:ssl * 160 + 128],
                               atr, tag=f"t1{glob}_{H}")
                    self.trans(pbx, pbx[0:8, pcol + 128:pcol + 256],
                               self.attn,
                               self.attn[:, ssl * 160 + 128:ssl * 160 + 136],
                               atr, tag=f"t2{glob}_{H}")
                    self.trans(pbx, pbx[0:4, pcol + 256:pcol + 384],
                               self.attn,
                               self.attn[:, ssl * 160 + 136:ssl * 160 + 140],
                               atr, tag=f"t3{glob}_{H}")

                def sc_evac(H):
                    ssl = H % 4
                    pbx = self.pb0 if H % 2 == 0 else self.pb1
                    pcol = (ssl // 2) * 384
                    t1a = self.T1sb[:, ssl * 128:ssl * 128 + 128]
                    t1r = self.R(self.T1sb, 0, 128, ssl * 128,
                                 ssl * 128 + 128)
                    self.copy("S", self.T1sb, t1a, t1r,
                              pbx, pbx[:, pcol:pcol + 128], self.RF(pbx),
                              tag=f"t1e{glob}_{H}")
                    self.copy("S", self.T2sb,
                              self.T2sb[0:8, ssl * 256:ssl * 256 + 128],
                              self.R(self.T2sb, 0, 8, ssl * 256,
                                     ssl * 256 + 128),
                              pbx, pbx[0:8, pcol + 128:pcol + 256],
                              self.RF(pbx), tag=f"t2e{glob}_{H}")
                    self.copy("S", self.T2sb,
                              self.T2sb[0:4, ssl * 256 + 128:ssl * 256 + 256],
                              self.R(self.T2sb, 0, 4, ssl * 256 + 128,
                                     ssl * 256 + 256),
                              pbx, pbx[0:4, pcol + 256:pcol + 384],
                              self.RF(pbx), tag=f"t3e{glob}_{H}")

                def sc_av(H):
                    kv = H // c.GQ
                    ssl = H % 4
                    avp = self.ps2 if H % 2 == 0 else self.ps3
                    acol = (ssl // 2) * 256
                    t1a = self.T1sb[:, ssl * 128:ssl * 128 + 128]
                    t1r = self.R(self.T1sb, 0, 128, ssl * 128,
                                 ssl * 128 + 128)
                    self.mm(avp, avp[:, acol:acol + 128], self.vna,
                            self.vna[:, cc * c.KV1 + kv * 128:
                                     cc * c.KV1 + (kv + 1) * 128],
                            self.T1sb, t1a, True, False,
                            lr=self.R(self.vna, 0, 128,
                                      cc * c.KV1 + kv * 128,
                                      cc * c.KV1 + (kv + 1) * 128),
                            rr=t1r, tag=f"av{glob}_{H}")
                    if cc == 0:
                        self.mm(avp, avp[:, acol:acol + 128], self.vst8,
                                self.vst8[0:8, kv * 128:(kv + 1) * 128],
                                self.T2sb,
                                self.T2sb[0:8, ssl * 256:ssl * 256 + 128],
                                False, False,
                                lr=self.R(self.vst8, 0, 8, kv * 128,
                                          (kv + 1) * 128),
                                rr=self.R(self.T2sb, 0, 8, ssl * 256,
                                          ssl * 256 + 128),
                                tag=f"av8{glob}_{H}")
                    else:
                        self.mm(avp, avp[:, acol:acol + 128], self.vroll,
                                self.vroll[0:8, (cc - 1) * c.KV1 + kv * 128:
                                           (cc - 1) * c.KV1
                                           + (kv + 1) * 128],
                                self.T2sb,
                                self.T2sb[0:8, ssl * 256:ssl * 256 + 128],
                                False, False,
                                lr=self.R(self.vroll, 0, 8,
                                          (cc - 1) * c.KV1 + kv * 128,
                                          (cc - 1) * c.KV1 + (kv + 1) * 128),
                                rr=self.R(self.T2sb, 0, 8, ssl * 256,
                                          ssl * 256 + 128),
                                tag=f"av8{glob}_{H}")
                    self.mm(avp, avp[:, acol:acol + 128], self.vst4,
                            self.vst4[0:4, kv * 128:(kv + 1) * 128],
                            self.T2sb,
                            self.T2sb[0:4, ssl * 256 + 128:ssl * 256 + 256],
                            False, True,
                            lr=self.R(self.vst4, 0, 4, kv * 128,
                                      (kv + 1) * 128),
                            rr=self.R(self.T2sb, 0, 4, ssl * 256 + 128,
                                      ssl * 256 + 256),
                            tag=f"av4{glob}_{H}")

                def sc_ye(H):
                    ssl = H % 4
                    avp = self.ps2 if H % 2 == 0 else self.ps3
                    acol = (ssl // 2) * 256
                    ya = self.yT[:, H * c.MACRO + cc * 128:
                                 H * c.MACRO + cc * 128 + 128]
                    yr = self.R(self.yT, 0, 128, H * c.MACRO + cc * 128,
                                H * c.MACRO + cc * 128 + 128)
                    self.copy("S", self.yT, ya, yr, avp,
                              avp[:, acol:acol + 128], self.RF(avp),
                              tag=f"ye{glob}_{H}")

                for g0 in range(0, c.L1H, 4):
                    hs = range(g0, min(g0 + 4, c.L1H))
                    for H in hs:
                        sc_mm(H)
                    for H in hs:
                        sc_soft(H)
                    for H in hs:
                        sc_exp(H)
                    for H in hs:
                        sc_norm(H)
                    for H in hs:
                        sc_trans(H)
                    for H in hs:
                        sc_evac(H)
                    for H in hs:
                        sc_av(H)
                    for H in hs:
                        sc_ye(H)
            if m < c.NM - 1:
                for j in range(nkv):
                    sa = self.kT[:, j * c.MACRO + c.MACRO - 8:
                                 j * c.MACRO + c.MACRO]
                    self.copy("V", self.stK8, self.stK8[:, j * 8:j * 8 + 8],
                              self.R(self.stK8, 0, 128, j * 8, j * 8 + 8),
                              self.kT, sa,
                              self.R(self.kT, 0, 128,
                                     j * c.MACRO + c.MACRO - 8,
                                     j * c.MACRO + c.MACRO), tag="su")
                b.dma("G", lambda nc: nc.gpsimd.dma_start(
                    self.vst8[:, :],
                    self.vna[120:128, (c.CPM - 1) * c.KV1:c.CPM * c.KV1]),
                    rd=[self.RF(self.vna)], wr=[self.RF(self.vst8)],
                    tag="su2")
            self.proj_T(self.i_wo1, c.D,
                        [dict(src=self.yT, sstride=c.MACRO, sc0=0,
                              ntok=c.MACRO,
                              epi=self.epi_radd(self.xT, c.T, m * c.MACRO))],
                        tag=f"1o{m}")
            for w0 in range(0, c.MACRO, 512):
                self.ln_T(self.xT, c.T, m * c.MACRO + w0,
                          min(512, c.MACRO - w0), tag=f"1n{m}_{w0}")

    def stage_moe(self):
        c = self.cfg
        b = self.b
        s = self.i_wr.rearrange("(bb p) e -> p bb e", p=128)
        d = self.wrb[:, :].rearrange("p (bb e) -> p bb e", e=c.E)
        b.dma("G", lambda nc, ss=s, dd=d: nc.gpsimd.dma_start(dd, ss),
              wr=[self.RF(self.wrb)], tag="rw")
        s2 = self.i_wsg.rearrange("(bb p) e -> p bb e", p=128)
        d2 = self.wsgb[:, :].rearrange("p (bb e) -> p bb e", e=1)
        b.dma("G", lambda nc, ss=s2, dd=d2: nc.gpsimd.dma_start(
            dd, ss, allow_slow_non_contiguous=True),
            wr=[self.RF(self.wsgb)], tag="sgw")
        for tt in range(c.TT):
            ps = self.ps4
            for kb in range(c.DB):
                self.mm(ps, ps[:, 0:c.E], self.xT,
                        self.xT[:, kb * c.T + tt * 128:
                                kb * c.T + tt * 128 + 128],
                        self.wrb, self.wrb[:, kb * c.E:(kb + 1) * c.E],
                        kb == 0, kb == c.DB - 1,
                        lr=self.R(self.xT, 0, 128, kb * c.T + tt * 128,
                                  kb * c.T + tt * 128 + 128),
                        rr=self.R(self.wrb, 0, 128, kb * c.E, (kb + 1) * c.E),
                        tag=f"rt{tt}")
            for kb in range(c.DB):
                self.mm(ps, ps[:, c.E:c.E + 1], self.xT,
                        self.xT[:, kb * c.T + tt * 128:
                                kb * c.T + tt * 128 + 128],
                        self.wsgb, self.wsgb[:, kb:kb + 1],
                        kb == 0, kb == c.DB - 1,
                        lr=self.R(self.xT, 0, 128, kb * c.T + tt * 128,
                                  kb * c.T + tt * 128 + 128),
                        rr=self.R(self.wsgb, 0, 128, kb, kb + 1),
                        tag=f"sg{tt}")
            b.op("V", lambda nc: nc.vector.reduce_max(
                self.m1[:, 0:1], ps[:, 0:c.E], axis=AX.X, negate=True),
                rd=[self.RF(ps)], wr=[self.RF(self.m1)], tag=f"r1{tt}")
            b.op("S", lambda nc: nc.scalar.activation(
                self.prb[:, :], ps[:, 0:c.E], AF.Exp,
                bias=self.m1[:, 0:1], accum_out=self.m2[:, 0:1]),
                rd=[self.RF(ps), self.RF(self.m1)],
                wr=[self.RF(self.prb), self.RF(self.m2)], tag=f"r2{tt}")
            b.op("V", lambda nc: nc.vector.reciprocal(
                self.m1[:, 0:1], self.m2[:, 0:1]),
                rd=[self.RF(self.m2)], wr=[self.RF(self.m1)], tag=f"r3{tt}")
            b.op("V", lambda nc: nc.vector.tensor_scalar_mul(
                self.prb[:, :], self.prb[:, :], self.m1[:, 0:1]),
                rd=[self.RF(self.prb), self.RF(self.m1)],
                wr=[self.RF(self.prb)], tag=f"r4{tt}")
            b.op("V", lambda nc: nc.vector.reduce_max(
                self.m1[:, 0:1], self.prb[:, :], axis=AX.X),
                rd=[self.RF(self.prb)], wr=[self.RF(self.m1)], tag=f"r5{tt}")
            b.op("V", lambda nc: nc.vector.tensor_scalar(
                self.eqm[:, :], self.prb[:, :], self.m1[:, 0:1], None,
                ALU.is_ge), rd=[self.RF(self.prb), self.RF(self.m1)],
                wr=[self.RF(self.eqm)], tag=f"r6{tt}")
            b.op("V", lambda nc: nc.vector.scalar_tensor_tensor(
                self.eqm[:, :], self.eqm[:, :], -2.0, self.prb[:, :],
                ALU.mult, ALU.add),
                rd=[self.RF(self.eqm), self.RF(self.prb)],
                wr=[self.RF(self.eqm)], tag=f"r7{tt}")
            b.op("V", lambda nc: nc.vector.reduce_max(
                self.m2[:, 0:1], self.eqm[:, :], axis=AX.X),
                rd=[self.RF(self.eqm)], wr=[self.RF(self.m2)], tag=f"r8{tt}")
            b.op("V", lambda nc: nc.vector.tensor_scalar(
                self.eqm[:, :], self.prb[:, :], self.m2[:, 0:1], None,
                ALU.is_ge), rd=[self.RF(self.prb), self.RF(self.m2)],
                wr=[self.RF(self.eqm)], tag=f"r9{tt}")
            ca = self.comb[:, tt * c.E:(tt + 1) * c.E]
            cr = self.R(self.comb, 0, 128, tt * c.E, (tt + 1) * c.E)
            b.op("V", lambda nc, d=ca: nc.vector.tensor_mul(
                d, self.prb[:, :], self.eqm[:, :]),
                rd=[self.RF(self.prb), self.RF(self.eqm)], wr=[cr],
                tag=f"rA{tt}")
            self.copy("V", self.gatec, self.gatec[:, tt:tt + 1],
                      self.R(self.gatec, 0, 128, tt, tt + 1),
                      ps, ps[:, c.E:c.E + 1], self.RF(ps), tag=f"ge{tt}")
        b.op("S", lambda nc: nc.scalar.activation(
            self.gatec[:, :], self.gatec[:, :], AF.Sigmoid),
            rd=[self.RF(self.gatec)], wr=[self.RF(self.gatec)], tag="gs")

        TH = c.TH
        HTOK = c.HTOK
        fin_pending = []
        for half in range(TH):
            for pe in range(c.NPE):
                shared = pe < c.NSH
                e = pe - c.NSH
                for ib in range(c.NIB):
                    if shared:
                        wsrc, jj = self.i_wsgu, pe * c.NIB + ib
                    else:
                        wsrc, jj = self.i_wegu, e * c.NIB + ib
                    wbuf = self.wbm[ib % 2]
                    self.load_w(wsrc, jj, c.DB * 256, wbuf, tag=f"eg{pe}")
                    for tb in range((HTOK + 511) // 512):
                        nt = min(512, HTOK - tb * 512)
                        t0 = half * HTOK + tb * 512
                        psg, psu = self.ps0, self.ps1
                        for kb in range(c.DB):
                            self.mm(psg, psg[:, 0:nt], wbuf,
                                    wbuf[:, kb * 256:kb * 256 + 128],
                                    self.xT,
                                    self.xT[:, kb * c.T + t0:
                                            kb * c.T + t0 + nt],
                                    kb == 0, kb == c.DB - 1,
                                    lr=self.R(wbuf, 0, 128, kb * 256,
                                              kb * 256 + 128),
                                    rr=self.R(self.xT, 0, 128, kb * c.T + t0,
                                              kb * c.T + t0 + nt),
                                    tag=f"gg{pe}")
                        for kb in range(c.DB):
                            self.mm(psu, psu[:, 0:nt], wbuf,
                                    wbuf[:, kb * 256 + 128:kb * 256 + 256],
                                    self.xT,
                                    self.xT[:, kb * c.T + t0:
                                            kb * c.T + t0 + nt],
                                    kb == 0, kb == c.DB - 1,
                                    lr=self.R(wbuf, 0, 128, kb * 256 + 128,
                                              kb * 256 + 256),
                                    rr=self.R(self.xT, 0, 128, kb * c.T + t0,
                                              kb * c.T + t0 + nt),
                                    tag=f"gu{pe}")
                        sga = self.sg_scr[:, (tb % 2) * 512:
                                          (tb % 2) * 512 + nt]
                        sgr = self.R(self.sg_scr, 0, 128, (tb % 2) * 512,
                                     (tb % 2) * 512 + nt)
                        b.op("S", lambda nc, d=sga, p=psg, n=nt:
                             nc.scalar.activation(d, p[:, 0:n], AF.Silu),
                             rd=[self.RF(psg)], wr=[sgr], tag=f"sl{pe}")
                        ha = self.hT[:, ib * HTOK + tb * 512:
                                     ib * HTOK + tb * 512 + nt]
                        hr = self.R(self.hT, 0, 128, ib * HTOK + tb * 512,
                                    ib * HTOK + tb * 512 + nt)
                        b.op("V", lambda nc, d=ha, s3=sga, p=psu, n=nt:
                             nc.vector.tensor_mul(d, s3, p[:, 0:n]),
                             rd=[sgr, self.RF(psu)], wr=[hr], tag=f"hm{pe}")
                    if fin_pending:
                        self.moe_finalize_tile(fin_pending.pop(0))
                nd4 = (c.D + 511) // 512
                for d4 in range(nd4):
                    nd = min(512, c.D - d4 * 512)
                    wdb = self.wd[d4 % 2]
                    jj = (pe if shared else e) * nd4 + d4
                    self.load_w(self.i_wsdn if shared else self.i_wedn,
                                jj, c.NIB * nd, wdb, tag=f"dw{pe}")
                    for tl in range(HTOK // 128):
                        ttg = half * (HTOK // 128) + tl
                        psd = self.ps2 if tl % 2 == 0 else self.ps3
                        for ib in range(c.NIB):
                            self.mm(psd, psd[:, 0:nd], self.hT,
                                    self.hT[:, ib * HTOK + tl * 128:
                                            ib * HTOK + tl * 128 + 128],
                                    wdb,
                                    wdb[:, ib * nd:(ib + 1) * nd],
                                    ib == 0, ib == c.NIB - 1,
                                    lr=self.R(self.hT, 0, 128,
                                              ib * HTOK + tl * 128,
                                              ib * HTOK + tl * 128 + 128),
                                    rr=self.R(wdb, 0, 128, ib * nd,
                                              (ib + 1) * nd),
                                    tag=f"dn{pe}")
                        aa = self.acc[:, ttg * c.D + d4 * 512:
                                      ttg * c.D + d4 * 512 + nd]
                        ar = self.R(self.acc, 0, 128, ttg * c.D + d4 * 512,
                                    ttg * c.D + d4 * 512 + nd)
                        if shared:
                            ccol = self.gatec[:, ttg:ttg + 1]
                            creg = self.R(self.gatec, 0, 128, ttg, ttg + 1)
                        else:
                            ccol = self.comb[:, ttg * c.E + e:
                                             ttg * c.E + e + 1]
                            creg = self.R(self.comb, 0, 128, ttg * c.E + e,
                                          ttg * c.E + e + 1)
                        if pe == 0:
                            b.op("V", lambda nc, d=aa, p=psd, cc2=ccol, n=nd:
                                 nc.vector.tensor_scalar_mul(
                                     d, p[:, 0:n], cc2),
                                 rd=[self.RF(psd), creg], wr=[ar],
                                 tag=f"ac{pe}")
                        else:
                            b.op("V", lambda nc, d=aa, p=psd, cc2=ccol, n=nd:
                                 nc.vector.scalar_tensor_tensor(
                                     d, p[:, 0:n], cc2, d,
                                     ALU.mult, ALU.add),
                                 rd=[self.RF(psd), creg, ar], wr=[ar],
                                 tag=f"ac{pe}")
                        if pe == c.NPE - 1 and d4 == nd4 - 1:
                            if half < TH - 1:
                                fin_pending.append(ttg)
                            else:
                                self.moe_finalize_tile(ttg)
        while fin_pending:
            self.moe_finalize_tile(fin_pending.pop(0))

    def moe_finalize_tile(self, tt):
        c = self.cfg
        b = self.b
        HD = c.D // 2
        lr = self.RF(self.lnr)
        for kb in range(c.DB):
            pbx = self.pb0 if kb % 2 == 0 else self.pb1
            pcol = (kb // 2) * 128
            self.trans(pbx, pbx[:, pcol:pcol + 128], self.xT,
                       self.xT[:, kb * c.T + tt * 128:
                               kb * c.T + tt * 128 + 128],
                       self.R(self.xT, 0, 128, kb * c.T + tt * 128,
                              kb * c.T + tt * 128 + 128), tag=f"tx{tt}")
        for kb in range(c.DB):
            pbx = self.pb0 if kb % 2 == 0 else self.pb1
            pcol = (kb // 2) * 128
            aa = self.acc[:, tt * c.D + kb * 128:tt * c.D + kb * 128 + 128]
            ar = self.R(self.acc, 0, 128, tt * c.D + kb * 128,
                        tt * c.D + kb * 128 + 128)
            b.op("V", lambda nc, d=aa, p=pbx, o=pcol: nc.vector.tensor_add(
                d, d, p[:, o:o + 128]),
                rd=[ar, self.RF(pbx)], wr=[ar], tag=f"ra{tt}")
        ta = self.acc[:, tt * c.D:(tt + 1) * c.D]
        tr = self.R(self.acc, 0, 128, tt * c.D, (tt + 1) * c.D)
        b.op("V", lambda nc, s3=ta: nc.vector.reduce_sum(
            self.lnr[:, 0:1], s3, axis=AX.X),
            rd=[tr], wr=[lr], tag=f"ls{tt}")
        for ch in range(2):
            ca = self.acc[:, tt * c.D + ch * HD:tt * c.D + (ch + 1) * HD]
            b.op("S", lambda nc, s3=ca: nc.scalar.square(self.o32[:, :], s3),
                 rd=[tr], wr=[self.RF(self.o32)], tag=f"lq{tt}")
            b.op("V", lambda nc, cc2=ch: nc.vector.reduce_sum(
                self.lnr[:, 1 + 3 * cc2:2 + 3 * cc2], self.o32[:, :],
                axis=AX.X),
                rd=[self.RF(self.o32)], wr=[lr], tag=f"lqs{tt}")
        b.op("V", lambda nc: nc.vector.tensor_add(
            self.lnr[:, 1:2], self.lnr[:, 1:2], self.lnr[:, 4:5]),
            rd=[lr], wr=[lr], tag=f"lqa{tt}")
        b.op("V", lambda nc: nc.vector.tensor_scalar_mul(
            self.lnr[:, 2:4], self.lnr[:, 0:2], 1.0 / c.D),
            rd=[lr], wr=[lr], tag=f"lm{tt}")
        b.op("V", lambda nc: nc.vector.tensor_mul(
            self.lnr[:, 4:5], self.lnr[:, 2:3], self.lnr[:, 2:3]),
            rd=[lr], wr=[lr], tag=f"lm2{tt}")
        b.op("V", lambda nc: nc.vector.tensor_sub(
            self.lnr[:, 5:6], self.lnr[:, 3:4], self.lnr[:, 4:5]),
            rd=[lr], wr=[lr], tag=f"lv{tt}")
        b.op("V", lambda nc: nc.vector.tensor_scalar_add(
            self.lnr[:, 5:6], self.lnr[:, 5:6], EPS_LN),
            rd=[lr], wr=[lr], tag=f"lve{tt}")
        b.op("S", lambda nc: nc.scalar.activation(
            self.lnr[:, 6:7], self.lnr[:, 5:6], AF.Ln),
            rd=[lr], wr=[lr], tag=f"ll{tt}")
        b.op("V", lambda nc: nc.vector.tensor_scalar_mul(
            self.lnr[:, 6:7], self.lnr[:, 6:7], -0.5),
            rd=[lr], wr=[lr], tag=f"lvh{tt}")
        b.op("S", lambda nc: nc.scalar.activation(
            self.lnr[:, 7:8], self.lnr[:, 6:7], AF.Exp),
            rd=[lr], wr=[lr], tag=f"le{tt}")
        b.op("V", lambda nc: nc.vector.tensor_mul(
            self.lnr[:, 3:4], self.lnr[:, 2:3], self.lnr[:, 7:8]),
            rd=[lr], wr=[lr], tag=f"lmr{tt}")
        b.op("V", lambda nc: nc.vector.tensor_scalar_mul(
            self.lnr[:, 3:4], self.lnr[:, 3:4], -1.0),
            rd=[lr], wr=[lr], tag=f"lng{tt}")
        for ch in range(2):
            ca = self.acc[:, tt * c.D + ch * HD:tt * c.D + (ch + 1) * HD]
            b.op("S", lambda nc, s3=ca: nc.scalar.activation(
                self.o32[:, :], s3, AF.Identity, bias=self.lnr[:, 3:4],
                scale=self.lnr[:, 7:8]),
                rd=[tr, lr], wr=[self.RF(self.o32)], tag=f"la{tt}")
            b.dma("Q", lambda nc, t=tt, cc2=ch: nc.sync.dma_start(
                self.o_out[t * 128:(t + 1) * 128, cc2 * HD:(cc2 + 1) * HD],
                self.o32[:, :]),
                rd=[self.RF(self.o32)], tag=f"so{tt}")


def _k4own(mk, c):
    t = mk.kT.tensor if hasattr(mk.kT, "tensor") else mk.kT
    return bass.AP(tensor=t, offset=0,
                   ap=[[mk.kT.shape[1], 128], [c.MACRO, c.L1KV], [1, 4]])


STAGE_GROUPS = [["l0a", "l0f", "l1a", "moe"]]


def _swz(WT, cw=128):
    WT = np.asarray(WT, dtype=np.float32)
    din, dout = WT.shape
    cw = min(cw, dout)
    nb, nj = din // 128, dout // cw
    a = WT.reshape(nb, 128, nj, cw).transpose(2, 1, 0, 3)
    return np.ascontiguousarray(a.reshape(nj, 128, nb * cw))


def _swz_pair(Wg, Wu):
    din, I = Wg.shape
    nb, nj = din // 128, I // 128
    g = Wg.reshape(nb, 128, nj, 128).transpose(2, 1, 0, 3)
    u = Wu.reshape(nb, 128, nj, 128).transpose(2, 1, 0, 3)
    a = np.concatenate([g[..., None, :], u[..., None, :]], axis=3)
    return np.ascontiguousarray(a.reshape(nj, 128, nb * 256))


def _prep_weights(inp, cfg):
    c = cfg
    w = {}
    t = lambda a: np.ascontiguousarray(np.asarray(a, dtype=np.float32).T)
    w["l0WqT"] = _swz(t(inp["L0_attn_Wq"]))
    w["l0WkT"] = _swz(t(inp["L0_attn_Wk"]))
    w["l0WvT"] = _swz(t(inp["L0_attn_Wv"]))
    w["l0WoT"] = _swz(t(inp["L0_attn_Wo"]))
    wfq = np.zeros((128, c.NPAIR * 128), np.float32)
    wfk = np.zeros((128, c.NPAIR * 128), np.float32)
    for pr in range(c.NPAIR):
        wfq[0:64, pr * 128:pr * 128 + 64] = inp["L0_attn_Wfq"][2 * pr]
        wfq[64:128, pr * 128 + 64:pr * 128 + 128] = \
            inp["L0_attn_Wfq"][2 * pr + 1]
        wfk[0:64, pr * 128:pr * 128 + 64] = inp["L0_attn_Wfk"][2 * pr]
        wfk[64:128, pr * 128 + 64:pr * 128 + 128] = \
            inp["L0_attn_Wfk"][2 * pr + 1]
    w["l0Wfq"] = wfq
    w["l0Wfk"] = wfk
    w["l0fWT"] = _swz(t(inp["L0_ffn_weight"]))
    w["l0fscale"] = np.ascontiguousarray(inp["L0_ffn_scale_weight"],
                                         dtype=np.float32)
    w["l0fbias"] = np.ascontiguousarray(inp["L0_ffn_bias"], dtype=np.float32)
    w["l1WqT"] = _swz(t(inp["L1_attn_Wq"]))
    w["l1WkT"] = _swz(t(inp["L1_attn_Wk"]))
    w["l1WvT"] = _swz(t(inp["L1_attn_Wv"]))
    w["l1WoT"] = _swz(t(inp["L1_attn_Wo"]))
    w["rWT"] = t(inp["L1_ffn_W_router"])
    w["sgWT"] = t(inp["L1_ffn_W_sgate"])
    sgu = np.asarray(inp["L1_ffn_Ws_gate_up"], dtype=np.float32)
    w["WsGU"] = _swz_pair(sgu[:, :c.ISH], sgu[:, c.ISH:])
    nd4 = (c.D + 511) // 512
    sdn = np.asarray(inp["L1_ffn_Ws_down"], dtype=np.float32)
    w["WsDN"] = np.concatenate(
        [_swz(sdn[pe * c.IM:(pe + 1) * c.IM, :], cw=512)
         for pe in range(c.NSH)], axis=0)
    egu = np.asarray(inp["L1_ffn_We_gate_up"], dtype=np.float32)
    w["WeGU"] = np.concatenate(
        [_swz_pair(egu[e][:, :c.IM], egu[e][:, c.IM:]) for e in range(c.E)],
        axis=0)
    edn = np.asarray(inp["L1_ffn_We_down"], dtype=np.float32)
    w["WeDN"] = np.concatenate(
        [_swz(edn[e], cw=512) for e in range(c.E)], axis=0)
    return w


def _masks(cfg, second_half):
    c = cfg
    m = np.zeros((2, 128, 140), np.float32)
    qi = np.arange(128)[:, None]
    ki = np.arange(128)[None, :]
    band = (ki <= qi) & (qi - ki < c.NL)
    m[1, :, 0:128] = np.where(band, 0.0, NEG)
    kp = np.arange(8)[None, :]
    prev_ok = kp > qi
    m[1, :, 128:136] = np.where(prev_ok, 0.0, NEG)
    m[1, :, 136:140] = 0.0
    if second_half:
        m[0] = m[1]
    else:
        ki = np.arange(128)[None, :]
        allowed = (ki <= qi) & ((ki < c.NI) | (qi - ki < c.NL))
        m[0, :, 0:128] = np.where(allowed, 0.0, NEG)
        m[0, :, 128:140] = NEG
    return m


_PROG_CACHE = {}


def _get_prog(key, cfg, stages, first, last):
    if key not in _PROG_CACHE:
        _PROG_CACHE[key] = MK(cfg, stages, first, last).build()
    return _PROG_CACHE[key]


def kernel(**inputs):
    cfg = Cfg(False)
    c = cfg
    x = np.asarray(inputs["L0_attn_hidden_states"], dtype=np.float32)
    B, S, D = x.shape
    w = _prep_weights(inputs, cfg)
    ident = np.eye(128, dtype=np.float32)
    tri = np.triu(np.ones((128, 128), np.float32))
    ones = np.ones((128, 1), np.float32)

    halves = S // c.T
    core_meta = []
    for core in range(N_CORES):
        bb, hh = divmod(core, halves)
        xs = np.ascontiguousarray(x[bb, hh * c.T:(hh + 1) * c.T].T)
        xp = np.ascontiguousarray(x[bb, 0:c.T].T)
        flag = np.full((128, 1), 1.0 if hh > 0 else 0.0, np.float32)
        mask = _masks(cfg, hh > 0)
        core_meta.append(dict(xT=xs, xpreT=xp, c_flag=flag, l1mask=mask))

    consts = dict(c_ident=ident, c_tri=tri, c_ones=ones)
    state = [dict() for _ in range(N_CORES)]
    n_groups = len(STAGE_GROUPS)
    out = None
    for gi, stages in enumerate(STAGE_GROUPS):
        first = gi == 0
        last = gi == n_groups - 1
        mk = _get_prog((tuple(stages), first, last), cfg, stages, first, last)
        in_names = set()
        for alloc in mk.nc.m.functions[0].allocations:
            if getattr(alloc, "kind", None) == "ExternalInput":
                n = alloc.memorylocations[0].name
                if n != "partition_id":
                    in_names.add(n)
        in_maps = []
        for core in range(N_CORES):
            im = {}
            for name in in_names:
                if name in w:
                    im[name] = w[name]
                elif name in consts:
                    im[name] = consts[name]
                elif name in core_meta[core]:
                    im[name] = core_meta[core][name]
                elif name in state[core]:
                    im[name] = state[core][name]
                else:
                    raise KeyError(f"missing input {name}")
            in_maps.append(im)
        res = run_bass_kernel_spmd(mk.nc, in_maps,
                                   core_ids=list(range(N_CORES)))
        if last:
            out = np.empty((B, S, D), np.float32)
            for core in range(N_CORES):
                bb, hh = divmod(core, halves)
                out[bb, hh * c.T:(hh + 1) * c.T] = res.results[core]["out"]
        else:
            for core in range(N_CORES):
                state[core] = {"xstate": res.results[core]["xstate_o"],
                               "halostate": res.results[core]["halostate_o"]}
    return out
